# version 06
# speedup vs baseline: 1.2848x; 1.0185x over previous
import contextlib
import math
import numpy as np

import concourse.bass as bass
import concourse.mybir as mybir
from concourse.bass_utils import run_bass_kernel_spmd

F32 = mybir.dt.float32
BF16 = mybir.dt.bfloat16
FP8 = mybir.dt.float8e4
ALU = mybir.AluOpType
AF = mybir.ActivationFunctionType
AX = mybir.AxisListType

B, L, D = 2, 8192, 1024
NCORE = 8
SPAN = 2048
HL, HR = 4, 2
NL = SPAN + HL + HR
NEG = -1e30

H0, DK0, DV0 = 4, 192, 384
H1, DK1, DV1 = 8, 128, 128
CC = 128

DEBUG = ()


def _ntiles(n, t=512):
    out, c = [], 0
    while c < n:
        out.append((c, min(t, n - c)))
        c += t
    return out



def _host_consts():
    iv = np.arange(128)
    col, row = iv[None, :], iv[:, None]
    up_strict = col > row
    up_incl = col >= row
    same_blk = col // 32 == row // 32
    mb = lambda m: np.where(m, 0.0, NEG).astype(np.float32)
    mk = lambda m: m.astype(np.float32)
    ident = np.eye(128, dtype=np.float32)
    cols = [np.ones((128, 1), np.float32)]
    for v in (1e-5, 1e-6, 1e-12, 0.5, 1.0 / 1024.0, 1.0 / 384.0, 1.0 / 128.0):
        cols.append(np.full((128, 1), v, np.float32))
    cf = np.concatenate(cols + [
        ident,
        mb(up_strict), mb(up_incl),
        mb(up_strict & same_blk), mb(up_strict & ~same_blk),
        mk(up_strict), mk(up_incl),
        mk(up_strict & same_blk), mk(up_strict & ~same_blk),
    ], axis=1)
    top64 = (iv < 64).astype(np.float32)[:, None]
    cbm = np.concatenate([
        np.eye(128, dtype=np.float32),
        np.ones((128, 1), np.float32),
        np.ones((128, 1), np.float32), top64, 1.0 - top64,
    ], axis=1)
    import ml_dtypes
    return cf, cbm.astype(ml_dtypes.bfloat16)

CF_ONES = 0
CF_EPS5, CF_EPS6, CF_EPS12, CF_HALF, CF_RD, CF_RDV, CF_R128 = 1, 2, 3, 4, 5, 6, 7
CF_ID = 8
CF_MB_UP_STRICT, CF_MB_UP_INCL, CF_MB_BD, CF_MB_BELOW = 136, 264, 392, 520
CF_MSK_UP_STRICT, CF_MSK_UP_INCL, CF_MSK_BD, CF_MSK_BELOW = 648, 776, 904, 1032
CB_ID, CB_ONES, CB_FULL, CB_TOP, CB_BOT = 0, 128, 129, 130, 131

SC_CXW, SC_CZW, SC_CQW, SC_CKW, SC_CVW = 0, 32, 64, 96, 128
SC_CXB, SC_CZB = 160, 168
SC_AL, SC_EAL, SC_DTB, SC_BB = 176, 177, 178, 179
SC_GNW = 180
SC_ONW = 183
SC_MXB = 184
SC_SIN, SC_RSIN, SC_WS = 186, 187, 188
SC_F0B = 190
SC_STAT = 200
SC_NRM = 260
SC_END = 260 + 2048


class _Last:
    _SKIP = {"wait_ge", "nop", "maybe_nop_then_inc", "drain"}

    def __init__(self, eng, autodrain):
        object.__setattr__(self, "_e", eng)
        object.__setattr__(self, "last", None)
        object.__setattr__(self, "_ad", autodrain)

    def __getattr__(self, name):
        f = getattr(self._e, name)
        if callable(f):
            skip = name in _Last._SKIP
            def g(*a, **k):
                if (not skip) and self._ad:
                    self._e.drain()
                r = f(*a, **k)
                if (not skip) and r is not None and hasattr(r, "then_inc"):
                    object.__setattr__(self, "last", r)
                return r
            return g
        return f


class KB:
    def __init__(self):
        self.nc = bass.Bass("TRN2", target_bir_lowering=False)
        self.es = contextlib.ExitStack()
        self.block = None
        self._sems = {}
        self.n = {}
        self.seen = {}

    def sem(self, name):
        if name not in self._sems:
            self._sems[name] = self.es.enter_context(self.nc.semaphore(name))
            self.n[name] = 0
        return self._sems[name]

    def inc(self, instr, name, k=1):
        instr.then_inc(self.sem(name), k)
        self.n[name] += k
        return self.n[name]

    ENG = {"te": "tensor", "ve": "vector", "ae": "scalar", "ge": "gpsimd",
           "se": "sync"}

    def te(self, fn): self.block.tensor(lambda e: fn(_Last(e, False)))
    def ve(self, fn): self.block.vector(lambda e: fn(_Last(e, True)))
    def ae(self, fn): self.block.scalar(lambda e: fn(_Last(e, True)))
    def ge(self, fn): self.block.gpsimd(lambda e: fn(_Last(e, True)))
    def se(self, fn): self.block.sync(lambda e: fn(_Last(e, False)))

    def incl(self, proxy, name, k=1):
        assert proxy.last is not None, "no instruction to attach inc to"
        return self.inc(proxy.last, name, k)

    def wait(self, eng, *sems):
        vals = [(self.sem(s), self.n[s]) for s in sems if self.n.get(s, 0) > 0]
        if not vals:
            return
        def f(e):
            for h, v in vals:
                e.wait_ge(h, v)
        getattr(self.block, self.ENG[eng])(f)

    def sync_step(self, eng, sems=("te", "ve", "ae", "ge", "qgB", "qt0", "qt1",
                                   "qo", "qrw", "qbc", "qbg", "qg", "qa", "qs")):
        seen = self.seen.setdefault(eng, {})
        vals = []
        for s in sems:
            n = self.n.get(s, 0)
            if n > seen.get(s, 0):
                vals.append((self.sem(s), n))
                seen[s] = n
        if vals:
            def f(e):
                for h, v in vals:
                    e.wait_ge(h, v)
            getattr(self.block, self.ENG[eng])(f)

    def waitm(self, eng, marks):
        vals = [(self.sem(s), v) for (s, v) in marks if v > 0]
        if not vals:
            return
        def f(e):
            for h, v in vals:
                e.wait_ge(h, v)
        getattr(self.block, self.ENG[eng])(f)


def build_program():
    kb = KB()
    nc, es = kb.nc, kb.es

    def din(name, shape, dtype=F32):
        return nc.dram_tensor(name, shape, dtype, kind="ExternalInput").ap()

    x_in = din("x_fm", [D, NL])
    gidx_in = din("gidx", [128, 96], mybir.dt.uint32)
    pe_in = din("pe_fm", [32, NL])
    hmask = din("hmask", [1, NL])
    consts_f = din("consts_f", [128, 1160])
    consts_b = din("consts_b", [128, 132], BF16)

    w = {}
    for name, shape in [
        ("L0_attn_W_in", [1024, 2048]), ("L0_attn_conv_x_w", [1024, 4]),
        ("L0_attn_conv_x_b", [1024, 1]), ("L0_attn_conv_z_w", [1024, 4]),
        ("L0_attn_conv_z_b", [1024, 1]), ("L0_attn_W_q", [1024, 768]),
        ("L0_attn_W_k", [1024, 768]), ("L0_attn_W_v", [1024, 1536]),
        ("L0_attn_W_g", [1024, 1536]), ("L0_attn_W_gk", [1024, 4]),
        ("L0_attn_W_b", [1024, 4]), ("L0_attn_b_b", [4, 1]),
        ("L0_attn_A_log", [4, 1]), ("L0_attn_dt_bias", [4, 1]),
        ("L0_attn_g_norm_w", [384, 1]), ("L0_attn_W_so", [1536, 1024]),
        ("L0_attn_W_out", [2048, 1024]), ("L0_ffn_weight_fp8", [1024, 1024]),
        ("L0_ffn_bias", [1024, 1]), ("L0_ffn_input_scale", [1, 1]),
        ("L0_ffn_weight_scale", [1, 1]), ("L1_attn_q_w", [1024, 1024]),
        ("L1_attn_k_w", [1024, 1024]), ("L1_attn_v_w", [1024, 1024]),
        ("L1_attn_conv_q_w", [1024, 4]), ("L1_attn_conv_k_w", [1024, 4]),
        ("L1_attn_conv_v_w", [1024, 4]), ("L1_attn_b_w", [1024, 8]),
        ("L1_attn_mix_w", [1024, 8]), ("L1_attn_mix_b", [8, 1]),
        ("L1_attn_mix_bias", [8, 1]), ("L1_attn_o_norm_w", [128, 1]),
        ("L1_attn_o_w", [1024, 1024]), ("L1_ffn_tile_sigs", [8, 32]),
        ("L1_ffn_W", [8 * 1024, 1024]), ("L1_ffn_b", [8, 1024]),
    ]:
        w[name] = din(name, shape)

    y_out = nc.dram_tensor("y_fm", [D, SPAN], F32, kind="ExternalOutput").ap()

    dbg = {}

    def dbg_out(name, shape, dtype=F32):
        if name in DEBUG and name not in dbg:
            dbg[name] = nc.dram_tensor("dbg_" + name, shape, dtype,
                                       kind="ExternalOutput").ap()
        return dbg.get(name)

    def dint(name, shape, dtype):
        return nc.dram_tensor(name, shape, dtype).ap()

    ag1b_i = dint("ag1b_i", [3072, SPAN], BF16)
    ag1b_o = dint("ag1b_o", [8 * 3072, SPAN], BF16)
    ag1f_i = dint("ag1f_i", [8, SPAN], F32)
    ag1f_o = dint("ag1f_o", [64, SPAN], F32)
    ag2_i = dint("ag2_i", [DV0, L], BF16)
    ag2_o = dint("ag2_o", [8 * DV0, L], BF16)
    ag3b_i = dint("ag3b_i", [3072, SPAN], BF16)
    ag3b_o = dint("ag3b_o", [8 * 3072, SPAN], BF16)
    ag3f_i = dint("ag3f_i", [8, SPAN], F32)
    ag3f_o = dint("ag3f_o", [64, SPAN], F32)
    ag4_i = dint("ag4_i", [2 * DV1, L], BF16)
    ag4_o = dint("ag4_o", [16 * DV1, L], BF16)
    zs_hbm = dint("zs_hbm", [1024, 2051], BF16)
    g_hbm = dint("g_hbm", [1536, 2051], BF16)
    stats_hbm = dint("stats_hbm", [60, 128], F32)
    rows_hbm = dint("rows_hbm", [64, L], F32)
    wf0T_hbm = dint("wf0T_hbm", [1024, 1024], BF16)
    onehot_hbm = dint("onehot_hbm", [8, NL], F32)
    g1_hbm = dint("g1_hbm", [8, SPAN], F32)
    wT_hbm = dint("wT_hbm", [8 * 1024, 1024], BF16)

    sb = lambda name, shape, dt=F32: es.enter_context(nc.sbuf_tensor(name, shape, dt))
    res = sb("res", [128, 8 * NL])
    xln = sb("xln", [128, 8 * NL], BF16)
    a1 = sb("a1", [128, 8 * NL], BF16)
    a2 = sb("a2", [128, 8 * NL], BF16)
    wr = sb("wr", [128, 2 * 2048], BF16)
    wst = sb("wst", [128, 2048])
    mrep = sb("mrep", [128, NL])
    rrep = sb("rrep", [128, NL], BF16)
    sc = sb("sc", [128, SC_END])
    scb = sb("scb", [128, 2048], BF16)
    q8t = sb("q8t", [128, 512], FP8)
    sd = sb("sd", [128, 96])
    gix = sb("gix", [128, 96], mybir.dt.uint32)

    ps = lambda name, shape, dt=F32: es.enter_context(nc.psum_tensor(name, shape, dt))
    P0 = ps("P0", [128, 512])
    P1 = ps("P1", [128, 512])
    P2 = ps("P2", [128, 512])
    P3 = ps("P3", [128, 512])
    P4 = ps("P4", [128, 512])
    P5 = ps("P5", [128, 512])
    P6 = ps("P6", [128, 512])
    PT = ps("PT", [128, 1024], BF16)

    kb.block = es.enter_context(nc.Block())

    cfb = sb("cfb", [128, 1160])
    cbb = sb("cbb", [128, 132], BF16)
    identF = cfb[:, CF_ID:CF_ID + 128]
    identB = cbb[:, CB_ID:CB_ID + 128]

    def cfm(off):
        return cfb[:, off:off + 128]

    def f(s):
        i1 = s.dma_start(out=cfb[:], in_=consts_f)
        kb.inc(i1, "qs", 16)
        kb.inc(s.dma_start(out=gix[:], in_=gidx_in), "qs", 16)
        i2 = s.dma_start(out=cbb[:], in_=consts_b)
        kb.inc(i2, "qs", 16)
        for c in range(8):
            kb.inc(s.dma_start(out=res[:, c * NL:(c + 1) * NL],
                               in_=x_in[128 * c:128 * (c + 1), :]), "qs", 16)
    kb.se(f)

    def f(s):
        for cget, dst in ((w["L0_attn_conv_x_w"], SC_CXW), (w["L0_attn_conv_z_w"], SC_CZW),
                          (w["L1_attn_conv_q_w"], SC_CQW), (w["L1_attn_conv_k_w"], SC_CKW),
                          (w["L1_attn_conv_v_w"], SC_CVW)):
            for c in range(8):
                kb.inc(s.dma_start(out=sc[:, dst + 4 * c:dst + 4 * c + 4],
                                   in_=cget[128 * c:128 * (c + 1), :]), "qa", 16)
        for cget, dst in ((w["L0_attn_conv_x_b"], SC_CXB), (w["L0_attn_conv_z_b"], SC_CZB),
                          (w["L0_ffn_bias"], SC_F0B)):
            kb.inc(s.dma_start(out=sc[:, dst:dst + 8],
                               in_=cget.rearrange("(a b) c -> b (a c)", b=128),
                               allow_slow_non_contiguous=True), "qa", 16)
        kb.inc(s.dma_start(out=sc[0:4, SC_AL:SC_AL + 1], in_=w["L0_attn_A_log"]), "qa", 16)
        kb.inc(s.dma_start(out=sc[0:4, SC_DTB:SC_DTB + 1], in_=w["L0_attn_dt_bias"]), "qa", 16)
        kb.inc(s.dma_start(out=sc[0:4, SC_BB:SC_BB + 1], in_=w["L0_attn_b_b"]), "qa", 16)
        kb.inc(s.dma_start(out=sc[:, SC_GNW:SC_GNW + 3],
                           in_=w["L0_attn_g_norm_w"].rearrange("(a b) c -> b (a c)", b=128),
                           allow_slow_non_contiguous=True), "qa", 16)
        kb.inc(s.dma_start(out=sc[:, SC_ONW:SC_ONW + 1], in_=w["L1_attn_o_norm_w"]), "qa", 16)
        kb.inc(s.dma_start(out=sc[0:8, SC_MXB:SC_MXB + 1], in_=w["L1_attn_mix_b"]), "qa", 16)
        kb.inc(s.dma_start(out=sc[0:8, SC_MXB + 1:SC_MXB + 2], in_=w["L1_attn_mix_bias"]), "qa", 16)
        kb.inc(s.dma_start(out=sc[:, SC_SIN:SC_SIN + 1],
                           in_=w["L0_ffn_input_scale"].broadcast_to((128, 1))), "qa", 16)
        kb.inc(s.dma_start(out=sc[:, SC_WS:SC_WS + 1],
                           in_=w["L0_ffn_weight_scale"].broadcast_to((128, 1))), "qa", 16)
    kb.ae(f)
    kb.wait("ve", "qa")
    def f(v):
        v.memset(sc[:, SC_STAT:SC_STAT + 60], 0.0)
        v.reciprocal(sc[:, SC_RSIN:SC_RSIN + 1], sc[:, SC_SIN:SC_SIN + 1])
        v.tensor_add(sc[0:8, SC_MXB:SC_MXB + 1], sc[0:8, SC_MXB:SC_MXB + 1],
                     sc[0:8, SC_MXB + 1:SC_MXB + 2])
        kb.incl(v, "ve")
    kb.ve(f)
    kb.wait("ae", "qa")
    def f(a):
        a.activation(sc[0:4, SC_EAL:SC_EAL + 1], sc[0:4, SC_AL:SC_AL + 1], AF.Exp)
        kb.incl(a, "ae")
    kb.ae(f)

    def emit_ln(ncols=NL, eps_off=CF_EPS5, quant=False, col0=0):
        tiles = _ntiles(ncols, 128)
        nt = len(tiles)
        kb.wait("te", "ve", "ae", "qs")
        extract_marks = {}
        for ti, (t0, tw) in enumerate(tiles):
            pp = P2 if ti % 2 == 0 else P3
            pq = P4 if ti % 2 == 0 else P5
            if ti >= 2:
                kb.waitm("te", [extract_marks[ti - 2]])
            def f(t, t0=t0, tw=tw, pp=pp, pq=pq):
                for c in range(8):
                    xt = res[:, c * NL + col0 + t0:c * NL + col0 + t0 + tw]
                    t.matmul(pp[0:tw, 0:tw], xt, xt, start=(c == 0), stop=(c == 7))
                    t.matmul(pq[0:tw, 0:1], xt, cfb[:, CF_ONES:CF_ONES + 1],
                             start=(c == 0), stop=(c == 7))
                kb.incl(t, "te")
            kb.te(f)
            tev = kb.n["te"]
            def f(v, ti=ti, tw=tw, pp=pp, pq=pq, tev=tev):
                v.wait_ge(kb.sem("te"), tev)
                v.tensor_tensor(sc[0:tw, SC_NRM:SC_NRM + tw], pp[0:tw, 0:tw],
                                identF[0:tw, 0:tw], ALU.mult)
                v.reduce_sum(sc[0:tw, SC_STAT + ti:SC_STAT + ti + 1],
                             sc[0:tw, SC_NRM:SC_NRM + tw], axis=AX.X)
                v.tensor_copy(sc[0:tw, SC_STAT + 20 + ti:SC_STAT + 21 + ti],
                              pq[0:tw, 0:1])
                kb.incl(v, "ve")
            kb.ve(f)
            extract_marks[ti] = ("ve", kb.n["ve"])
        def f(v):
            st = sc[:, SC_STAT:SC_STAT + 60]
            v.tensor_scalar(st[:, 20:20 + nt], st[:, 20:20 + nt], 1.0 / 1024.0, None, ALU.mult)
            v.tensor_scalar(st[:, 0:nt], st[:, 0:nt], 1.0 / 1024.0, None, ALU.mult)
            kb.incl(v, "ve")
        kb.ve(f)
        kb.wait("ae", "ve")
        def f(a):
            st = sc[:, SC_STAT:SC_STAT + 60]
            a.activation(st[:, 40:40 + nt], st[:, 20:20 + nt], AF.Square)
            kb.incl(a, "ae")
        kb.ae(f)
        kb.wait("ve", "ae")
        def f(v):
            st = sc[:, SC_STAT:SC_STAT + 60]
            v.tensor_tensor(st[:, 0:nt], st[:, 0:nt], st[:, 40:40 + nt], ALU.subtract)
            kb.incl(v, "ve")
        kb.ve(f)
        kb.wait("ae", "ve")
        def f(a):
            st = sc[:, SC_STAT:SC_STAT + 60]
            a.activation(st[:, 0:nt], st[:, 0:nt], AF.Sqrt,
                         bias=cfb[:, eps_off:eps_off + 1])
            kb.incl(a, "ae")
        kb.ae(f)
        kb.wait("ve", "ae")
        def f(v):
            st = sc[:, SC_STAT:SC_STAT + 60]
            v.reciprocal(st[:, 0:nt], st[:, 0:nt])
            if quant:
                v.tensor_scalar(st[:, 40:40 + nt], st[:, 0:nt],
                                sc[:, SC_RSIN:SC_RSIN + 1], None, ALU.mult)
            kb.incl(v, "ve")
        kb.ve(f)
        kb.wait("te", "ve")
        def f(t):
            t.transpose(P2[0:60, 0:128], sc[:, SC_STAT:SC_STAT + 60], identF)
            kb.incl(t, "te")
        kb.te(f)
        kb.wait("ve", "te")
        def f(v):
            v.tensor_copy(sc[0:60, SC_NRM:SC_NRM + 128], P2[0:60, 0:128])
            kb.incl(v, "ve")
        kb.ve(f)
        kb.wait("ae", "ve")
        def f(a):
            kb.inc(a.dma_start(out=stats_hbm, in_=sc[0:60, SC_NRM:SC_NRM + 128]), "qln", 16)
        kb.ae(f)
        kb.wait("ge", "qln")
        def f(g):
            flat_r = bass.AP(stats_hbm.tensor, 0, [[0, 128], [1, ncols]])
            flat_m = bass.AP(stats_hbm.tensor, 20 * 128, [[0, 128], [1, ncols]])
            kb.inc(g.dma_start(out=rrep[:, 0:ncols], in_=flat_r), "qg", 16)
            kb.inc(g.dma_start(out=mrep[:, 0:ncols], in_=flat_m), "qg", 16)
        kb.ge(f)
        if not quant:
            kb.wait("ve", "qg")
            def f(v):
                for c in range(8):
                    v.tensor_tensor(xln[:, c * NL:c * NL + ncols],
                                    res[:, c * NL + col0:c * NL + col0 + ncols],
                                    mrep[:, 0:ncols], ALU.subtract)
                    v.tensor_tensor(xln[:, c * NL:c * NL + ncols],
                                    xln[:, c * NL:c * NL + ncols], rrep[:, 0:ncols],
                                    ALU.mult)
                kb.incl(v, "ve")
            kb.ve(f)
        else:
            kb.wait("se", "te", "ve", "qln")
            for w0, ww in _ntiles(ncols, 2048):
                if w0 > 0:
                    kb.wait("se", "ve")
                def f(s, w0=w0, ww=ww):
                    flat_rp = bass.AP(stats_hbm.tensor, 40 * 128 + w0, [[0, 128], [1, ww]])
                    kb.inc(s.dma_start(out=wst[:, 0:ww], in_=flat_rp), "qln2", 16)
                kb.se(f)
                kb.wait("ve", "qln2", "qg")
                def f(v, w0=w0, ww=ww):
                    for c in range(8):
                        t = sc[:, SC_NRM:SC_NRM + ww]
                        v.tensor_tensor(t, res[:, c * NL + col0 + w0:c * NL + col0 + w0 + ww],
                                        mrep[:, w0:w0 + ww], ALU.subtract)
                        v.tensor_tensor(t, t, wst[:, 0:ww], ALU.mult)
                        for h0, hw in _ntiles(ww, 512):
                            v.tensor_scalar(q8t[:, 0:hw], t[:, h0:h0 + hw],
                                            448.0, -448.0, ALU.min, ALU.max)
                            v.tensor_copy(a2[:, c * NL + w0 + h0:c * NL + w0 + h0 + hw],
                                          q8t[:, 0:hw])
                        v.tensor_scalar(xln[:, c * NL + w0:c * NL + w0 + ww], t,
                                        sc[:, SC_SIN:SC_SIN + 1], None, ALU.mult)
                    kb.incl(v, "ve")
                kb.ve(f)

    def emit_proj(w_ap, kchunks, colchunks, ncols, moving, evac, bf16src=False):
        tiles = _ntiles(ncols, 512)
        ready = {}
        mm_done = {}
        psum_free = {}
        kb.wait("te", "ve", "ae")
        kb.wait("se", "te")
        for col in range(colchunks):
            slot = col % 2
            qw = "qw%d" % slot
            if bf16src:
                marks = [mm_done[col - 2]] if col >= 2 else []
                kb.waitm("se", marks)
                def f(s, col=col, slot=slot, qw=qw):
                    for k in range(kchunks):
                        kb.inc(s.dma_start(
                            out=wr[:, 2048 * slot + 128 * k:2048 * slot + 128 * k + 128],
                            in_=w_ap[128 * k:128 * (k + 1), 128 * col:128 * col + 128]),
                            qw, 16)
                kb.se(f)
                ready[col] = (qw, kb.n[qw])
            else:
                marks = [ready[col - 1]] if col >= 1 else []
                if col >= 2:
                    marks.append(mm_done[col - 2])
                kb.waitm("se", marks)
                def f(s, col=col, qw=qw):
                    for k in range(kchunks):
                        kb.inc(s.dma_start(
                            out=wst[:, 128 * k:128 * k + 128],
                            in_=w_ap[128 * k:128 * (k + 1), 128 * col:128 * col + 128]),
                            qw, 16)
                kb.se(f)
                marks = [(qw, kb.n[qw])]
                if col >= 2:
                    marks.append(mm_done[col - 2])
                kb.waitm("ve", marks)
                def f(v, slot=slot, kchunks=kchunks):
                    v.tensor_copy(wr[:, 2048 * slot:2048 * slot + 128 * kchunks],
                                  wst[:, 0:128 * kchunks])
                    kb.incl(v, "ve")
                kb.ve(f)
                ready[col] = ("ve", kb.n["ve"])
            kb.waitm("te", [ready[col]])
            for ti, (n0, nw) in enumerate(tiles):
                gi = col * len(tiles) + ti
                pp = P0 if gi % 2 == 0 else P1
                if gi >= 2:
                    kb.waitm("te", [psum_free[gi - 2]])
                def f(t, col=col, slot=slot, n0=n0, nw=nw, pp=pp, kchunks=kchunks):
                    for k in range(kchunks):
                        mv = moving(k, n0, nw)
                        t.matmul(pp[0:128, 0:nw],
                                 wr[0:mv.shape[0],
                                    2048 * slot + 128 * k:2048 * slot + 128 * k + 128],
                                 mv, start=(k == 0), stop=(k == kchunks - 1))
                    kb.incl(t, "te")
                kb.te(f)
                psum_free[gi] = evac(col, pp, n0, nw, ("te", kb.n["te"]))
            mm_done[col] = ("te", kb.n["te"])

    def xln_moving(k, n0, nw):
        return xln[:, k * NL + n0:k * NL + n0 + nw]

    def emit_trans_prep(src_ap, dst_ap, kchunks, colchunks, scale=None):
        for col in range(colchunks):
            kb.wait("se", "ve", "qa")
            def f(s, col=col):
                for k in range(kchunks):
                    kb.inc(s.dma_start(
                        out=wst[:, 128 * k:128 * k + 128],
                        in_=src_ap[128 * col:128 * col + 128,
                                   128 * k:128 * k + 128]), "qs", 16)
            kb.se(f)
            kb.wait("ve", "qs", "qa")
            if scale is not None:
                def f(v, kchunks=kchunks):
                    v.tensor_scalar(scb[:, 0:128 * kchunks], wst[:, 0:128 * kchunks],
                                    scale, None, ALU.mult)
                    kb.incl(v, "ve")
            else:
                def f(v, kchunks=kchunks):
                    v.tensor_copy(scb[:, 0:128 * kchunks], wst[:, 0:128 * kchunks])
                    kb.incl(v, "ve")
            kb.ve(f)
            kb.wait("te", "ve")
            def f(t, kchunks=kchunks):
                for k in range(kchunks):
                    t.transpose(PT[:, 128 * k:128 * k + 128],
                                scb[:, 128 * k:128 * k + 128], identB)
                kb.incl(t, "te")
            kb.te(f)
            kb.wait("ve", "te")
            def f(v, kchunks=kchunks):
                v.tensor_copy(scb[:, 0:128 * kchunks], PT[:, 0:128 * kchunks])
                kb.incl(v, "ve")
            kb.ve(f)
            kb.wait("ae", "ve")
            def f(a, col=col, kchunks=kchunks):
                for k in range(kchunks):
                    kb.inc(a.dma_start(
                        out=dst_ap[128 * k:128 * k + 128, 128 * col:128 * col + 128],
                        in_=scb[:, 128 * k:128 * k + 128]), "qa", 16)
            kb.ae(f)

    def emit_wf0_prep():
        emit_trans_prep(w["L0_ffn_weight_fp8"], wf0T_hbm, 8, 8,
                        scale=sc[:, SC_WS:SC_WS + 1])

    def emit_ffn0(ncols, col0=0):
        def moving(k, n0, nw):
            return a2[:, k * NL + n0:k * NL + n0 + nw]

        sl_state = [0]
        slot_dve = {0: None, 1: None}

        def evac(col, pp, n0, nw, tem):
            sl = sl_state[0] % 2
            sl_state[0] += 1
            prev = slot_dve[sl]
            def f(a, col=col, nw=nw, pp=pp, tem=tem, sl=sl, prev=prev):
                a.wait_ge(kb.sem(tem[0]), tem[1])
                if prev is not None:
                    a.wait_ge(kb.sem(prev[0]), prev[1])
                a.activation(sc[:, SC_NRM + 600 * sl:SC_NRM + 600 * sl + nw],
                             pp[0:128, 0:nw], AF.Identity,
                             bias=sc[:, SC_F0B + col:SC_F0B + col + 1],
                             scale=sc[:, SC_SIN:SC_SIN + 1])
                kb.incl(a, "ae")
            kb.ae(f)
            aev = kb.n["ae"]
            def f(v, col=col, n0=n0, nw=nw, aev=aev, sl=sl):
                v.wait_ge(kb.sem("ae"), aev)
                v.scalar_tensor_tensor(res[:, col * NL + col0 + n0:col * NL + col0 + n0 + nw],
                                       sc[:, SC_NRM + 600 * sl:SC_NRM + 600 * sl + nw],
                                       0.5, res[:, col * NL + col0 + n0:col * NL + col0 + n0 + nw],
                                       ALU.mult, ALU.add)
                kb.incl(v, "ve")
            kb.ve(f)
            slot_dve[sl] = ("ve", kb.n["ve"])
            return ("ae", aev)

        kb.wait("se", "qa")
        emit_proj(wf0T_hbm, 8, 8, ncols, moving, evac, bf16src=True)
        kb.wait("ae", "ve")

    emit_wf0_prep()
    emit_ln(quant=True)
    emit_ffn0(NL)

    d = dbg_out("x1", [D, NL])
    if d is not None:
        kb.wait("ae", "ve")
        def f(a, d=d):
            for c in range(8):
                kb.inc(a.dma_start(out=d[128 * c:128 * (c + 1), :],
                                   in_=res[:, c * NL:(c + 1) * NL]), "qa", 16)
            a.wait_ge(kb.sem("qa"), kb.n["qa"])
        kb.ae(f)

    emit_ln()
    d = dbg_out("xln2", [D, NL], BF16)
    if d is not None:
        kb.wait("ae", "ve")
        def f(a, d=d):
            for c in range(8):
                kb.inc(a.dma_start(out=d[128 * c:128 * (c + 1), :],
                                   in_=xln[:, c * NL:(c + 1) * NL]), "qa", 16)
            a.wait_ge(kb.sem("qa"), kb.n["qa"])
        kb.ae(f)

    kb.wait("ge", "ve")
    def f(g):
        kb.inc(g.dma_start(out=rrep[:, 0:NL], in_=hmask.broadcast_to((128, NL))),
               "qg", 16)
    kb.ge(f)
    kb.wait("ve", "qg")

    def conv_half(zhalf):
        col_off = 8 if zhalf else 0
        store = {}

        def evac(col, pp, n0, nw, tem):
            def f(v, col=col, pp=pp, n0=n0, nw=nw, tem=tem):
                v.wait_ge(kb.sem(tem[0]), tem[1])
                v.tensor_copy(a1[:, col * NL + n0:col * NL + n0 + nw], pp[0:128, 0:nw])
                kb.incl(v, "ve")
            kb.ve(f)
            return ("ve", kb.n["ve"])

        emit_proj(w["L0_attn_W_in"][:, col_off * 128:(col_off + 8) * 128],
                  8, 8, NL, xln_moving, evac)
        tapoff = SC_CZW if zhalf else SC_CXW
        boff = SC_CZB if zhalf else SC_CXB
        kb.wait("ve", "te")
        for c in range(8):
            def f(v, c=c):
                xc = a1[:, c * NL:(c + 1) * NL]
                v.tensor_tensor(xc[:, 0:4], xc[:, 0:4], rrep[:, 0:4], ALU.mult)
                v.tensor_tensor(xc[:, NL - 2:NL], xc[:, NL - 2:NL],
                                rrep[:, NL - 2:NL], ALU.mult)
                v.tensor_scalar(mrep[:, 0:2051], xc[:, 0:2051],
                                sc[:, tapoff + 4 * c:tapoff + 4 * c + 1], None, ALU.mult)
                for j in range(1, 4):
                    v.scalar_tensor_tensor(mrep[:, 0:2051], xc[:, j:j + 2051],
                                           sc[:, tapoff + 4 * c + j:tapoff + 4 * c + j + 1],
                                           mrep[:, 0:2051], ALU.mult, ALU.add)
                kb.incl(v, "ve")
            kb.ve(f)
            vev = kb.n["ve"]
            def f(a, c=c, vev=vev):
                a.wait_ge(kb.sem("ve"), vev)
                a.activation(a2[:, c * NL:c * NL + 2051], mrep[:, 0:2051], AF.Silu,
                             bias=sc[:, boff + c:boff + c + 1])
                kb.incl(a, "ae")
            kb.ae(f)
            kb.wait("ve", "ae")
        if zhalf:
            kb.wait("ae", "ae")
            def f(a):
                for c in range(8):
                    kb.inc(a.dma_start(out=zs_hbm[128 * c:128 * (c + 1), :],
                                       in_=a2[:, c * NL:c * NL + 2051]), "qa", 16)
            kb.ae(f)

    conv_half(zhalf=True)
    d = dbg_out("zs", [D, 2051], BF16)
    if d is not None:
        kb.wait("ae", "ve", "ae")
        def f(a, d=d):
            for c in range(8):
                kb.inc(a.dma_start(out=d[128 * c:128 * (c + 1), :],
                                   in_=a2[:, c * NL:c * NL + 2051]), "qa", 16)
            a.wait_ge(kb.sem("qa"), kb.n["qa"])
        kb.ae(f)
    kb.wait("ae", "qa")
    conv_half(zhalf=False)
    kb.wait("ve", "ae", "qa")
    def f(v):
        v.tensor_copy(xln[:, 0:8 * NL], a2[:, 0:8 * NL])
        kb.incl(v, "ve")
    kb.ve(f)

    d = dbg_out("xs", [D, 2051], BF16)
    if d is not None:
        kb.wait("ae", "ve")
        def f(a, d=d):
            for c in range(8):
                kb.inc(a.dma_start(out=d[128 * c:128 * (c + 1), :],
                                   in_=xln[:, c * NL:c * NL + 2051]), "qa", 16)
            a.wait_ge(kb.sem("qa"), kb.n["qa"])
        kb.ae(f)

    XSOFF = 3

    def xs_moving(k, n0, nw):
        return xln[:, k * NL + XSOFF + n0:k * NL + XSOFF + n0 + nw]

    def emit_qk(which):
        w_ap = w["L0_attn_W_q"] if which == "q" else w["L0_attn_W_k"]
        row0 = 0 if which == "q" else 768
        kb.wait("ve", "qa")

        def evac(col, pp, n0, nw, tem):
            def f(v, col=col, pp=pp, n0=n0, nw=nw, tem=tem):
                v.wait_ge(kb.sem(tem[0]), tem[1])
                v.tensor_copy(a1[:, col * NL + n0:col * NL + n0 + nw], pp[0:128, 0:nw])
                kb.incl(v, "ve")
            kb.ve(f)
            return ("ve", kb.n["ve"])

        emit_proj(w_ap, 8, 6, SPAN, xs_moving, evac)
        kb.wait("ae", "ve")
        def f(a):
            for c in range(6):
                a.activation(a2[:, c * NL:c * NL + SPAN],
                             a1[:, c * NL:c * NL + SPAN], AF.Square)
            kb.incl(a, "ae")
        kb.ae(f)
        seg = {0: [(0, CB_FULL), (1, CB_TOP)], 1: [(1, CB_BOT), (2, CB_FULL)],
               2: [(3, CB_FULL), (4, CB_TOP)], 3: [(4, CB_BOT), (5, CB_FULL)]}
        kb.wait("te", "ae")
        for h in range(4):
            for ti, (n0, nw) in enumerate(_ntiles(SPAN, 512)):
                def f(t, h=h, n0=n0, nw=nw):
                    for si, (c, p) in enumerate(seg[h]):
                        t.matmul(P4[0:1, 0:nw], cbb[:, p:p + 1],
                                 a2[:, c * NL + n0:c * NL + n0 + nw],
                                 start=(si == 0), stop=(si == 1))
                    kb.incl(t, "te")
                kb.te(f)
                tev = kb.n["te"]
                def f(v, n0=n0, nw=nw, tev=tev):
                    v.wait_ge(kb.sem("te"), tev)
                    v.tensor_scalar(sc[0:1, SC_NRM + n0:SC_NRM + n0 + nw],
                                    P4[0:1, 0:nw], 1e-6, None, ALU.add)
                    kb.incl(v, "ve")
                kb.ve(f)
                kb.wait("te", "ve")
            kb.wait("ae", "ve")
            def f(a):
                a.activation(sc[0:1, SC_NRM:SC_NRM + SPAN],
                             sc[0:1, SC_NRM:SC_NRM + SPAN], AF.Sqrt)
                kb.incl(a, "ae")
            kb.ae(f)
            kb.wait("ve", "ae")
            def f(v):
                v.reciprocal(sc[0:1, SC_NRM:SC_NRM + SPAN], sc[0:1, SC_NRM:SC_NRM + SPAN])
                kb.incl(v, "ve")
            kb.ve(f)
            kb.wait("ae", "ve")
            def f(a, h=h):
                kb.inc(a.dma_start(out=rows_hbm[h:h + 1, 0:SPAN],
                                   in_=sc[0:1, SC_NRM:SC_NRM + SPAN]), "qnr", 16)
            kb.ae(f)
            kb.wait("se", "qnr")
            def f(s, h=h):
                kb.inc(s.dma_start(out=mrep[:, 0:SPAN],
                                   in_=rows_hbm[h:h + 1, 0:SPAN].broadcast_to((128, SPAN))),
                       "qnb", 16)
            kb.se(f)
            kb.wait("ve", "qnb")
            for (c, p) in seg[h]:
                pr = (slice(0, 128) if p == CB_FULL else
                      (slice(0, 64) if p == CB_TOP else slice(64, 128)))
                def f(v, c=c, pr=pr):
                    v.tensor_tensor(a1[pr, c * NL:c * NL + SPAN],
                                    a1[pr, c * NL:c * NL + SPAN],
                                    mrep[pr, 0:SPAN], ALU.mult)
                    kb.incl(v, "ve")
                kb.ve(f)
            kb.wait("se", "ve")
        kb.wait("ae", "ve")
        def f(a, row0=row0):
            for c in range(6):
                kb.inc(a.dma_start(out=ag1b_i[row0 + 128 * c:row0 + 128 * (c + 1), :],
                                   in_=a1[:, c * NL:c * NL + SPAN]), "qa", 16)
        kb.ae(f)

    emit_qk("q")
    emit_qk("k")

    def emit_v():
        def evac(col, pp, n0, nw, tem):
            sl = (col * 4 + n0 // 512) % 2
            kb.wait("ve", "qa")
            def f(v, pp=pp, nw=nw, tem=tem, sl=sl):
                v.wait_ge(kb.sem(tem[0]), tem[1])
                v.tensor_copy(scb[:, 1024 * sl:1024 * sl + nw], pp[0:128, 0:nw])
                kb.incl(v, "ve")
            kb.ve(f)
            vev = kb.n["ve"]
            def f(a, col=col, n0=n0, nw=nw, vev=vev, sl=sl):
                a.wait_ge(kb.sem("ve"), vev)
                kb.inc(a.dma_start(
                    out=ag1b_i[1536 + 128 * col:1536 + 128 * col + 128, n0:n0 + nw],
                    in_=scb[:, 1024 * sl:1024 * sl + nw]), "qa", 16)
            kb.ae(f)
            return ("ve", vev)
        emit_proj(w["L0_attn_W_v"], 8, 12, SPAN, xs_moving, evac)

    emit_v()

    def emit_gkbeta():
        kb.wait("se", "ve", "qa")
        def f(s):
            for k in range(8):
                kb.inc(s.dma_start(out=wst[:, 4 * k:4 * k + 4],
                                   in_=w["L0_attn_W_gk"][128 * k:128 * k + 128, :]),
                       "qs", 16)
                kb.inc(s.dma_start(out=wst[:, 32 + 4 * k:32 + 4 * k + 4],
                                   in_=w["L0_attn_W_b"][128 * k:128 * k + 128, :]),
                       "qs", 16)
        kb.se(f)
        kb.wait("ve", "qs")
        def f(v):
            v.tensor_copy(scb[:, 0:64], wst[:, 0:64])
            kb.incl(v, "ve")
        kb.ve(f)
        kb.wait("te", "ve")
        for ti, (n0, nw) in enumerate(_ntiles(SPAN, 512)):
            def f(t, n0=n0, nw=nw):
                for k in range(8):
                    t.matmul(P0[0:4, 0:nw], scb[:, 4 * k:4 * k + 4],
                             xs_moving(k, n0, nw), start=(k == 0), stop=(k == 7))
                for k in range(8):
                    t.matmul(P1[0:4, 0:nw], scb[:, 32 + 4 * k:32 + 4 * k + 4],
                             xs_moving(k, n0, nw), start=(k == 0), stop=(k == 7))
                kb.incl(t, "te")
            kb.te(f)
            tev = kb.n["te"]
            def f(a, n0=n0, nw=nw, tev=tev):
                a.wait_ge(kb.sem("te"), tev)
                a.activation(sc[0:4, SC_NRM + n0:SC_NRM + n0 + nw], P0[0:4, 0:nw],
                             AF.Identity, bias=sc[0:4, SC_DTB:SC_DTB + 1])
                a.activation(sc[32:36, SC_NRM + n0:SC_NRM + n0 + nw], P1[0:4, 0:nw],
                             AF.Sigmoid, bias=sc[0:4, SC_BB:SC_BB + 1])
                kb.incl(a, "ae")
            kb.ae(f)
            kb.wait("te", "ae")
        kb.wait("ve", "ae")
        def f(v):
            v.tensor_relu(mrep[0:4, 0:SPAN], sc[0:4, SC_NRM:SC_NRM + SPAN])
            kb.incl(v, "ve")
        kb.ve(f)
        kb.wait("ae", "ve")
        def f(a):
            u = sc[0:4, SC_NRM:SC_NRM + SPAN]
            a.activation(u, u, AF.Abs)
            a.activation(u, u, AF.Exp, scale=-1.0)
            a.activation(u, u, AF.Ln, bias=1.0)
            kb.incl(a, "ae")
        kb.ae(f)
        kb.wait("ve", "ae")
        def f(v):
            u = sc[0:4, SC_NRM:SC_NRM + SPAN]
            v.tensor_tensor(u, u, mrep[0:4, 0:SPAN], ALU.add)
            v.tensor_scalar(u, u, sc[0:4, SC_EAL:SC_EAL + 1], -1.0, ALU.mult, ALU.mult)
            kb.incl(v, "ve")
        kb.ve(f)
        kb.wait("ae", "ve")
        def f(a):
            kb.inc(a.dma_start(out=ag1f_i[0:4, :], in_=sc[0:4, SC_NRM:SC_NRM + SPAN]),
                   "qa", 16)
            kb.inc(a.dma_start(out=ag1f_i[4:8, :], in_=sc[32:36, SC_NRM:SC_NRM + SPAN]),
                   "qa", 16)
        kb.ae(f)

    emit_gkbeta()

    def emit_g():
        def evac(col, pp, n0, nw, tem):
            sl = (col * 5 + n0 // 512) % 2
            kb.wait("ve", "qa")
            def f(v, pp=pp, n0=n0, nw=nw, tem=tem, sl=sl):
                v.wait_ge(kb.sem(tem[0]), tem[1])
                v.tensor_copy(scb[:, 1024 * sl:1024 * sl + nw], pp[0:128, 0:nw])
                kb.incl(v, "ve")
            kb.ve(f)
            vev = kb.n["ve"]
            def f(a, col=col, n0=n0, nw=nw, vev=vev, sl=sl):
                a.wait_ge(kb.sem("ve"), vev)
                kb.inc(a.dma_start(out=g_hbm[128 * col:128 * col + 128, n0:n0 + nw],
                                   in_=scb[:, 1024 * sl:1024 * sl + nw]), "qa", 16)
            kb.ae(f)
            return ("ve", vev)

        def g_moving(k, n0, nw):
            return xln[:, k * NL + n0:k * NL + n0 + nw]

        emit_proj(w["L0_attn_W_g"], 8, 12, 2051, g_moving, evac)

    emit_g()

    d = dbg_out("ag1b", [3072, SPAN], BF16)
    if d is not None:
        kb.wait("ae", "qa", "ve")
        def f(a, d=d):
            kb.inc(a.dma_start(out=d, in_=ag1b_i), "qa", 16)
            a.wait_ge(kb.sem("qa"), kb.n["qa"])
        kb.ae(f)
    d = dbg_out("ag1f", [8, SPAN], F32)
    if d is not None:
        kb.wait("ae", "qa", "ve")
        def f(a, d=d):
            kb.inc(a.dma_start(out=d, in_=ag1f_i), "qa", 16)
            a.wait_ge(kb.sem("qa"), kb.n["qa"])
        kb.ae(f)

    kb.wait("ge", "qa", "qs", "ve", "ae", "te")
    def f(g):
        kb.inc(g.collective_compute("AllGather", ALU.bypass,
                                    replica_groups=[list(range(8))],
                                    ins=[ag1b_i.opt()], outs=[ag1b_o.opt()]), "cc")
        kb.inc(g.collective_compute("AllGather", ALU.bypass,
                                    replica_groups=[list(range(8))],
                                    ins=[ag1f_i.opt()], outs=[ag1f_o.opt()]), "cc")
    kb.ge(f)

    GX_GB = 0
    GX_SLAB = 4
    GX_OC = 32
    GX_OH = 44
    GX_L1 = 56
    GX_B1 = 80
    GX_OE = 84

    A1_QH, A1_QL, A1_KH, A1_KL = 0, 2048, 4096, 6144
    A1_V0, A1_V1, A1_V2 = 8192, 10240, 12288
    A1_MT, A1_DT, A1_NT, A1_DD = 14336, 14464, 14592, 14720
    A1_GT, A1_PW, A1_PWT, A1_TF = 14848, 14976, 15104, 15232
    A1_TT, A1_LT, A1_TS = 15360, 15488, 15616
    A1_GB = 16000
    A2_KTM, A2_VTM, A2_VT, A2_KBAR, A2_KTIL = 0, 192, 576, 960, 1152
    A2_QBH, A2_QBL, A2_WHI, A2_WLO = 1344, 1472, 1600, 1728
    A2_EMT, A2_EDT, A2_ENT, A2_ELT, A2_ES = 1856, 1984, 2112, 2240, 2368
    A2_SBH, A2_SBL, A2_OST = 2496, 2880, 3264
    SD_C, SD_CST, SD_CEND, SD_TIL, SD_BET, SD_BEC = 0, 16, 32, 48, 64, 80

    def delta_rowprep_L0(j):
        kb.sync_step("ge")
        def f(g):
            kb.inc(g.indirect_dma_start(
                out=mrep[0:64, 0:SPAN], out_offset=None, in_=ag1f_o,
                in_offset=bass.IndirectOffsetOnAxis(ap=gix[0:64, GX_GB + j:GX_GB + j + 1], axis=0)),
                "qgB", 16)
        kb.ge(f)
        kb.sync_step("ve")
        def f(v):
            v.tensor_tensor_scan(sc[0:1, SC_NRM:SC_NRM + SPAN], mrep[0:1, 0:SPAN],
                                 mrep[0:1, 0:SPAN], 0.0, ALU.add, ALU.bypass)
            v.tensor_copy(sc[0:1, SC_STAT:SC_STAT + 16],
                          bass.AP(sc.tensor if hasattr(sc, "tensor") else None, 0, []) if False else
                          sc[0:1, SC_NRM + 127:SC_NRM + SPAN].rearrange("a (b c) -> a (b c)") if False else
                          sc[0:1, SC_NRM:SC_NRM + SPAN][:, 127::128])
            v.memset(sc[0:1, SC_STAT + 16:SC_STAT + 17], 0.0)
            v.tensor_copy(sc[0:1, SC_STAT + 17:SC_STAT + 32],
                          sc[0:1, SC_STAT:SC_STAT + 15])
            v.tensor_copy(mrep[0:1, 0:SPAN], mrep[32:33, 0:SPAN])
            kb.incl(v, "ve")
        kb.ve(f)
        kb.sync_step("ae")
        def f(a):
            a.activation(mrep[0:1, 0:SPAN], mrep[0:1, 0:SPAN], AF.Ln)
            kb.incl(a, "ae")
        kb.ae(f)
        kb.sync_step("ve")
        def f(v):
            v.tensor_scalar(mrep[0:1, 0:SPAN], mrep[0:1, 0:SPAN], -1e4, None, ALU.max)
            v.tensor_tensor(mrep[0:1, 0:SPAN], mrep[0:1, 0:SPAN],
                            sc[0:1, SC_NRM:SC_NRM + SPAN], ALU.add)
            kb.incl(v, "ve")
        kb.ve(f)
        kb.sync_step("ae")
        def f(a):
            kb.inc(a.dma_start(out=rows_hbm[0:1, 0:SPAN], in_=sc[0:1, SC_NRM:SC_NRM + SPAN]), "qrw", 16)
            kb.inc(a.dma_start(out=rows_hbm[1:2, 0:SPAN], in_=mrep[0:1, 0:SPAN]), "qrw", 16)
            kb.inc(a.dma_start(out=rows_hbm[2:3, 0:16], in_=sc[0:1, SC_STAT:SC_STAT + 16]), "qrw", 16)
            kb.inc(a.dma_start(out=rows_hbm[3:4, 0:16], in_=sc[0:1, SC_STAT + 16:SC_STAT + 32]), "qrw", 16)
            kb.inc(a.dma_start(out=rows_hbm[4:5, 0:SPAN], in_=mrep[32:33, 0:SPAN]), "qrw", 16)
        kb.ae(f)
        kb.sync_step("se")
        def f(s):
            rh = rows_hbm.tensor
            LL = rows_hbm.shape[1] if hasattr(rows_hbm, "shape") else L
            kb.inc(s.dma_start(out=mrep[:, 0:SPAN],
                               in_=bass.AP(rh, 0, [[0, 128], [1, SPAN]])), "qbc", 16)
            kb.inc(s.dma_start(out=sc[:, SC_NRM:SC_NRM + SPAN],
                               in_=bass.AP(rh, L, [[0, 128], [1, SPAN]])), "qbc", 16)
            kb.inc(s.dma_start(out=wst[0:16, 1024:1152],
                               in_=bass.AP(rh, 0, [[128, 16], [1, 128]])), "qbc", 16)
            kb.inc(s.dma_start(out=sd[:, SD_CEND:SD_CEND + 16],
                               in_=bass.AP(rh, 2 * L, [[0, 128], [1, 16]])), "qbc", 16)
            kb.inc(s.dma_start(out=sd[:, SD_CST:SD_CST + 16],
                               in_=bass.AP(rh, 3 * L, [[0, 128], [1, 16]])), "qbc", 16)
            kb.inc(s.dma_start(out=wst[0:16, 1152:1280],
                               in_=bass.AP(rh, 4 * L, [[128, 16], [1, 128]])), "qbc", 16)
        kb.se(f)
        kb.sync_step("te")
        def f(t):
            t.transpose(P2[0:128, 0:16], wst[0:16, 1024:1152], identF[0:16, 0:16])
            t.transpose(P3[0:128, 0:16], wst[0:16, 1152:1280], identF[0:16, 0:16])
            kb.incl(t, "te")
        kb.te(f)
        kb.sync_step("ve")
        def f(v):
            v.tensor_copy(sd[:, SD_C:SD_C + 16], P2[0:128, 0:16])
            v.tensor_copy(sd[:, SD_BET:SD_BET + 16], P3[0:128, 0:16])
            kb.incl(v, "ve")
        kb.ve(f)
        kb.sync_step("ve")
        def f(v):
            v.tensor_tensor(sd[:, SD_TIL:SD_TIL + 16], sd[:, SD_CEND:SD_CEND + 16],
                            sd[:, SD_C:SD_C + 16], ALU.subtract)
            v.tensor_tensor(sd[:, SD_BEC:SD_BEC + 16], sd[:, SD_C:SD_C + 16],
                            sd[:, SD_CST:SD_CST + 16], ALU.subtract)
            kb.incl(v, "ve")
        kb.ve(f)
        kb.sync_step("ae")
        def f(a):
            a.activation(sd[:, SD_TIL:SD_TIL + 16], sd[:, SD_TIL:SD_TIL + 16], AF.Exp)
            a.activation(sd[:, SD_BEC:SD_BEC + 16], sd[:, SD_BEC:SD_BEC + 16], AF.Exp)
            kb.incl(a, "ae")
        kb.ae(f)
        kb.sync_step("ve")
        def f(v):
            v.tensor_tensor(sd[:, SD_CEND:SD_CEND + 16], sd[:, SD_CEND:SD_CEND + 16],
                            sd[:, SD_CST:SD_CST + 16], ALU.subtract)
            v.tensor_tensor(sd[:, SD_BEC:SD_BEC + 16], sd[:, SD_BEC:SD_BEC + 16],
                            sd[:, SD_BET:SD_BET + 16], ALU.mult)
            kb.incl(v, "ve")
        kb.ve(f)
        kb.sync_step("ae")
        def f(a):
            a.activation(sd[:, SD_CEND:SD_CEND + 16], sd[:, SD_CEND:SD_CEND + 16], AF.Exp)
            kb.incl(a, "ae")
        kb.ae(f)

    def delta_slabs_L0(j):
        kb.sync_step("ge")
        def f(g):
            for t, (col, n_p) in enumerate([(A1_QH, 128), (A1_QL, 64), (A1_KH, 128),
                                            (A1_KL, 64), (A1_V0, 128), (A1_V1, 128),
                                            (A1_V2, 128)]):
                kb.inc(g.indirect_dma_start(
                    out=a1[0:n_p, col:col + SPAN], out_offset=None, in_=ag1b_o,
                    in_offset=bass.IndirectOffsetOnAxis(
                        ap=gix[0:n_p, GX_SLAB + 7 * j + t:GX_SLAB + 7 * j + t + 1], axis=0)),
                    "qgB", 16)
        kb.ge(f)

    def delta_chunk_L0(j, c):
        tok0 = 128 * c
        cS = lambda off: sd[:, off + c:off + c + 1]
        kb.sync_step("se")
        def f(s):
            kb.inc(s.dma_start(out=a2[:, A2_KTM:A2_KTM + 128],
                               in_=a1[:, A1_KH + tok0:A1_KH + tok0 + 128], transpose=True), "qt0", 16)
            kb.inc(s.dma_start(out=a2[:, A2_KTM + 128:A2_KTM + 192],
                               in_=a1[0:64, A1_KL + tok0:A1_KL + tok0 + 128], transpose=True), "qt0", 16)
            for t, col in enumerate((A1_V0, A1_V1, A1_V2)):
                kb.inc(s.dma_start(out=a2[:, A2_VTM + 128 * t:A2_VTM + 128 * (t + 1)],
                                   in_=a1[:, col + tok0:col + tok0 + 128], transpose=True), "qt0", 16)
        kb.se(f)
        kb.sync_step("te")
        def f(t):
            t.matmul(P2[0:128, 0:128], a1[:, A1_KH + tok0:A1_KH + tok0 + 128],
                     a1[:, A1_KH + tok0:A1_KH + tok0 + 128], start=True, stop=False)
            t.matmul(P2[0:128, 0:128], a1[0:64, A1_KL + tok0:A1_KL + tok0 + 128],
                     a1[0:64, A1_KL + tok0:A1_KL + tok0 + 128], start=False, stop=True)
            t.matmul(P3[0:128, 0:128], a1[:, A1_KH + tok0:A1_KH + tok0 + 128],
                     a1[:, A1_QH + tok0:A1_QH + tok0 + 128], start=True, stop=False)
            t.matmul(P3[0:128, 0:128], a1[0:64, A1_KL + tok0:A1_KL + tok0 + 128],
                     a1[0:64, A1_QL + tok0:A1_QL + tok0 + 128], start=False, stop=True)
            kb.incl(t, "te")
        kb.te(f)
        kb.sync_step("ve")
        def f(v):
            cln = sc[:, SC_NRM + tok0:SC_NRM + tok0 + 128]
            crp = mrep[:, tok0:tok0 + 128]
            v.scalar_tensor_tensor(wst[:, 0:128], crp, cS(SD_C), cfm(CF_MB_UP_INCL),
                                   ALU.subtract, ALU.add)
            v.scalar_tensor_tensor(wst[:, 128:256], cln, cS(SD_C), cfm(CF_MB_BD),
                                   ALU.subtract, ALU.add)
            v.scalar_tensor_tensor(wst[:, 1280:1408], cln, cS(SD_C), cfm(CF_MB_BELOW),
                                   ALU.subtract, ALU.add)
            kb.incl(v, "ve")
        kb.ve(f)
        kb.sync_step("ae")
        def f(a):
            a.activation(a2[:, A2_ELT:A2_ELT + 128], wst[:, 0:128], AF.Exp)
            a.activation(a2[:, A2_EDT:A2_EDT + 128], wst[:, 128:256], AF.Exp)
            a.activation(a2[:, A2_ENT:A2_ENT + 128], wst[:, 1280:1408], AF.Exp)
            kb.incl(a, "ae")
        kb.ae(f)
        kb.sync_step("ve")
        def f(v):
            v.tensor_tensor(a1[:, A1_DT:A1_DT + 128], P2[0:128, 0:128],
                            a2[:, A2_EDT:A2_EDT + 128], ALU.mult)
            v.tensor_tensor(a1[:, A1_NT:A1_NT + 128], P2[0:128, 0:128],
                            a2[:, A2_ENT:A2_ENT + 128], ALU.mult)
            v.tensor_tensor(a1[:, A1_LT:A1_LT + 128], P3[0:128, 0:128],
                            a2[:, A2_ELT:A2_ELT + 128], ALU.mult)
            v.transpose(a1[:, A1_DD:A1_DD + 128], a1[:, A1_DT:A1_DT + 128])
            v.scalar_tensor_tensor(a1[:, A1_GT:A1_GT + 128], a1[:, A1_DT:A1_DT + 128],
                                   -1.0, identB, ALU.mult, ALU.add)
            kb.incl(v, "ve")
        kb.ve(f)
        kb.sync_step("te")
        def f(t):
            t.matmul(P4[0:128, 0:128], a1[:, A1_DD:A1_DD + 128],
                     a1[:, A1_DT:A1_DT + 128], start=True, stop=True)
            kb.incl(t, "te")
        kb.te(f)
        for lvl in range(4):
            kb.sync_step("ve")
            def f(v):
                v.tensor_copy(a1[:, A1_PW:A1_PW + 128], P4[0:128, 0:128])
                v.transpose(a1[:, A1_PWT:A1_PWT + 128], a1[:, A1_PW:A1_PW + 128])
                kb.incl(v, "ve")
            kb.ve(f)
            kb.sync_step("te")
            def f(t, lvl=lvl):
                t.matmul(P5[0:128, 0:128], a1[:, A1_PWT:A1_PWT + 128],
                         a1[:, A1_GT:A1_GT + 128], start=True, stop=True)
                if lvl < 3:
                    t.matmul(P4[0:128, 0:128], a1[:, A1_PWT:A1_PWT + 128],
                             a1[:, A1_PW:A1_PW + 128], start=True, stop=True)
                kb.incl(t, "te")
            kb.te(f)
            kb.sync_step("ve")
            def f(v):
                v.tensor_tensor(a1[:, A1_GT:A1_GT + 128], a1[:, A1_GT:A1_GT + 128],
                                P5[0:128, 0:128], ALU.add)
                kb.incl(v, "ve")
            kb.ve(f)
        kb.sync_step("ve")
        def f(v):
            v.transpose(a1[:, A1_GB:A1_GB + 128], a1[:, A1_GT:A1_GT + 128])
            v.tensor_copy(a1[:, A1_TF:A1_TF + 128], a1[:, A1_GB:A1_GB + 128])
            kb.incl(v, "ve")
        kb.ve(f)
        for it in range(3):
            kb.sync_step("te")
            def f(t):
                t.matmul(P4[0:128, 0:128], a1[:, A1_NT:A1_NT + 128],
                         a1[:, A1_TF:A1_TF + 128], start=True, stop=True)
                kb.incl(t, "te")
            kb.te(f)
            kb.sync_step("ve")
            def f(v):
                v.tensor_copy(a1[0:128, A1_TS:A1_TS + 128], P4[0:128, 0:128])
                kb.incl(v, "ve")
            kb.ve(f)
            kb.sync_step("te")
            def f(t):
                t.matmul(P5[0:128, 0:128], a1[:, A1_GT:A1_GT + 128],
                         a1[:, A1_TS:A1_TS + 128], start=True, stop=True)
                kb.incl(t, "te")
            kb.te(f)
            kb.sync_step("ve")
            def f(v):
                v.tensor_tensor(a1[:, A1_TF:A1_TF + 128], a1[:, A1_GB:A1_GB + 128],
                                P5[0:128, 0:128], ALU.subtract)
                kb.incl(v, "ve")
            kb.ve(f)
        kb.sync_step("te")
        def f(t):
            t.transpose(PT[:, 0:128], a1[:, A1_TF:A1_TF + 128], identB)
            kb.incl(t, "te")
        kb.te(f)
        kb.sync_step("ve")
        def f(v):
            v.tensor_copy(a1[:, A1_TT:A1_TT + 128], PT[:, 0:128])
            v.tensor_scalar(a2[:, A2_VT:A2_VT + 384], a2[:, A2_VTM:A2_VTM + 384],
                            cS(SD_BET), None, ALU.mult)
            v.tensor_scalar(a2[:, A2_KBAR:A2_KBAR + 192], a2[:, A2_KTM:A2_KTM + 192],
                            cS(SD_BEC), None, ALU.mult)
            v.tensor_scalar(a2[:, A2_KTIL:A2_KTIL + 192], a2[:, A2_KTM:A2_KTM + 192],
                            cS(SD_TIL), None, ALU.mult)
            v.tensor_scalar(wst[:, 0:128], mrep[:, tok0:tok0 + 128], cS(SD_CST),
                            None, ALU.subtract)
            kb.incl(v, "ve")
        kb.ve(f)
        kb.sync_step("ae")
        def f(a):
            a.activation(a2[:, A2_ES:A2_ES + 128], wst[:, 0:128], AF.Exp)
            kb.incl(a, "ae")
        kb.ae(f)
        kb.sync_step("ve")
        def f(v):
            v.tensor_tensor(a2[:, A2_QBH:A2_QBH + 128], a1[:, A1_QH + tok0:A1_QH + tok0 + 128],
                            a2[:, A2_ES:A2_ES + 128], ALU.mult)
            v.tensor_tensor(a2[0:64, A2_QBL:A2_QBL + 128], a1[0:64, A1_QL + tok0:A1_QL + tok0 + 128],
                            a2[0:64, A2_ES:A2_ES + 128], ALU.mult)
            kb.incl(v, "ve")
        kb.ve(f)
        kb.sync_step("te")
        def f(t):
            t.matmul(P4[0:128, 0:384], a1[:, A1_TT:A1_TT + 128],
                     a2[:, A2_VT:A2_VT + 384], start=True, stop=True)
            t.matmul(P5[0:128, 0:128], a2[:, A2_KBAR:A2_KBAR + 128],
                     a1[:, A1_TT:A1_TT + 128], start=True, stop=True)
            t.matmul(P5[0:64, 128:256], a2[:, A2_KBAR + 128:A2_KBAR + 192],
                     a1[:, A1_TT:A1_TT + 128], start=True, stop=True)
            kb.incl(t, "te")
        kb.te(f)
        kb.sync_step("ve")
        def f(v):
            v.tensor_copy(a1[:, A1_TS:A1_TS + 384] if False else a2[:, A2_WHI:A2_WHI + 128],
                          P5[0:128, 0:128])
            v.tensor_copy(a2[0:64, A2_WLO:A2_WLO + 128], P5[0:64, 128:256])
            kb.incl(v, "ve")
        kb.ve(f)
        kb.sync_step("ve")
        def f(v):
            v.tensor_copy(a1[:, A1_TS:A1_TS + 384], P4[0:128, 0:384])
            kb.incl(v, "ve")
        kb.ve(f)
        kb.sync_step("te")
        def f(t):
            t.matmul(P4[0:128, 0:384], a2[:, A2_WHI:A2_WHI + 128],
                     a2[:, A2_SBH:A2_SBH + 384], start=True, stop=False)
            t.matmul(P4[0:128, 0:384], a2[0:64, A2_WLO:A2_WLO + 128],
                     a2[0:64, A2_SBL:A2_SBL + 384], start=False, stop=True)
            kb.incl(t, "te")
        kb.te(f)
        kb.sync_step("ve")
        def f(v):
            v.tensor_tensor(a2[:, A2_VT:A2_VT + 384], a1[:, A1_TS:A1_TS + 384],
                            P4[0:128, 0:384], ALU.subtract)
            kb.incl(v, "ve")
        kb.ve(f)
        kb.sync_step("te")
        def f(t):
            for ds, pp in enumerate((P6, P0, P1)):
                t.matmul(pp[0:128, 0:128], a2[:, A2_SBH + 128 * ds:A2_SBH + 128 * (ds + 1)],
                         a2[:, A2_QBH:A2_QBH + 128], start=True, stop=False)
                t.matmul(pp[0:128, 0:128], a2[0:64, A2_SBL + 128 * ds:A2_SBL + 128 * (ds + 1)],
                         a2[0:64, A2_QBL:A2_QBL + 128], start=False, stop=False)
                t.matmul(pp[0:128, 0:128], a2[:, A2_VT + 128 * ds:A2_VT + 128 * (ds + 1)],
                         a1[:, A1_LT:A1_LT + 128], start=False, stop=True)
            kb.incl(t, "te")
        kb.te(f)
        kb.sync_step("ve")
        def f(v):
            for ds, pp in enumerate((P6, P0, P1)):
                v.tensor_copy(a2[:, A2_OST + 128 * ds:A2_OST + 128 * (ds + 1)], pp[0:128, 0:128])
            kb.incl(v, "ve")
        kb.ve(f)
        kb.sync_step("ae")
        def f(a):
            for ds in range(3):
                kb.inc(a.dma_start(out=ag2_i[128 * ds:128 * (ds + 1),
                                             SPAN * j + tok0:SPAN * j + tok0 + 128],
                                   in_=a2[:, A2_OST + 128 * ds:A2_OST + 128 * (ds + 1)]),
                       "qo", 16)
        kb.ae(f)
        kb.sync_step("te")
        def f(t):
            t.matmul(P4[0:128, 0:384], a2[:, A2_KTIL:A2_KTIL + 128],
                     a2[:, A2_VT:A2_VT + 384], start=True, stop=True)
            t.matmul(P5[0:64, 0:384], a2[:, A2_KTIL + 128:A2_KTIL + 192],
                     a2[:, A2_VT:A2_VT + 384], start=True, stop=True)
            kb.incl(t, "te")
        kb.te(f)
        kb.sync_step("ve")
        def f(v):
            v.scalar_tensor_tensor(wst[:, 256:640], wst[:, 256:640], cS(SD_CEND),
                                   P4[0:128, 0:384], ALU.mult, ALU.add)
            v.scalar_tensor_tensor(wst[0:64, 640:1024], wst[0:64, 640:1024],
                                   sd[0:64, SD_CEND + c:SD_CEND + c + 1],
                                   P5[0:64, 0:384], ALU.mult, ALU.add)
            v.tensor_copy(a2[:, A2_SBH:A2_SBH + 384], wst[:, 256:640])
            v.tensor_copy(a2[0:64, A2_SBL:A2_SBL + 384], wst[0:64, 640:1024])
            kb.incl(v, "ve")
        kb.ve(f)

    def emit_delta_L0():
        kb.sync_step("ve")
        def f(v):
            v.memset(wst[:, 256:640], 0.0)
            v.memset(wst[0:64, 640:1024], 0.0)
            v.memset(a2[:, A2_SBH:A2_SBH + 384], 0.0)
            v.memset(a2[0:64, A2_SBL:A2_SBL + 384], 0.0)
            kb.incl(v, "ve")
        kb.ve(f)
        for j in range(4):
            delta_slabs_L0(j)
            delta_rowprep_L0(j)
            for c in range(16):
                delta_chunk_L0(j, c)

    kb.wait("ge", "cc")
    emit_delta_L0()

    d = dbg_out("oft", [DV0, L], BF16)
    if d is not None:
        kb.wait("ae", "qo", "ve", "te")
        def f(a, d=d):
            kb.inc(a.dma_start(out=d, in_=ag2_i), "qa", 16)
            a.wait_ge(kb.sem("qa"), kb.n["qa"])
        kb.ae(f)

    kb.wait("ge", "qo", "ve", "ae", "te")
    def f(g):
        kb.inc(g.collective_compute("AllGather", ALU.bypass,
                                    replica_groups=[list(range(8))],
                                    ins=[ag2_i.opt()], outs=[ag2_o.opt()]), "cc")
    kb.ge(f)

    kb.wait("ge", "cc")

    def ogch(i):
        return (a1, i * NL) if i < 8 else (a2, (i - 8) * NL)

    def emit_oread_L0():
        kb.sync_step("ge")
        def f(g):
            inmain = bass.AP(ag2_o.tensor, 0, [[2048, 4 * 3072], [1, 2048]])
            for cc in range(12):
                tns, off = ogch(cc)
                kb.inc(g.indirect_dma_start(
                    out=tns[:, off + 3:off + 2051], out_offset=None, in_=inmain,
                    in_offset=bass.IndirectOffsetOnAxis(
                        ap=gix[:, GX_OC + cc:GX_OC + cc + 1], axis=0)), "qgB", 16)
            inhalo = bass.AP(ag2_o.tensor, 0, [[64, 128 * 3072], [1, 64]])
            for cc in range(12):
                kb.inc(g.indirect_dma_start(
                    out=scb[:, 64 * cc:64 * cc + 64], out_offset=None, in_=inhalo,
                    in_offset=bass.IndirectOffsetOnAxis(
                        ap=gix[:, GX_OH + cc:GX_OH + cc + 1], axis=0)), "qgB", 16)
        kb.ge(f)
        kb.sync_step("ve")
        def f(v):
            for cc in range(12):
                tns, off = ogch(cc)
                v.tensor_copy(tns[:, off:off + 3], scb[:, 64 * cc + 61:64 * cc + 64])
            kb.incl(v, "ve")
        kb.ve(f)

    def emit_onorm_gate_L0():
        tiles = _ntiles(2051, 512)
        for hh in range(4):
            for ti, (n0, nw) in enumerate(tiles):
                kb.sync_step("ae")
                def f(a, hh=hh, n0=n0, nw=nw):
                    for ds in range(3):
                        tns, off = ogch(3 * hh + ds)
                        a.activation(scb[0:128, 512 * ds:512 * ds + nw],
                                     tns[:, off + n0:off + n0 + nw], AF.Square)
                    kb.incl(a, "ae")
                kb.ae(f)
                kb.sync_step("te")
                def f(t, n0=n0, nw=nw):
                    for ds in range(3):
                        t.matmul(P4[0:1, 0:nw], cbb[:, CB_ONES:CB_ONES + 1],
                                 scb[0:128, 512 * ds:512 * ds + nw],
                                 start=(ds == 0), stop=(ds == 2))
                    kb.incl(t, "te")
                kb.te(f)
                kb.sync_step("ve")
                def f(v, n0=n0, nw=nw):
                    v.tensor_scalar(mrep[0:1, n0:n0 + nw], P4[0:1, 0:nw],
                                    1.0 / 384.0, 1e-5, ALU.mult, ALU.add)
                    kb.incl(v, "ve")
                kb.ve(f)
            kb.sync_step("ae")
            def f(a, hh=hh):
                kb.inc(a.dma_start(out=rows_hbm[8 + hh:9 + hh, 0:2051],
                                   in_=mrep[0:1, 0:2051]), "qrw", 16)
            kb.ae(f)
            kb.sync_step("ve")
        kb.sync_step("se")
        def f(s):
            kb.inc(s.dma_start(out=mrep[0:4, 0:2051], in_=rows_hbm[8:12, 0:2051]), "qbc", 16)
        kb.se(f)
        kb.sync_step("ae")
        def f(a):
            a.activation(mrep[0:4, 0:2051], mrep[0:4, 0:2051], AF.Sqrt)
            kb.incl(a, "ae")
        kb.ae(f)
        kb.sync_step("ve")
        def f(v):
            v.reciprocal(mrep[0:4, 0:2051], mrep[0:4, 0:2051])
            kb.incl(v, "ve")
        kb.ve(f)
        kb.sync_step("ae")
        def f(a):
            kb.inc(a.dma_start(out=rows_hbm[8:12, 0:2051], in_=mrep[0:4, 0:2051]), "qrw", 16)
        kb.ae(f)
        for hh in range(4):
            kb.sync_step("ge")
            def f(g, hh=hh):
                kb.inc(g.dma_start(out=rrep[:, 0:2051],
                                   in_=rows_hbm[8 + hh:9 + hh, 0:2051].broadcast_to((128, 2051))),
                       "qbg", 16)
            kb.ge(f)
            for ds in range(3):
                cc = 3 * hh + ds
                tns, off = ogch(cc)
                kb.sync_step("se")
                def f(s, cc=cc):
                    kb.inc(s.dma_start(out=scb[:, 0:2048],
                                       in_=g_hbm[128 * cc:128 * cc + 128, 0:2048]), "qt0", 16)
                    kb.inc(s.dma_start(out=rrep[:, 2051:2054],
                                       in_=g_hbm[128 * cc:128 * cc + 128, 2048:2051]), "qt0", 16)
                kb.se(f)
                kb.sync_step("ve")
                def f(v, cc=cc, tns=tns, off=off, ds=ds):
                    v.scalar_tensor_tensor(tns[:, off:off + 2051], tns[:, off:off + 2051],
                                           sc[:, SC_GNW + ds:SC_GNW + ds + 1],
                                           rrep[:, 0:2051], ALU.mult, ALU.mult)
                    kb.incl(v, "ve")
                kb.ve(f)
                kb.sync_step("ae")
                def f(a, cc=cc):
                    a.activation(scb[:, 0:2048], scb[:, 0:2048], AF.Silu)
                    a.activation(rrep[:, 2051:2054], rrep[:, 2051:2054], AF.Silu)
                    kb.incl(a, "ae")
                kb.ae(f)
                kb.sync_step("ve")
                def f(v, tns=tns, off=off):
                    v.tensor_tensor(tns[:, off:off + 2048], tns[:, off:off + 2048],
                                    scb[:, 0:2048], ALU.mult)
                    v.tensor_tensor(tns[:, off + 2048:off + 2051], tns[:, off + 2048:off + 2051],
                                    rrep[:, 2051:2054], ALU.mult)
                    kb.incl(v, "ve")
                kb.ve(f)

    def emit_wso_wout():
        def og_moving(k, n0, nw):
            tns, off = ogch(k)
            return tns[:, off + n0:off + n0 + nw]

        def evac_y(col, pp, n0, nw, tem):
            def f(v, col=col, pp=pp, n0=n0, nw=nw, tem=tem):
                v.wait_ge(kb.sem(tem[0]), tem[1])
                v.tensor_copy(xln[:, col * NL + n0:col * NL + n0 + nw], pp[0:128, 0:nw])
                kb.incl(v, "ve")
            kb.ve(f)
            return ("ve", kb.n["ve"])

        emit_proj(w["L0_attn_W_so"], 12, 8, 2051, og_moving, evac_y)
        kb.wait("se", "ve", "te")
        def f(s):
            for c in range(8):
                kb.inc(s.dma_start(out=a1[:, c * NL:c * NL + 2051],
                                   in_=zs_hbm[128 * c:128 * (c + 1), :]), "qt0", 16)
        kb.se(f)
        kb.wait("ve", "qt0")
        def yz_moving(k, n0, nw):
            if k < 8:
                return xln[:, k * NL + n0:k * NL + n0 + nw]
            return a1[:, (k - 8) * NL + n0:(k - 8) * NL + n0 + nw]

        def evac_res(col, pp, n0, nw, tem):
            def f(v, col=col, pp=pp, n0=n0, nw=nw, tem=tem):
                v.wait_ge(kb.sem(tem[0]), tem[1])
                v.tensor_tensor(res[:, col * NL + 1 + n0:col * NL + 1 + n0 + nw],
                                res[:, col * NL + 1 + n0:col * NL + 1 + n0 + nw],
                                pp[0:128, 0:nw], ALU.add)
                kb.incl(v, "ve")
            kb.ve(f)
            return ("ve", kb.n["ve"])

        emit_proj(w["L0_attn_W_out"], 16, 8, 2051, yz_moving, evac_res)

    def emit_routing(ncols, col0):
        kb.wait("ve", "qa", "qs")
        kb.wait("se", "ve", "te", "ae")
        def f(s):
            kb.inc(s.dma_start(out=sc[0:8, SC_NRM:SC_NRM + 32], in_=w["L1_ffn_tile_sigs"]), "qs", 16)
        kb.se(f)
        kb.wait("ae", "qs")
        def f(a):
            a.activation(sc[0:8, SC_NRM:SC_NRM + 32], sc[0:8, SC_NRM:SC_NRM + 32], AF.Sign)
            kb.incl(a, "ae")
        kb.ae(f)
        kb.wait("te", "ae")
        def f(t):
            t.transpose(P4[0:32, 0:8], sc[0:8, SC_NRM:SC_NRM + 32], identF[0:8, 0:8])
            kb.incl(t, "te")
        kb.te(f)
        kb.wait("ve", "te")
        def f(v):
            v.tensor_copy(scb[0:32, 0:8], P4[0:32, 0:8])
            kb.incl(v, "ve")
        kb.ve(f)
        kb.wait("ge", "ve")
        def f(g):
            kb.inc(g.dma_start(out=rrep[0:32, 0:ncols], in_=pe_in[:, col0:col0 + ncols]), "qg", 16)
        kb.ge(f)
        kb.wait("te", "ge", "qg", "ve")
        for ti, (t0, tw) in enumerate(_ntiles(ncols, 128)):
            def f(t, t0=t0, tw=tw):
                t.matmul(P4[0:tw, 0:8], rrep[0:32, t0:t0 + tw], scb[0:32, 0:8],
                         start=True, stop=True)
                kb.incl(t, "te")
            kb.te(f)
            kb.sync_step("ve")
            def f(v, tw=tw):
                v.reduce_max(sc[0:tw, SC_STAT:SC_STAT + 1], P4[0:tw, 0:8], axis=AX.X)
                v.tensor_scalar(sc[0:tw, SC_STAT + 1:SC_STAT + 9], P4[0:tw, 0:8],
                                sc[0:tw, SC_STAT:SC_STAT + 1], None, ALU.is_ge)
                kb.incl(v, "ve")
            kb.ve(f)
            kb.sync_step("te")
            def f(t, tw=tw):
                t.transpose(P5[0:8, 0:tw], sc[0:tw, SC_STAT + 1:SC_STAT + 9],
                            identF[0:tw, 0:tw])
                kb.incl(t, "te")
            kb.te(f)
            kb.sync_step("ve")
            def f(v, tw=tw):
                v.tensor_copy(sc[0:8, SC_NRM + 100:SC_NRM + 100 + tw], P5[0:8, 0:tw])
                kb.incl(v, "ve")
            kb.ve(f)
            kb.sync_step("ae")
            def f(a, t0=t0, tw=tw):
                kb.inc(a.dma_start(out=onehot_hbm[:, t0:t0 + tw],
                                   in_=sc[0:8, SC_NRM + 100:SC_NRM + 100 + tw]), "qrw", 16)
            kb.ae(f)

    def emit_ffn1(ncols, col0):
        emit_routing(ncols, col0)
        kb.wait("ge", "ve", "qrw")
        def f(g):
            kb.inc(g.dma_start(out=scb[0:8, 0:1024], in_=w["L1_ffn_b"]), "qg", 16)
        kb.ge(f)
        kb.wait("ge", "qrw", "ve", "te")
        def f(g):
            kb.inc(g.dma_start(out=a1[0:8, 0:ncols], in_=onehot_hbm[:, 0:ncols]), "qg", 16)
        kb.ge(f)
        tiles = _ntiles(ncols, 512)
        kb.wait("te", "qg", "ge")
        bias_free = {}
        for col in range(8):
            for ti, (n0, nw) in enumerate(tiles):
                gi = col * len(tiles) + ti
                pp = P0 if gi % 2 == 0 else P1
                if gi >= 2:
                    kb.waitm("te", [bias_free[gi - 2]])
                def f(t, col=col, n0=n0, nw=nw, pp=pp):
                    t.matmul(pp[0:128, 0:nw], scb[0:8, 128 * col:128 * col + 128],
                             a1[0:8, n0:n0 + nw], start=True, stop=True)
                    kb.incl(t, "te")
                kb.te(f)
                tev = kb.n["te"]
                def f(v, col=col, n0=n0, nw=nw, pp=pp, tev=tev):
                    v.wait_ge(kb.sem("te"), tev)
                    v.scalar_tensor_tensor(res[:, col * NL + col0 + n0:col * NL + col0 + n0 + nw],
                                           pp[0:128, 0:nw], 0.5,
                                           res[:, col * NL + col0 + n0:col * NL + col0 + n0 + nw],
                                           ALU.mult, ALU.add)
                    kb.incl(v, "ve")
                kb.ve(f)
                bias_free[gi] = ("ve", kb.n["ve"])
        for e in range(8):
            kb.wait("ge", "ve", "te")
            def f(g, e=e):
                kb.inc(g.dma_start(out=rrep[:, 0:ncols],
                                   in_=onehot_hbm[e:e + 1, 0:ncols].broadcast_to((128, ncols))),
                       "qbg", 16)
            kb.ge(f)
            kb.wait("ve", "qbg")
            def f(v):
                for c in range(8):
                    v.tensor_tensor(a2[:, c * NL:c * NL + ncols],
                                    xln[:, c * NL:c * NL + ncols], rrep[:, 0:ncols],
                                    ALU.mult)
                kb.incl(v, "ve")
            kb.ve(f)

            def xt_moving(k, n0, nw):
                return a2[:, k * NL + n0:k * NL + n0 + nw]

            def evac_acc(col, pp, n0, nw, tem):
                def f(v, col=col, pp=pp, n0=n0, nw=nw, tem=tem):
                    v.wait_ge(kb.sem(tem[0]), tem[1])
                    v.scalar_tensor_tensor(res[:, col * NL + col0 + n0:col * NL + col0 + n0 + nw],
                                           pp[0:128, 0:nw], 0.5,
                                           res[:, col * NL + col0 + n0:col * NL + col0 + n0 + nw],
                                           ALU.mult, ALU.add)
                    kb.incl(v, "ve")
                kb.ve(f)
                return ("ve", kb.n["ve"])

            emit_proj(bass.AP(wT_hbm.tensor, 1024 * 1024 * e, [[1024, 1024], [1, 1024]]),
                      8, 8, ncols, xt_moving, evac_acc, bf16src=True)

    def emit_wT_prep():
        for e in range(8):
            emit_trans_prep(
                bass.AP(w["L1_ffn_W"].tensor, 1024 * 1024 * e, [[1024, 1024], [1, 1024]]),
                bass.AP(wT_hbm.tensor, 1024 * 1024 * e, [[1024, 1024], [1, 1024]]),
                8, 8)

    def emit_l1_proj(which):
        w_ap = {"q": w["L1_attn_q_w"], "k": w["L1_attn_k_w"], "v": w["L1_attn_v_w"]}[which]
        kb.wait("ae", "qa")
        kb.wait("ve", "qa")
        tap = {"q": SC_CQW, "k": SC_CKW, "v": SC_CVW}[which]
        row0 = {"q": 0, "k": 1024, "v": 2048}[which]

        def evac(col, pp, n0, nw, tem):
            def f(v, col=col, pp=pp, n0=n0, nw=nw, tem=tem):
                v.wait_ge(kb.sem(tem[0]), tem[1])
                v.tensor_copy(a1[:, col * NL + n0:col * NL + n0 + nw], pp[0:128, 0:nw])
                kb.incl(v, "ve")
            kb.ve(f)
            return ("ve", kb.n["ve"])

        emit_proj(w_ap, 8, 8, 2051, xln_moving, evac)
        kb.wait("ge", "te", "ve")
        def f(g):
            kb.inc(g.dma_start(out=rrep[:, 0:NL], in_=hmask.broadcast_to((128, NL))), "qbg", 16)
        kb.ge(f)
        kb.wait("ve", "qbg")
        for c in range(8):
            def f(v, c=c):
                u = a1[:, c * NL:c * NL + 2051]
                v.tensor_tensor(u[:, 0:3], u[:, 0:3], rrep[:, 1:4], ALU.mult)
                v.tensor_scalar(mrep[:, 0:SPAN], u[:, 0:SPAN],
                                sc[:, tap + 4 * c:tap + 4 * c + 1], None, ALU.mult)
                for jj in range(1, 4):
                    v.scalar_tensor_tensor(mrep[:, 0:SPAN], u[:, jj:jj + SPAN],
                                           sc[:, tap + 4 * c + jj:tap + 4 * c + jj + 1],
                                           mrep[:, 0:SPAN], ALU.mult, ALU.add)
                kb.incl(v, "ve")
            kb.ve(f)
            kb.sync_step("ae")
            def f(a, c=c):
                a.activation(a2[:, c * NL:c * NL + SPAN], mrep[:, 0:SPAN], AF.Silu)
                kb.incl(a, "ae")
            kb.ae(f)
            kb.sync_step("ve")
        if which == "v":
            kb.wait("ae", "ve", "ae")
            def f(a, row0=row0):
                for c in range(8):
                    kb.inc(a.dma_start(out=ag3b_i[row0 + 128 * c:row0 + 128 * (c + 1), :],
                                       in_=a2[:, c * NL:c * NL + SPAN]), "qa", 16)
            kb.ae(f)
            return
        for c in range(8):
            for ti, (n0, nw) in enumerate(_ntiles(SPAN, 512)):
                kb.sync_step("ae")
                def f(a, c=c, n0=n0, nw=nw):
                    a.activation(scb[:, 0:nw], a2[:, c * NL + n0:c * NL + n0 + nw], AF.Square)
                    kb.incl(a, "ae")
                kb.ae(f)
                kb.sync_step("te")
                def f(t, n0=n0, nw=nw):
                    t.matmul(P4[0:1, 0:nw], cbb[:, CB_ONES:CB_ONES + 1], scb[:, 0:nw],
                             start=True, stop=True)
                    kb.incl(t, "te")
                kb.te(f)
                kb.sync_step("ve")
                def f(v, n0=n0, nw=nw):
                    v.tensor_scalar(mrep[0:1, n0:n0 + nw], P4[0:1, 0:nw], 1e-12, None, ALU.add)
                    kb.incl(v, "ve")
                kb.ve(f)
            kb.sync_step("ae")
            def f(a, c=c):
                kb.inc(a.dma_start(out=rows_hbm[16 + c:17 + c, 0:SPAN], in_=mrep[0:1, 0:SPAN]), "qrw", 16)
            kb.ae(f)
            kb.sync_step("ve")
        kb.sync_step("se")
        def f(s):
            kb.inc(s.dma_start(out=mrep[0:8, 0:SPAN], in_=rows_hbm[16:24, 0:SPAN]), "qbc", 16)
        kb.se(f)
        kb.sync_step("ae")
        def f(a):
            a.activation(mrep[0:8, 0:SPAN], mrep[0:8, 0:SPAN], AF.Sqrt)
            kb.incl(a, "ae")
        kb.ae(f)
        kb.sync_step("ve")
        def f(v):
            v.reciprocal(mrep[0:8, 0:SPAN], mrep[0:8, 0:SPAN])
            kb.incl(v, "ve")
        kb.ve(f)
        kb.sync_step("ae")
        def f(a):
            kb.inc(a.dma_start(out=rows_hbm[16:24, 0:SPAN], in_=mrep[0:8, 0:SPAN]), "qrw", 16)
        kb.ae(f)
        for c in range(8):
            kb.sync_step("ge")
            def f(g, c=c):
                kb.inc(g.dma_start(out=rrep[:, 0:SPAN],
                                   in_=rows_hbm[16 + c:17 + c, 0:SPAN].broadcast_to((128, SPAN))),
                       "qbg", 16)
            kb.ge(f)
            kb.sync_step("ve")
            def f(v, c=c):
                v.tensor_tensor(a2[:, c * NL:c * NL + SPAN], a2[:, c * NL:c * NL + SPAN],
                                rrep[:, 0:SPAN], ALU.mult)
                kb.incl(v, "ve")
            kb.ve(f)
            kb.sync_step("ae")
            def f(a, c=c, row0=row0):
                kb.inc(a.dma_start(out=ag3b_i[row0 + 128 * c:row0 + 128 * (c + 1), :],
                                   in_=a2[:, c * NL:c * NL + SPAN]), "qa", 16)
            kb.ae(f)
            kb.sync_step("ge")

    def emit_l1_small():
        kb.wait("se", "ve", "te")
        def f(s):
            for k in range(8):
                kb.inc(s.dma_start(out=wst[:, 8 * k:8 * k + 8],
                                   in_=w["L1_attn_b_w"][128 * k:128 * k + 128, :]), "qs", 16)
                kb.inc(s.dma_start(out=wst[:, 64 + 8 * k:64 + 8 * k + 8],
                                   in_=w["L1_attn_mix_w"][128 * k:128 * k + 128, :]), "qs", 16)
        kb.se(f)
        kb.wait("ve", "qs")
        def f(v):
            v.tensor_copy(scb[:, 0:128], wst[:, 0:128])
            kb.incl(v, "ve")
        kb.ve(f)
        kb.wait("te", "ve")
        for ti, (n0, nw) in enumerate(_ntiles(SPAN, 512)):
            def f(t, n0=n0, nw=nw):
                for k in range(8):
                    t.matmul(P0[0:8, 0:nw], scb[:, 8 * k:8 * k + 8],
                             xln[:, k * NL + 3 + n0:k * NL + 3 + n0 + nw],
                             start=(k == 0), stop=(k == 7))
                for k in range(8):
                    t.matmul(P1[0:8, 0:nw], scb[:, 64 + 8 * k:64 + 8 * k + 8],
                             xln[:, k * NL + 3 + n0:k * NL + 3 + n0 + nw],
                             start=(k == 0), stop=(k == 7))
                kb.incl(t, "te")
            kb.te(f)
            tev = kb.n["te"]
            def f(a, n0=n0, nw=nw, tev=tev):
                a.wait_ge(kb.sem("te"), tev)
                a.activation(sc[0:8, SC_NRM + n0:SC_NRM + n0 + nw], P0[0:8, 0:nw], AF.Sigmoid)
                a.activation(sc[32:40, SC_NRM + n0:SC_NRM + n0 + nw], P1[0:8, 0:nw],
                             AF.Sigmoid, bias=sc[0:8, SC_MXB:SC_MXB + 1])
                kb.incl(a, "ae")
            kb.ae(f)
            kb.wait("te", "ae")
        kb.wait("ae", "ae")
        def f(a):
            kb.inc(a.dma_start(out=ag3f_i, in_=sc[0:8, SC_NRM:SC_NRM + SPAN]), "qa", 16)
            kb.inc(a.dma_start(out=g1_hbm, in_=sc[32:40, SC_NRM:SC_NRM + SPAN]), "qa", 16)
        kb.ae(f)

    emit_oread_L0()
    emit_onorm_gate_L0()
    emit_wso_wout()
    d = dbg_out("x2", [D, NL])
    if d is not None:
        kb.wait("ae", "ve")
        def f(a, d=d):
            for c in range(8):
                kb.inc(a.dma_start(out=d[128 * c:128 * (c + 1), :],
                                   in_=res[:, c * NL:(c + 1) * NL]), "qa", 16)
            a.wait_ge(kb.sem("qa"), kb.n["qa"])
        kb.ae(f)
    emit_ln(ncols=2051, col0=1, quant=True)
    emit_ffn0(2051, col0=1)
    emit_wT_prep()
    emit_ln(ncols=2051, col0=1)
    emit_ffn1(2051, col0=1)
    d = dbg_out("x3b", [D, NL])
    if d is not None:
        kb.wait("ae", "ve")
        def f(a, d=d):
            for c in range(8):
                kb.inc(a.dma_start(out=d[128 * c:128 * (c + 1), :],
                                   in_=res[:, c * NL:(c + 1) * NL]), "qa", 16)
            a.wait_ge(kb.sem("qa"), kb.n["qa"])
        kb.ae(f)
    emit_ln(ncols=2051, col0=1)
    emit_l1_proj("q")
    emit_l1_proj("k")
    emit_l1_proj("v")
    emit_l1_small()
    d = dbg_out("ag3b", [3072, SPAN], BF16)
    if d is not None:
        kb.wait("ae", "qa", "ve")
        def f(a, d=d):
            kb.inc(a.dma_start(out=d, in_=ag3b_i), "qa", 16)
            a.wait_ge(kb.sem("qa"), kb.n["qa"])
        kb.ae(f)
    d = dbg_out("ag3f", [8, SPAN], F32)
    if d is not None:
        kb.wait("ae", "qa", "ve")
        def f(a, d=d):
            kb.inc(a.dma_start(out=d, in_=ag3f_i), "qa", 16)
            a.wait_ge(kb.sem("qa"), kb.n["qa"])
        kb.ae(f)
    kb.wait("ge", "qa", "qs", "ve", "ae", "te", "qrw")
    def f(g):
        kb.inc(g.collective_compute("AllGather", ALU.bypass,
                                    replica_groups=[list(range(8))],
                                    ins=[ag3b_i.opt()], outs=[ag3b_o.opt()]), "cc")
        kb.inc(g.collective_compute("AllGather", ALU.bypass,
                                    replica_groups=[list(range(8))],
                                    ins=[ag3f_i.opt()], outs=[ag3f_o.opt()]), "cc")
    kb.ge(f)

    def delta_chunk_L1(u, j, c):
        tok0 = 128 * c
        kb.sync_step("se")
        def f(s):
            kb.inc(s.dma_start(out=a2[:, A2_KTM:A2_KTM + 128],
                               in_=a1[:, A1_KH + tok0:A1_KH + tok0 + 128], transpose=True), "qt0", 16)
            kb.inc(s.dma_start(out=a2[:, A2_VTM:A2_VTM + 128],
                               in_=a1[:, A1_V0 + tok0:A1_V0 + tok0 + 128], transpose=True), "qt0", 16)
        kb.se(f)
        kb.sync_step("te")
        def f(t):
            t.matmul(P2[0:128, 0:128], a1[:, A1_KH + tok0:A1_KH + tok0 + 128],
                     a1[:, A1_KL + tok0:A1_KL + tok0 + 128], start=True, stop=True)
            t.matmul(P3[0:128, 0:128], a1[:, A1_KH + tok0:A1_KH + tok0 + 128],
                     a1[:, A1_QH + tok0:A1_QH + tok0 + 128], start=True, stop=True)
            kb.incl(t, "te")
        kb.te(f)
        kb.sync_step("ve")
        def f(v):
            v.tensor_tensor(a1[:, A1_DT:A1_DT + 128], P2[0:128, 0:128],
                            cfm(CF_MSK_BD), ALU.mult)
            v.tensor_tensor(a1[:, A1_NT:A1_NT + 128], P2[0:128, 0:128],
                            cfm(CF_MSK_BELOW), ALU.mult)
            v.tensor_tensor(a1[:, A1_LT:A1_LT + 128], P3[0:128, 0:128],
                            cfm(CF_MSK_UP_INCL), ALU.mult)
            v.transpose(a1[:, A1_DD:A1_DD + 128], a1[:, A1_DT:A1_DT + 128])
            v.scalar_tensor_tensor(a1[:, A1_GT:A1_GT + 128], a1[:, A1_DT:A1_DT + 128],
                                   -1.0, identB, ALU.mult, ALU.add)
            kb.incl(v, "ve")
        kb.ve(f)
        kb.sync_step("te")
        def f(t):
            t.matmul(P4[0:128, 0:128], a1[:, A1_DD:A1_DD + 128],
                     a1[:, A1_DT:A1_DT + 128], start=True, stop=True)
            kb.incl(t, "te")
        kb.te(f)
        for lvl in range(4):
            kb.sync_step("ve")
            def f(v):
                v.tensor_copy(a1[:, A1_PW:A1_PW + 128], P4[0:128, 0:128])
                v.transpose(a1[:, A1_PWT:A1_PWT + 128], a1[:, A1_PW:A1_PW + 128])
                kb.incl(v, "ve")
            kb.ve(f)
            kb.sync_step("te")
            def f(t, lvl=lvl):
                t.matmul(P5[0:128, 0:128], a1[:, A1_PWT:A1_PWT + 128],
                         a1[:, A1_GT:A1_GT + 128], start=True, stop=True)
                if lvl < 3:
                    t.matmul(P4[0:128, 0:128], a1[:, A1_PWT:A1_PWT + 128],
                             a1[:, A1_PW:A1_PW + 128], start=True, stop=True)
                kb.incl(t, "te")
            kb.te(f)
            kb.sync_step("ve")
            def f(v):
                v.tensor_tensor(a1[:, A1_GT:A1_GT + 128], a1[:, A1_GT:A1_GT + 128],
                                P5[0:128, 0:128], ALU.add)
                kb.incl(v, "ve")
            kb.ve(f)
        kb.sync_step("ve")
        def f(v):
            v.transpose(a1[:, A1_GB:A1_GB + 128], a1[:, A1_GT:A1_GT + 128])
            v.tensor_copy(a1[:, A1_TF:A1_TF + 128], a1[:, A1_GB:A1_GB + 128])
            kb.incl(v, "ve")
        kb.ve(f)
        for it in range(3):
            kb.sync_step("te")
            def f(t):
                t.matmul(P4[0:128, 0:128], a1[:, A1_NT:A1_NT + 128],
                         a1[:, A1_TF:A1_TF + 128], start=True, stop=True)
                kb.incl(t, "te")
            kb.te(f)
            kb.sync_step("ve")
            def f(v):
                v.tensor_copy(a1[0:128, A1_TS:A1_TS + 128], P4[0:128, 0:128])
                kb.incl(v, "ve")
            kb.ve(f)
            kb.sync_step("te")
            def f(t):
                t.matmul(P5[0:128, 0:128], a1[:, A1_GT:A1_GT + 128],
                         a1[:, A1_TS:A1_TS + 128], start=True, stop=True)
                kb.incl(t, "te")
            kb.te(f)
            kb.sync_step("ve")
            def f(v):
                v.tensor_tensor(a1[:, A1_TF:A1_TF + 128], a1[:, A1_GB:A1_GB + 128],
                                P5[0:128, 0:128], ALU.subtract)
                kb.incl(v, "ve")
            kb.ve(f)
        kb.sync_step("te")
        def f(t):
            t.transpose(PT[:, 0:128], a1[:, A1_TF:A1_TF + 128], identB)
            kb.incl(t, "te")
        kb.te(f)
        kb.sync_step("ve")
        def f(v):
            v.tensor_copy(a1[:, A1_TT:A1_TT + 128], PT[:, 0:128])
            v.tensor_scalar(a2[:, A2_VT:A2_VT + 128], a2[:, A2_VTM:A2_VTM + 128],
                            sd[:, SD_BET + c:SD_BET + c + 1], None, ALU.mult)
            v.tensor_scalar(a2[:, A2_KBAR:A2_KBAR + 128], a2[:, A2_KTM:A2_KTM + 128],
                            sd[:, SD_BET + c:SD_BET + c + 1], None, ALU.mult)
            kb.incl(v, "ve")
        kb.ve(f)
        kb.sync_step("te")
        def f(t):
            t.matmul(P4[0:128, 0:128], a1[:, A1_TT:A1_TT + 128],
                     a2[:, A2_VT:A2_VT + 128], start=True, stop=True)
            t.matmul(P5[0:128, 0:128], a2[:, A2_KBAR:A2_KBAR + 128],
                     a1[:, A1_TT:A1_TT + 128], start=True, stop=True)
            kb.incl(t, "te")
        kb.te(f)
        kb.sync_step("ve")
        def f(v):
            v.tensor_copy(a2[:, A2_WHI:A2_WHI + 128], P5[0:128, 0:128])
            kb.incl(v, "ve")
        kb.ve(f)
        kb.sync_step("ve")
        def f(v):
            v.tensor_copy(a1[:, A1_TS:A1_TS + 128], P4[0:128, 0:128])
            kb.incl(v, "ve")
        kb.ve(f)
        kb.sync_step("te")
        def f(t):
            t.matmul(P4[0:128, 0:128], a2[:, A2_WHI:A2_WHI + 128],
                     a2[:, A2_SBH:A2_SBH + 128], start=True, stop=True)
            kb.incl(t, "te")
        kb.te(f)
        kb.sync_step("ve")
        def f(v):
            v.tensor_tensor(a2[:, A2_VT:A2_VT + 128], a1[:, A1_TS:A1_TS + 128],
                            P4[0:128, 0:128], ALU.subtract)
            kb.incl(v, "ve")
        kb.ve(f)
        kb.sync_step("te")
        def f(t):
            t.matmul(P6[0:128, 0:128], a2[:, A2_SBH:A2_SBH + 128],
                     a1[:, A1_QH + tok0:A1_QH + tok0 + 128], start=True, stop=False)
            t.matmul(P6[0:128, 0:128], a2[:, A2_VT:A2_VT + 128],
                     a1[:, A1_LT:A1_LT + 128], start=False, stop=True)
            kb.incl(t, "te")
        kb.te(f)
        kb.sync_step("ve")
        def f(v):
            v.tensor_copy(a2[:, A2_OST:A2_OST + 128], P6[0:128, 0:128])
            kb.incl(v, "ve")
        kb.ve(f)
        kb.sync_step("ae")
        def f(a, u=u, j=j, c=c):
            kb.inc(a.dma_start(out=ag4_i[128 * u:128 * (u + 1),
                                         SPAN * j + tok0:SPAN * j + tok0 + 128],
                               in_=a2[:, A2_OST:A2_OST + 128]), "qo", 16)
        kb.ae(f)
        kb.sync_step("te")
        def f(t):
            t.matmul(P4[0:128, 0:128], a2[:, A2_KTM:A2_KTM + 128],
                     a2[:, A2_VT:A2_VT + 128], start=True, stop=True)
            kb.incl(t, "te")
        kb.te(f)
        kb.sync_step("ve")
        def f(v):
            v.tensor_tensor(wst[:, 256:384], wst[:, 256:384], P4[0:128, 0:128], ALU.add)
            v.tensor_copy(a2[:, A2_SBH:A2_SBH + 128], wst[:, 256:384])
            kb.incl(v, "ve")
        kb.ve(f)

    def emit_delta_L1():
        for u in range(2):
            kb.sync_step("ve")
            def f(v):
                v.memset(wst[:, 256:384], 0.0)
                v.memset(a2[:, A2_SBH:A2_SBH + 128], 0.0)
                kb.incl(v, "ve")
            kb.ve(f)
            for j in range(4):
                kb.sync_step("ge")
                def f(g, u=u, j=j):
                    for t_, col in enumerate((A1_QH, A1_KH, A1_V0)):
                        gc = GX_L1 + u * 12 + j * 3 + t_
                        kb.inc(g.indirect_dma_start(
                            out=a1[:, col:col + SPAN], out_offset=None, in_=ag3b_o,
                            in_offset=bass.IndirectOffsetOnAxis(
                                ap=gix[:, gc:gc + 1], axis=0)), "qgB", 16)
                    kb.inc(g.indirect_dma_start(
                        out=mrep[0:64, 0:SPAN], out_offset=None, in_=ag3f_o,
                        in_offset=bass.IndirectOffsetOnAxis(
                            ap=gix[0:64, GX_B1 + j:GX_B1 + j + 1], axis=0)), "qgB", 16)
                kb.ge(f)
                kb.sync_step("ae")
                def f(a, u=u):
                    kb.inc(a.dma_start(out=rows_hbm[4:5, 0:SPAN],
                                       in_=mrep[32 * u:32 * u + 1, 0:SPAN]), "qrw", 16)
                kb.ae(f)
                kb.sync_step("se")
                def f(s):
                    rh = rows_hbm.tensor
                    kb.inc(s.dma_start(out=wst[0:16, 1152:1280],
                                       in_=bass.AP(rh, 4 * L, [[128, 16], [1, 128]])), "qbc", 16)
                kb.se(f)
                kb.sync_step("ge")
                def f(g):
                    kb.inc(g.dma_start(out=rrep[:, 0:SPAN],
                                       in_=rows_hbm[4:5, 0:SPAN].broadcast_to((128, SPAN))),
                           "qbg", 16)
                kb.ge(f)
                kb.sync_step("te")
                def f(t):
                    t.transpose(P3[0:128, 0:16], wst[0:16, 1152:1280], identF[0:16, 0:16])
                    kb.incl(t, "te")
                kb.te(f)
                kb.sync_step("ve")
                def f(v):
                    v.tensor_copy(sd[:, SD_BET:SD_BET + 16], P3[0:128, 0:16])
                    v.tensor_tensor(a1[:, A1_KL:A1_KL + SPAN], a1[:, A1_KH:A1_KH + SPAN],
                                    rrep[:, 0:SPAN], ALU.mult)
                    kb.incl(v, "ve")
                kb.ve(f)
                for c in range(16):
                    delta_chunk_L1(u, j, c)

    kb.wait("ge", "cc")
    emit_delta_L1()
    d = dbg_out("oft1", [2 * DV1, L], BF16)
    if d is not None:
        kb.wait("ae", "qo", "ve", "te")
        def f(a, d=d):
            kb.inc(a.dma_start(out=d, in_=ag4_i), "qa", 16)
            a.wait_ge(kb.sem("qa"), kb.n["qa"])
        kb.ae(f)
    kb.wait("ge", "qo", "ve", "ae", "te")
    def f(g):
        kb.inc(g.collective_compute("AllGather", ALU.bypass,
                                    replica_groups=[list(range(8))],
                                    ins=[ag4_i.opt()], outs=[ag4_o.opt()]), "cc")
    kb.ge(f)

    kb.wait("ge", "cc")
    kb.sync_step("ge")
    def f(g):
        inw = bass.AP(ag4_o.tensor, 0, [[2048, 4 * 2048], [1, 2048]])
        for hh in range(8):
            kb.inc(g.indirect_dma_start(
                out=a1[:, hh * NL:hh * NL + SPAN], out_offset=None, in_=inw,
                in_offset=bass.IndirectOffsetOnAxis(
                    ap=gix[:, GX_OE + hh:GX_OE + hh + 1], axis=0)), "qgB", 16)
    kb.ge(f)
    kb.sync_step("se")
    def f(s):
        for hh in range(8):
            kb.inc(s.dma_start(out=a2[:, hh * NL:hh * NL + SPAN],
                               in_=ag3b_i[2048 + 128 * hh:2048 + 128 * (hh + 1), :]), "qt0", 16)
    kb.se(f)
    for hh in range(8):
        kb.sync_step("ge")
        def f(g, hh=hh):
            kb.inc(g.dma_start(out=rrep[:, 0:SPAN],
                               in_=g1_hbm[hh:hh + 1, :].broadcast_to((128, SPAN))), "qbg", 16)
        kb.ge(f)
        kb.sync_step("ve")
        def f(v, hh=hh):
            o = a1[:, hh * NL:hh * NL + SPAN]
            vt = a2[:, hh * NL:hh * NL + SPAN]
            v.tensor_tensor(o, o, vt, ALU.subtract)
            v.tensor_tensor(o, o, rrep[:, 0:SPAN], ALU.mult)
            v.tensor_tensor(o, o, vt, ALU.add)
            kb.incl(v, "ve")
        kb.ve(f)
        for ti, (n0, nw) in enumerate(_ntiles(SPAN, 512)):
            kb.sync_step("ae")
            def f(a, hh=hh, n0=n0, nw=nw):
                a.activation(scb[:, 0:nw], a1[:, hh * NL + n0:hh * NL + n0 + nw], AF.Square)
                kb.incl(a, "ae")
            kb.ae(f)
            kb.sync_step("te")
            def f(t, n0=n0, nw=nw):
                t.matmul(P4[0:1, 0:nw], cbb[:, CB_ONES:CB_ONES + 1], scb[:, 0:nw],
                         start=True, stop=True)
                kb.incl(t, "te")
            kb.te(f)
            kb.sync_step("ve")
            def f(v, n0=n0, nw=nw):
                v.tensor_scalar(mrep[0:1, n0:n0 + nw], P4[0:1, 0:nw],
                                1.0 / 128.0, 1e-5, ALU.mult, ALU.add)
                kb.incl(v, "ve")
            kb.ve(f)
        kb.sync_step("ae")
        def f(a, hh=hh):
            kb.inc(a.dma_start(out=rows_hbm[24 + hh:25 + hh, 0:SPAN], in_=mrep[0:1, 0:SPAN]), "qrw", 16)
        kb.ae(f)
        kb.sync_step("ve")
    kb.sync_step("se")
    def f(s):
        kb.inc(s.dma_start(out=mrep[0:8, 0:SPAN], in_=rows_hbm[24:32, 0:SPAN]), "qbc", 16)
    kb.se(f)
    kb.sync_step("ae")
    def f(a):
        a.activation(mrep[0:8, 0:SPAN], mrep[0:8, 0:SPAN], AF.Sqrt)
        kb.incl(a, "ae")
    kb.ae(f)
    kb.sync_step("ve")
    def f(v):
        v.reciprocal(mrep[0:8, 0:SPAN], mrep[0:8, 0:SPAN])
        kb.incl(v, "ve")
    kb.ve(f)
    kb.sync_step("ae")
    def f(a):
        kb.inc(a.dma_start(out=rows_hbm[24:32, 0:SPAN], in_=mrep[0:8, 0:SPAN]), "qrw", 16)
    kb.ae(f)
    for hh in range(8):
        kb.sync_step("ge")
        def f(g, hh=hh):
            kb.inc(g.dma_start(out=rrep[:, 0:SPAN],
                               in_=rows_hbm[24 + hh:25 + hh, 0:SPAN].broadcast_to((128, SPAN))),
                   "qbg", 16)
        kb.ge(f)
        kb.sync_step("ve")
        def f(v, hh=hh):
            v.scalar_tensor_tensor(a1[:, hh * NL:hh * NL + SPAN], a1[:, hh * NL:hh * NL + SPAN],
                                   sc[:, SC_ONW:SC_ONW + 1], rrep[:, 0:SPAN],
                                   ALU.mult, ALU.mult)
            kb.incl(v, "ve")
        kb.ve(f)
        kb.sync_step("ge")
    def on_moving(k, n0, nw):
        return a1[:, k * NL + n0:k * NL + n0 + nw]

    def evac_res4(col, pp, n0, nw, tem):
        def f(v, col=col, pp=pp, n0=n0, nw=nw, tem=tem):
            v.wait_ge(kb.sem(tem[0]), tem[1])
            v.tensor_tensor(res[:, col * NL + 4 + n0:col * NL + 4 + n0 + nw],
                            res[:, col * NL + 4 + n0:col * NL + 4 + n0 + nw],
                            pp[0:128, 0:nw], ALU.add)
            kb.incl(v, "ve")
        kb.ve(f)
        return ("ve", kb.n["ve"])

    emit_proj(w["L1_attn_o_w"], 8, 8, SPAN, on_moving, evac_res4)
    d = dbg_out("x5", [D, NL])
    if d is not None:
        kb.wait("ae", "ve")
        def f(a, d=d):
            for c in range(8):
                kb.inc(a.dma_start(out=d[128 * c:128 * (c + 1), :],
                                   in_=res[:, c * NL:(c + 1) * NL]), "qa", 16)
            a.wait_ge(kb.sem("qa"), kb.n["qa"])
        kb.ae(f)
    emit_ln(ncols=SPAN, col0=4)
    emit_ffn1(SPAN, col0=4)

    kb.wait("se", "ve", "te", "ae", "ge", "cc", "qa", "qo")
    def f(s):
        for c in range(8):
            kb.inc(s.dma_start(out=y_out[128 * c:128 * (c + 1), :],
                               in_=res[:, c * NL + HL:c * NL + HL + SPAN]), "qs", 16)
        s.wait_ge(kb.sem("qs"), kb.n["qs"])
    kb.se(f)

    es.close()
    return nc, sorted(dbg.keys())



def _gidx_table(r):
    b, qd = divmod(r, 4)
    h = r % 4
    t = np.zeros((128, 96), np.uint32)
    p = np.arange(128)
    for j in range(4):
        rank = 4 * b + j
        col = np.zeros(128, np.uint32)
        col[:32] = 8 * rank + h
        col[32:64] = 8 * rank + 4 + h
        t[:, 0 + j] = col
        offs = [192 * h, 192 * h + 128, 768 + 192 * h, 768 + 192 * h + 128,
                1536 + 384 * h, 1536 + 384 * h + 128, 1536 + 384 * h + 256]
        for tt_, off in enumerate(offs):
            t[:, 4 + 7 * j + tt_] = 3072 * rank + off + np.minimum(p, 127)
    for cc in range(12):
        hh, ds = divmod(cc, 3)
        row = 384 * (4 * b + hh) + 128 * ds + p
        t[:, 32 + cc] = row * 4 + qd
        t[:, 44 + cc] = np.maximum(row * 128 + 32 * qd - 1, 0)
    for u in range(2):
        hh = 2 * qd + u if False else (2 * r + u) % 8
        for j in range(4):
            rank = 4 * b + j
            for tt_, off in enumerate((0, 1024, 2048)):
                t[:, 56 + u * 12 + j * 3 + tt_] = 3072 * rank + off + 128 * hh + p
    for j in range(4):
        rank = 4 * b + j
        col = np.zeros(128, np.uint32)
        col[:32] = 8 * rank + (2 * r) % 8
        col[32:64] = 8 * rank + (2 * r + 1) % 8
        t[:, 80 + j] = col
    for hh in range(8):
        u = 8 * b + hh
        row = 256 * (u // 2) + 128 * (u % 2) + p
        t[:, 84 + hh] = row * 4 + qd
    return t


def _host_layout(inputs):
    x = np.asarray(inputs["L0_attn_x"])
    cf, cbm = _host_consts()
    pos_all = np.arange(L, dtype=np.float32)
    div = np.exp(np.arange(0, 32, 2, dtype=np.float32) * (-math.log(10000.0) / 32))
    pe_full = np.zeros((L, 32), np.float32)
    ang = pos_all[:, None] * div[None, :]
    pe_full[:, 0::2] = np.sin(ang)
    pe_full[:, 1::2] = np.cos(ang)

    maps = []
    for r in range(NCORE):
        b, qd = divmod(r, 4)
        s0 = qd * SPAN
        lo, hi = s0 - HL, s0 + SPAN + HR
        clo, chi = max(lo, 0), min(hi, L)
        xs = np.zeros((NL, D), np.float32)
        xs[clo - lo:chi - lo] = x[b, clo:chi]
        pes = np.zeros((NL, 32), np.float32)
        pes[clo - lo:chi - lo] = pe_full[clo:chi]
        g = np.arange(lo, hi)
        hm = ((g >= 0) & (g < L)).astype(np.float32)[None, :]
        m = {
            "x_fm": np.ascontiguousarray(xs.T),
            "pe_fm": np.ascontiguousarray(pes.T),
            "hmask": np.ascontiguousarray(hm),
            "consts_f": cf,
            "consts_b": cbm,
            "gidx": _gidx_table(r),
        }
        for name in ["L0_attn_W_in", "L0_attn_W_q", "L0_attn_W_k", "L0_attn_W_v",
                     "L0_attn_W_g", "L0_attn_W_gk", "L0_attn_W_b", "L0_attn_W_so",
                     "L0_attn_W_out", "L0_ffn_weight_fp8", "L1_attn_q_w",
                     "L1_attn_k_w", "L1_attn_v_w", "L1_attn_b_w", "L1_attn_mix_w",
                     "L1_attn_o_w", "L1_ffn_tile_sigs"]:
            m[name] = np.ascontiguousarray(np.asarray(inputs[name], dtype=np.float32))
        m["L1_ffn_W"] = np.ascontiguousarray(
            np.asarray(inputs["L1_ffn_W"], np.float32).reshape(8 * 1024, 1024))
        m["L1_ffn_b"] = np.ascontiguousarray(np.asarray(inputs["L1_ffn_b"], np.float32))
        for name in ["L0_attn_conv_x_w", "L0_attn_conv_z_w", "L1_attn_conv_q_w",
                     "L1_attn_conv_k_w", "L1_attn_conv_v_w"]:
            m[name] = np.ascontiguousarray(
                np.asarray(inputs[name], np.float32).reshape(1024, 4))
        for name in ["L0_attn_conv_x_b", "L0_attn_conv_z_b", "L0_attn_b_b",
                     "L0_attn_A_log", "L0_attn_dt_bias", "L0_attn_g_norm_w",
                     "L0_ffn_bias", "L0_ffn_input_scale", "L0_ffn_weight_scale",
                     "L1_attn_mix_b", "L1_attn_mix_bias", "L1_attn_o_norm_w"]:
            m[name] = np.ascontiguousarray(
                np.asarray(inputs[name], np.float32).reshape(-1, 1))
        maps.append(m)
    return maps


def kernel(**inputs):
    maps = _host_layout(inputs)
    nc, dbg_names = build_program()
    res = run_bass_kernel_spmd(nc, maps, core_ids=list(range(NCORE)))
    out = np.zeros((B, L, D), np.float32)
    for r in range(NCORE):
        b, qd = divmod(r, 4)
        out[b, qd * SPAN:(qd + 1) * SPAN, :] = np.asarray(
            res.results[r]["y_fm"], np.float32).T
    if DEBUG:
        kernel.debug = [{n: np.asarray(res.results[r]["dbg_" + n]) for n in dbg_names}
                        for r in range(NCORE)]
    return out


# version 07
# speedup vs baseline: 1.2914x; 1.0052x over previous
import contextlib
import math
import numpy as np

import concourse.bass as bass
import concourse.mybir as mybir
from concourse.bass_utils import run_bass_kernel_spmd

F32 = mybir.dt.float32
BF16 = mybir.dt.bfloat16
FP8 = mybir.dt.float8e4
ALU = mybir.AluOpType
AF = mybir.ActivationFunctionType
AX = mybir.AxisListType

B, L, D = 2, 8192, 1024
NCORE = 8
SPAN = 2048
HL, HR = 4, 2
NL = SPAN + HL + HR
NEG = -1e30

H0, DK0, DV0 = 4, 192, 384
H1, DK1, DV1 = 8, 128, 128
CC = 128

DEBUG = ()


def _ntiles(n, t=512):
    out, c = [], 0
    while c < n:
        out.append((c, min(t, n - c)))
        c += t
    return out



def _host_consts():
    iv = np.arange(128)
    col, row = iv[None, :], iv[:, None]
    up_strict = col > row
    up_incl = col >= row
    same_blk = col // 32 == row // 32
    mb = lambda m: np.where(m, 0.0, NEG).astype(np.float32)
    mk = lambda m: m.astype(np.float32)
    ident = np.eye(128, dtype=np.float32)
    cols = [np.ones((128, 1), np.float32)]
    for v in (1e-5, 1e-6, 1e-12, 0.5, 1.0 / 1024.0, 1.0 / 384.0, 1.0 / 128.0):
        cols.append(np.full((128, 1), v, np.float32))
    cf = np.concatenate(cols + [
        ident,
        mb(up_strict), mb(up_incl),
        mb(up_strict & same_blk), mb(up_strict & ~same_blk),
        mk(up_strict), mk(up_incl),
        mk(up_strict & same_blk), mk(up_strict & ~same_blk),
    ], axis=1)
    top64 = (iv < 64).astype(np.float32)[:, None]
    cbm = np.concatenate([
        np.eye(128, dtype=np.float32),
        np.ones((128, 1), np.float32),
        np.ones((128, 1), np.float32), top64, 1.0 - top64,
    ], axis=1)
    import ml_dtypes
    return cf, cbm.astype(ml_dtypes.bfloat16)

CF_ONES = 0
CF_EPS5, CF_EPS6, CF_EPS12, CF_HALF, CF_RD, CF_RDV, CF_R128 = 1, 2, 3, 4, 5, 6, 7
CF_ID = 8
CF_MB_UP_STRICT, CF_MB_UP_INCL, CF_MB_BD, CF_MB_BELOW = 136, 264, 392, 520
CF_MSK_UP_STRICT, CF_MSK_UP_INCL, CF_MSK_BD, CF_MSK_BELOW = 648, 776, 904, 1032
CB_ID, CB_ONES, CB_FULL, CB_TOP, CB_BOT = 0, 128, 129, 130, 131

SC_CXW, SC_CZW, SC_CQW, SC_CKW, SC_CVW = 0, 32, 64, 96, 128
SC_CXB, SC_CZB = 160, 168
SC_AL, SC_EAL, SC_DTB, SC_BB = 176, 177, 178, 179
SC_GNW = 180
SC_ONW = 183
SC_MXB = 184
SC_SIN, SC_RSIN, SC_WS = 186, 187, 188
SC_F0B = 190
SC_STAT = 200
SC_NRM = 260
SC_END = 260 + 2048


class _Last:
    _SKIP = {"wait_ge", "nop", "maybe_nop_then_inc", "drain"}

    def __init__(self, eng, autodrain):
        object.__setattr__(self, "_e", eng)
        object.__setattr__(self, "last", None)
        object.__setattr__(self, "_ad", autodrain)

    def __getattr__(self, name):
        f = getattr(self._e, name)
        if callable(f):
            skip = name in _Last._SKIP
            def g(*a, **k):
                if (not skip) and self._ad:
                    self._e.drain()
                r = f(*a, **k)
                if (not skip) and r is not None and hasattr(r, "then_inc"):
                    object.__setattr__(self, "last", r)
                return r
            return g
        return f


class KB:
    def __init__(self):
        self.nc = bass.Bass("TRN2", target_bir_lowering=False)
        self.es = contextlib.ExitStack()
        self.block = None
        self._sems = {}
        self.n = {}
        self.seen = {}

    def sem(self, name):
        if name not in self._sems:
            self._sems[name] = self.es.enter_context(self.nc.semaphore(name))
            self.n[name] = 0
        return self._sems[name]

    def inc(self, instr, name, k=1):
        instr.then_inc(self.sem(name), k)
        self.n[name] += k
        return self.n[name]

    ENG = {"te": "tensor", "ve": "vector", "ae": "scalar", "ge": "gpsimd",
           "se": "sync"}

    def te(self, fn): self.block.tensor(lambda e: fn(_Last(e, False)))
    def ve(self, fn): self.block.vector(lambda e: fn(_Last(e, True)))
    def ae(self, fn): self.block.scalar(lambda e: fn(_Last(e, True)))
    def ge(self, fn): self.block.gpsimd(lambda e: fn(_Last(e, True)))
    def se(self, fn): self.block.sync(lambda e: fn(_Last(e, False)))

    def incl(self, proxy, name, k=1):
        assert proxy.last is not None, "no instruction to attach inc to"
        return self.inc(proxy.last, name, k)

    def wait(self, eng, *sems):
        vals = [(self.sem(s), self.n[s]) for s in sems if self.n.get(s, 0) > 0]
        if not vals:
            return
        def f(e):
            for h, v in vals:
                e.wait_ge(h, v)
        getattr(self.block, self.ENG[eng])(f)

    def sync_step(self, eng, sems=("te", "ve", "ae", "ge", "qgB", "qt0", "qt1",
                                   "qo", "qrw", "qbc", "qbg", "qg", "qa", "qs")):
        seen = self.seen.setdefault(eng, {})
        vals = []
        for s in sems:
            n = self.n.get(s, 0)
            if n > seen.get(s, 0):
                vals.append((self.sem(s), n))
                seen[s] = n
        if vals:
            def f(e):
                for h, v in vals:
                    e.wait_ge(h, v)
            getattr(self.block, self.ENG[eng])(f)

    def waitm(self, eng, marks):
        vals = [(self.sem(s), v) for (s, v) in marks if v > 0]
        if not vals:
            return
        def f(e):
            for h, v in vals:
                e.wait_ge(h, v)
        getattr(self.block, self.ENG[eng])(f)


def build_program():
    kb = KB()
    nc, es = kb.nc, kb.es

    def din(name, shape, dtype=F32):
        return nc.dram_tensor(name, shape, dtype, kind="ExternalInput").ap()

    x_in = din("x_fm", [D, NL])
    gidx_in = din("gidx", [128, 96], mybir.dt.uint32)
    pe_in = din("pe_fm", [32, NL])
    hmask = din("hmask", [1, NL])
    consts_f = din("consts_f", [128, 1160])
    consts_b = din("consts_b", [128, 132], BF16)

    w = {}
    for name, shape in [
        ("L0_attn_W_in", [1024, 2048]), ("L0_attn_conv_x_w", [1024, 4]),
        ("L0_attn_conv_x_b", [1024, 1]), ("L0_attn_conv_z_w", [1024, 4]),
        ("L0_attn_conv_z_b", [1024, 1]), ("L0_attn_W_q", [1024, 768]),
        ("L0_attn_W_k", [1024, 768]), ("L0_attn_W_v", [1024, 1536]),
        ("L0_attn_W_g", [1024, 1536]), ("L0_attn_W_gk", [1024, 4]),
        ("L0_attn_W_b", [1024, 4]), ("L0_attn_b_b", [4, 1]),
        ("L0_attn_A_log", [4, 1]), ("L0_attn_dt_bias", [4, 1]),
        ("L0_attn_g_norm_w", [384, 1]), ("L0_attn_W_so", [1536, 1024]),
        ("L0_attn_W_out", [2048, 1024]), ("L0_ffn_weight_fp8", [1024, 1024]),
        ("L0_ffn_bias", [1024, 1]), ("L0_ffn_input_scale", [1, 1]),
        ("L0_ffn_weight_scale", [1, 1]), ("L1_attn_q_w", [1024, 1024]),
        ("L1_attn_k_w", [1024, 1024]), ("L1_attn_v_w", [1024, 1024]),
        ("L1_attn_conv_q_w", [1024, 4]), ("L1_attn_conv_k_w", [1024, 4]),
        ("L1_attn_conv_v_w", [1024, 4]), ("L1_attn_b_w", [1024, 8]),
        ("L1_attn_mix_w", [1024, 8]), ("L1_attn_mix_b", [8, 1]),
        ("L1_attn_mix_bias", [8, 1]), ("L1_attn_o_norm_w", [128, 1]),
        ("L1_attn_o_w", [1024, 1024]), ("L1_ffn_tile_sigs", [8, 32]),
        ("L1_ffn_W", [8 * 1024, 1024]), ("L1_ffn_b", [8, 1024]),
    ]:
        w[name] = din(name, shape)

    y_out = nc.dram_tensor("y_fm", [D, SPAN], F32, kind="ExternalOutput").ap()

    dbg = {}

    def dbg_out(name, shape, dtype=F32):
        if name in DEBUG and name not in dbg:
            dbg[name] = nc.dram_tensor("dbg_" + name, shape, dtype,
                                       kind="ExternalOutput").ap()
        return dbg.get(name)

    def dint(name, shape, dtype):
        return nc.dram_tensor(name, shape, dtype).ap()

    ag1b_i = dint("ag1b_i", [3072, SPAN], BF16)
    ag1b_o = dint("ag1b_o", [8 * 3072, SPAN], BF16)
    ag1f_i = dint("ag1f_i", [8, SPAN], F32)
    ag1f_o = dint("ag1f_o", [64, SPAN], F32)
    ag2_i = dint("ag2_i", [DV0, L], BF16)
    ag2_o = dint("ag2_o", [8 * DV0, L], BF16)
    ag3b_i = dint("ag3b_i", [3072, SPAN], BF16)
    ag3b_o = dint("ag3b_o", [8 * 3072, SPAN], BF16)
    ag3f_i = dint("ag3f_i", [8, SPAN], F32)
    ag3f_o = dint("ag3f_o", [64, SPAN], F32)
    ag4_i = dint("ag4_i", [2 * DV1, L], BF16)
    ag4_o = dint("ag4_o", [16 * DV1, L], BF16)
    zs_hbm = dint("zs_hbm", [1024, 2051], BF16)
    g_hbm = dint("g_hbm", [1536, 2051], BF16)
    stats_hbm = dint("stats_hbm", [60, 128], F32)
    rows_hbm = dint("rows_hbm", [64, L], F32)
    wf0T_hbm = dint("wf0T_hbm", [1024, 1024], BF16)
    onehot_hbm = dint("onehot_hbm", [8, NL], F32)
    g1_hbm = dint("g1_hbm", [8, SPAN], F32)
    wT_hbm = dint("wT_hbm", [8 * 1024, 1024], BF16)

    sb = lambda name, shape, dt=F32: es.enter_context(nc.sbuf_tensor(name, shape, dt))
    res = sb("res", [128, 8 * NL])
    xln = sb("xln", [128, 8 * NL], BF16)
    a1 = sb("a1", [128, 8 * NL], BF16)
    a2 = sb("a2", [128, 8 * NL], BF16)
    wr = sb("wr", [128, 2 * 2048], BF16)
    wst = sb("wst", [128, 2048])
    mrep = sb("mrep", [128, NL])
    rrep = sb("rrep", [128, NL], BF16)
    sc = sb("sc", [128, SC_END])
    scb = sb("scb", [128, 2048], BF16)
    q8t = sb("q8t", [128, 512], FP8)
    sd = sb("sd", [128, 96])
    gix = sb("gix", [128, 96], mybir.dt.uint32)

    ps = lambda name, shape, dt=F32: es.enter_context(nc.psum_tensor(name, shape, dt))
    P0 = ps("P0", [128, 512])
    P1 = ps("P1", [128, 512])
    P2 = ps("P2", [128, 512])
    P3 = ps("P3", [128, 512])
    P4 = ps("P4", [128, 512])
    P5 = ps("P5", [128, 512])
    P6 = ps("P6", [128, 512])
    PT = ps("PT", [128, 1024], BF16)

    kb.block = es.enter_context(nc.Block())

    cfb = sb("cfb", [128, 1160])
    cbb = sb("cbb", [128, 132], BF16)
    identF = cfb[:, CF_ID:CF_ID + 128]
    identB = cbb[:, CB_ID:CB_ID + 128]

    def cfm(off):
        return cfb[:, off:off + 128]

    def f(s):
        i1 = s.dma_start(out=cfb[:], in_=consts_f)
        kb.inc(i1, "qs", 16)
        kb.inc(s.dma_start(out=gix[:], in_=gidx_in), "qs", 16)
        i2 = s.dma_start(out=cbb[:], in_=consts_b)
        kb.inc(i2, "qs", 16)
        for c in range(8):
            kb.inc(s.dma_start(out=res[:, c * NL:(c + 1) * NL],
                               in_=x_in[128 * c:128 * (c + 1), :]), "qs", 16)
    kb.se(f)

    def f(s):
        for cget, dst in ((w["L0_attn_conv_x_w"], SC_CXW), (w["L0_attn_conv_z_w"], SC_CZW),
                          (w["L1_attn_conv_q_w"], SC_CQW), (w["L1_attn_conv_k_w"], SC_CKW),
                          (w["L1_attn_conv_v_w"], SC_CVW)):
            for c in range(8):
                kb.inc(s.dma_start(out=sc[:, dst + 4 * c:dst + 4 * c + 4],
                                   in_=cget[128 * c:128 * (c + 1), :]), "qa", 16)
        for cget, dst in ((w["L0_attn_conv_x_b"], SC_CXB), (w["L0_attn_conv_z_b"], SC_CZB),
                          (w["L0_ffn_bias"], SC_F0B)):
            kb.inc(s.dma_start(out=sc[:, dst:dst + 8],
                               in_=cget.rearrange("(a b) c -> b (a c)", b=128),
                               allow_slow_non_contiguous=True), "qa", 16)
        kb.inc(s.dma_start(out=sc[0:4, SC_AL:SC_AL + 1], in_=w["L0_attn_A_log"]), "qa", 16)
        kb.inc(s.dma_start(out=sc[0:4, SC_DTB:SC_DTB + 1], in_=w["L0_attn_dt_bias"]), "qa", 16)
        kb.inc(s.dma_start(out=sc[0:4, SC_BB:SC_BB + 1], in_=w["L0_attn_b_b"]), "qa", 16)
        kb.inc(s.dma_start(out=sc[:, SC_GNW:SC_GNW + 3],
                           in_=w["L0_attn_g_norm_w"].rearrange("(a b) c -> b (a c)", b=128),
                           allow_slow_non_contiguous=True), "qa", 16)
        kb.inc(s.dma_start(out=sc[:, SC_ONW:SC_ONW + 1], in_=w["L1_attn_o_norm_w"]), "qa", 16)
        kb.inc(s.dma_start(out=sc[0:8, SC_MXB:SC_MXB + 1], in_=w["L1_attn_mix_b"]), "qa", 16)
        kb.inc(s.dma_start(out=sc[0:8, SC_MXB + 1:SC_MXB + 2], in_=w["L1_attn_mix_bias"]), "qa", 16)
        kb.inc(s.dma_start(out=sc[:, SC_SIN:SC_SIN + 1],
                           in_=w["L0_ffn_input_scale"].broadcast_to((128, 1))), "qa", 16)
        kb.inc(s.dma_start(out=sc[:, SC_WS:SC_WS + 1],
                           in_=w["L0_ffn_weight_scale"].broadcast_to((128, 1))), "qa", 16)
    kb.ae(f)
    kb.wait("ve", "qa")
    def f(v):
        v.memset(sc[:, SC_STAT:SC_STAT + 60], 0.0)
        v.reciprocal(sc[:, SC_RSIN:SC_RSIN + 1], sc[:, SC_SIN:SC_SIN + 1])
        v.tensor_add(sc[0:8, SC_MXB:SC_MXB + 1], sc[0:8, SC_MXB:SC_MXB + 1],
                     sc[0:8, SC_MXB + 1:SC_MXB + 2])
        kb.incl(v, "ve")
    kb.ve(f)
    kb.wait("ae", "qa")
    def f(a):
        a.activation(sc[0:4, SC_EAL:SC_EAL + 1], sc[0:4, SC_AL:SC_AL + 1], AF.Exp)
        kb.incl(a, "ae")
    kb.ae(f)

    def emit_ln(ncols=NL, eps_off=CF_EPS5, quant=False, col0=0):
        tiles = _ntiles(ncols, 128)
        nt = len(tiles)
        kb.wait("te", "ve", "ae", "qs")
        extract_marks = {}
        for ti, (t0, tw) in enumerate(tiles):
            pp = P2 if ti % 2 == 0 else P3
            pq = P4 if ti % 2 == 0 else P5
            if ti >= 2:
                kb.waitm("te", [extract_marks[ti - 2]])
            def f(t, t0=t0, tw=tw, pp=pp, pq=pq):
                for c in range(8):
                    xt = res[:, c * NL + col0 + t0:c * NL + col0 + t0 + tw]
                    t.matmul(pp[0:tw, 0:tw], xt, xt, start=(c == 0), stop=(c == 7))
                    t.matmul(pq[0:tw, 0:1], xt, cfb[:, CF_ONES:CF_ONES + 1],
                             start=(c == 0), stop=(c == 7))
                kb.incl(t, "te")
            kb.te(f)
            tev = kb.n["te"]
            def f(v, ti=ti, tw=tw, pp=pp, pq=pq, tev=tev):
                v.wait_ge(kb.sem("te"), tev)
                v.tensor_tensor(sc[0:tw, SC_NRM:SC_NRM + tw], pp[0:tw, 0:tw],
                                identF[0:tw, 0:tw], ALU.mult)
                v.reduce_sum(sc[0:tw, SC_STAT + ti:SC_STAT + ti + 1],
                             sc[0:tw, SC_NRM:SC_NRM + tw], axis=AX.X)
                v.tensor_copy(sc[0:tw, SC_STAT + 20 + ti:SC_STAT + 21 + ti],
                              pq[0:tw, 0:1])
                kb.incl(v, "ve")
            kb.ve(f)
            extract_marks[ti] = ("ve", kb.n["ve"])
        def f(v):
            st = sc[:, SC_STAT:SC_STAT + 60]
            v.tensor_scalar(st[:, 20:20 + nt], st[:, 20:20 + nt], 1.0 / 1024.0, None, ALU.mult)
            v.tensor_scalar(st[:, 0:nt], st[:, 0:nt], 1.0 / 1024.0, None, ALU.mult)
            kb.incl(v, "ve")
        kb.ve(f)
        kb.wait("ae", "ve")
        def f(a):
            st = sc[:, SC_STAT:SC_STAT + 60]
            a.activation(st[:, 40:40 + nt], st[:, 20:20 + nt], AF.Square)
            kb.incl(a, "ae")
        kb.ae(f)
        kb.wait("ve", "ae")
        def f(v):
            st = sc[:, SC_STAT:SC_STAT + 60]
            v.tensor_tensor(st[:, 0:nt], st[:, 0:nt], st[:, 40:40 + nt], ALU.subtract)
            kb.incl(v, "ve")
        kb.ve(f)
        kb.wait("ae", "ve")
        def f(a):
            st = sc[:, SC_STAT:SC_STAT + 60]
            a.activation(st[:, 0:nt], st[:, 0:nt], AF.Sqrt,
                         bias=cfb[:, eps_off:eps_off + 1])
            kb.incl(a, "ae")
        kb.ae(f)
        kb.wait("ve", "ae")
        def f(v):
            st = sc[:, SC_STAT:SC_STAT + 60]
            v.reciprocal(st[:, 0:nt], st[:, 0:nt])
            if quant:
                v.tensor_scalar(st[:, 40:40 + nt], st[:, 0:nt],
                                sc[:, SC_RSIN:SC_RSIN + 1], None, ALU.mult)
            kb.incl(v, "ve")
        kb.ve(f)
        kb.wait("te", "ve")
        def f(t):
            t.transpose(P2[0:60, 0:128], sc[:, SC_STAT:SC_STAT + 60], identF)
            kb.incl(t, "te")
        kb.te(f)
        kb.wait("ve", "te")
        def f(v):
            v.tensor_copy(sc[0:60, SC_NRM:SC_NRM + 128], P2[0:60, 0:128])
            kb.incl(v, "ve")
        kb.ve(f)
        kb.wait("ae", "ve")
        def f(a):
            kb.inc(a.dma_start(out=stats_hbm, in_=sc[0:60, SC_NRM:SC_NRM + 128]), "qln", 16)
        kb.ae(f)
        kb.wait("ge", "qln")
        def f(g):
            flat_r = bass.AP(stats_hbm.tensor, 0, [[0, 128], [1, ncols]])
            flat_m = bass.AP(stats_hbm.tensor, 20 * 128, [[0, 128], [1, ncols]])
            kb.inc(g.dma_start(out=rrep[:, 0:ncols], in_=flat_r), "qg", 16)
            kb.inc(g.dma_start(out=mrep[:, 0:ncols], in_=flat_m), "qg", 16)
        kb.ge(f)
        if not quant:
            kb.wait("ve", "qg")
            def f(v):
                for c in range(8):
                    v.tensor_tensor(xln[:, c * NL:c * NL + ncols],
                                    res[:, c * NL + col0:c * NL + col0 + ncols],
                                    mrep[:, 0:ncols], ALU.subtract)
                    v.tensor_tensor(xln[:, c * NL:c * NL + ncols],
                                    xln[:, c * NL:c * NL + ncols], rrep[:, 0:ncols],
                                    ALU.mult)
                kb.incl(v, "ve")
            kb.ve(f)
        else:
            kb.wait("se", "te", "ve", "qln")
            for w0, ww in _ntiles(ncols, 2048):
                if w0 > 0:
                    kb.wait("se", "ve")
                def f(s, w0=w0, ww=ww):
                    flat_rp = bass.AP(stats_hbm.tensor, 40 * 128 + w0, [[0, 128], [1, ww]])
                    kb.inc(s.dma_start(out=wst[:, 0:ww], in_=flat_rp), "qln2", 16)
                kb.se(f)
                kb.wait("ve", "qln2", "qg")
                def f(v, w0=w0, ww=ww):
                    for c in range(8):
                        t = sc[:, SC_NRM:SC_NRM + ww]
                        v.tensor_tensor(t, res[:, c * NL + col0 + w0:c * NL + col0 + w0 + ww],
                                        mrep[:, w0:w0 + ww], ALU.subtract)
                        v.tensor_tensor(t, t, wst[:, 0:ww], ALU.mult)
                        for h0, hw in _ntiles(ww, 512):
                            v.tensor_scalar(q8t[:, 0:hw], t[:, h0:h0 + hw],
                                            448.0, -448.0, ALU.min, ALU.max)
                            v.tensor_copy(a2[:, c * NL + w0 + h0:c * NL + w0 + h0 + hw],
                                          q8t[:, 0:hw])
                        v.tensor_scalar(xln[:, c * NL + w0:c * NL + w0 + ww], t,
                                        sc[:, SC_SIN:SC_SIN + 1], None, ALU.mult)
                    kb.incl(v, "ve")
                kb.ve(f)

    def emit_proj(w_ap, kchunks, colchunks, ncols, moving, evac, bf16src=False):
        tiles = _ntiles(ncols, 512)
        ready = {}
        mm_done = {}
        psum_free = {}
        kb.wait("te", "ve", "ae")
        kb.wait("se", "te")
        for col in range(colchunks):
            slot = col % 2
            qw = "qw%d" % slot
            if bf16src:
                marks = [mm_done[col - 2]] if col >= 2 else []
                kb.waitm("se", marks)
                def f(s, col=col, slot=slot, qw=qw):
                    for k in range(kchunks):
                        kb.inc(s.dma_start(
                            out=wr[:, 2048 * slot + 128 * k:2048 * slot + 128 * k + 128],
                            in_=w_ap[128 * k:128 * (k + 1), 128 * col:128 * col + 128]),
                            qw, 16)
                kb.se(f)
                ready[col] = (qw, kb.n[qw])
            else:
                marks = [ready[col - 1]] if col >= 1 else []
                if col >= 2:
                    marks.append(mm_done[col - 2])
                kb.waitm("se", marks)
                def f(s, col=col, qw=qw):
                    for k in range(kchunks):
                        kb.inc(s.dma_start(
                            out=wst[:, 128 * k:128 * k + 128],
                            in_=w_ap[128 * k:128 * (k + 1), 128 * col:128 * col + 128]),
                            qw, 16)
                kb.se(f)
                marks = [(qw, kb.n[qw])]
                if col >= 2:
                    marks.append(mm_done[col - 2])
                kb.waitm("ve", marks)
                def f(v, slot=slot, kchunks=kchunks):
                    v.tensor_copy(wr[:, 2048 * slot:2048 * slot + 128 * kchunks],
                                  wst[:, 0:128 * kchunks])
                    kb.incl(v, "ve")
                kb.ve(f)
                ready[col] = ("ve", kb.n["ve"])
            kb.waitm("te", [ready[col]])
            for ti, (n0, nw) in enumerate(tiles):
                gi = col * len(tiles) + ti
                pp = P0 if gi % 2 == 0 else P1
                if gi >= 2:
                    kb.waitm("te", [psum_free[gi - 2]])
                def f(t, col=col, slot=slot, n0=n0, nw=nw, pp=pp, kchunks=kchunks):
                    for k in range(kchunks):
                        mv = moving(k, n0, nw)
                        t.matmul(pp[0:128, 0:nw],
                                 wr[0:mv.shape[0],
                                    2048 * slot + 128 * k:2048 * slot + 128 * k + 128],
                                 mv, start=(k == 0), stop=(k == kchunks - 1))
                    kb.incl(t, "te")
                kb.te(f)
                psum_free[gi] = evac(col, pp, n0, nw, ("te", kb.n["te"]))
            mm_done[col] = ("te", kb.n["te"])

    def xln_moving(k, n0, nw):
        return xln[:, k * NL + n0:k * NL + n0 + nw]

    def emit_trans_prep(src_ap, dst_ap, kchunks, colchunks, scale=None):
        for col in range(colchunks):
            kb.wait("se", "ve", "qa")
            def f(s, col=col):
                for k in range(kchunks):
                    kb.inc(s.dma_start(
                        out=wst[:, 128 * k:128 * k + 128],
                        in_=src_ap[128 * col:128 * col + 128,
                                   128 * k:128 * k + 128]), "qs", 16)
            kb.se(f)
            kb.wait("ve", "qs", "qa")
            if scale is not None:
                def f(v, kchunks=kchunks):
                    v.tensor_scalar(scb[:, 0:128 * kchunks], wst[:, 0:128 * kchunks],
                                    scale, None, ALU.mult)
                    kb.incl(v, "ve")
            else:
                def f(v, kchunks=kchunks):
                    v.tensor_copy(scb[:, 0:128 * kchunks], wst[:, 0:128 * kchunks])
                    kb.incl(v, "ve")
            kb.ve(f)
            kb.wait("te", "ve")
            def f(t, kchunks=kchunks):
                for k in range(kchunks):
                    t.transpose(PT[:, 128 * k:128 * k + 128],
                                scb[:, 128 * k:128 * k + 128], identB)
                kb.incl(t, "te")
            kb.te(f)
            kb.wait("ve", "te")
            def f(v, kchunks=kchunks):
                v.tensor_copy(scb[:, 0:128 * kchunks], PT[:, 0:128 * kchunks])
                kb.incl(v, "ve")
            kb.ve(f)
            kb.wait("ae", "ve")
            def f(a, col=col, kchunks=kchunks):
                for k in range(kchunks):
                    kb.inc(a.dma_start(
                        out=dst_ap[128 * k:128 * k + 128, 128 * col:128 * col + 128],
                        in_=scb[:, 128 * k:128 * k + 128]), "qa", 16)
            kb.ae(f)

    def emit_wf0_prep():
        emit_trans_prep(w["L0_ffn_weight_fp8"], wf0T_hbm, 8, 8,
                        scale=sc[:, SC_WS:SC_WS + 1])

    def emit_ffn0(ncols, col0=0):
        def moving(k, n0, nw):
            return a2[:, k * NL + n0:k * NL + n0 + nw]

        sl_state = [0]
        slot_dve = {0: None, 1: None}

        def evac(col, pp, n0, nw, tem):
            sl = sl_state[0] % 2
            sl_state[0] += 1
            prev = slot_dve[sl]
            def f(a, col=col, nw=nw, pp=pp, tem=tem, sl=sl, prev=prev):
                a.wait_ge(kb.sem(tem[0]), tem[1])
                if prev is not None:
                    a.wait_ge(kb.sem(prev[0]), prev[1])
                a.activation(sc[:, SC_NRM + 600 * sl:SC_NRM + 600 * sl + nw],
                             pp[0:128, 0:nw], AF.Identity,
                             bias=sc[:, SC_F0B + col:SC_F0B + col + 1],
                             scale=sc[:, SC_SIN:SC_SIN + 1])
                kb.incl(a, "ae")
            kb.ae(f)
            aev = kb.n["ae"]
            def f(v, col=col, n0=n0, nw=nw, aev=aev, sl=sl):
                v.wait_ge(kb.sem("ae"), aev)
                v.scalar_tensor_tensor(res[:, col * NL + col0 + n0:col * NL + col0 + n0 + nw],
                                       sc[:, SC_NRM + 600 * sl:SC_NRM + 600 * sl + nw],
                                       0.5, res[:, col * NL + col0 + n0:col * NL + col0 + n0 + nw],
                                       ALU.mult, ALU.add)
                kb.incl(v, "ve")
            kb.ve(f)
            slot_dve[sl] = ("ve", kb.n["ve"])
            return ("ae", aev)

        kb.wait("se", "qa")
        emit_proj(wf0T_hbm, 8, 8, ncols, moving, evac, bf16src=True)
        kb.wait("ae", "ve")

    emit_wf0_prep()
    emit_ln(quant=True)
    emit_ffn0(NL)

    d = dbg_out("x1", [D, NL])
    if d is not None:
        kb.wait("ae", "ve")
        def f(a, d=d):
            for c in range(8):
                kb.inc(a.dma_start(out=d[128 * c:128 * (c + 1), :],
                                   in_=res[:, c * NL:(c + 1) * NL]), "qa", 16)
            a.wait_ge(kb.sem("qa"), kb.n["qa"])
        kb.ae(f)

    emit_ln()
    d = dbg_out("xln2", [D, NL], BF16)
    if d is not None:
        kb.wait("ae", "ve")
        def f(a, d=d):
            for c in range(8):
                kb.inc(a.dma_start(out=d[128 * c:128 * (c + 1), :],
                                   in_=xln[:, c * NL:(c + 1) * NL]), "qa", 16)
            a.wait_ge(kb.sem("qa"), kb.n["qa"])
        kb.ae(f)

    kb.wait("ge", "ve")
    def f(g):
        kb.inc(g.dma_start(out=rrep[:, 0:NL], in_=hmask.broadcast_to((128, NL))),
               "qg", 16)
    kb.ge(f)
    kb.wait("ve", "qg")

    def conv_half(zhalf):
        col_off = 8 if zhalf else 0
        store = {}

        def evac(col, pp, n0, nw, tem):
            def f(v, col=col, pp=pp, n0=n0, nw=nw, tem=tem):
                v.wait_ge(kb.sem(tem[0]), tem[1])
                v.tensor_copy(a1[:, col * NL + n0:col * NL + n0 + nw], pp[0:128, 0:nw])
                kb.incl(v, "ve")
            kb.ve(f)
            return ("ve", kb.n["ve"])

        emit_proj(w["L0_attn_W_in"][:, col_off * 128:(col_off + 8) * 128],
                  8, 8, NL, xln_moving, evac)
        tapoff = SC_CZW if zhalf else SC_CXW
        boff = SC_CZB if zhalf else SC_CXB
        kb.wait("ve", "te")
        for c in range(8):
            def f(v, c=c):
                xc = a1[:, c * NL:(c + 1) * NL]
                v.tensor_tensor(xc[:, 0:4], xc[:, 0:4], rrep[:, 0:4], ALU.mult)
                v.tensor_tensor(xc[:, NL - 2:NL], xc[:, NL - 2:NL],
                                rrep[:, NL - 2:NL], ALU.mult)
                v.tensor_scalar(mrep[:, 0:2051], xc[:, 0:2051],
                                sc[:, tapoff + 4 * c:tapoff + 4 * c + 1], None, ALU.mult)
                for j in range(1, 4):
                    v.scalar_tensor_tensor(mrep[:, 0:2051], xc[:, j:j + 2051],
                                           sc[:, tapoff + 4 * c + j:tapoff + 4 * c + j + 1],
                                           mrep[:, 0:2051], ALU.mult, ALU.add)
                kb.incl(v, "ve")
            kb.ve(f)
            vev = kb.n["ve"]
            def f(a, c=c, vev=vev):
                a.wait_ge(kb.sem("ve"), vev)
                a.activation(a2[:, c * NL:c * NL + 2051], mrep[:, 0:2051], AF.Silu,
                             bias=sc[:, boff + c:boff + c + 1])
                kb.incl(a, "ae")
            kb.ae(f)
            kb.wait("ve", "ae")
        if zhalf:
            kb.wait("ae", "ae")
            def f(a):
                for c in range(8):
                    kb.inc(a.dma_start(out=zs_hbm[128 * c:128 * (c + 1), :],
                                       in_=a2[:, c * NL:c * NL + 2051]), "qa", 16)
            kb.ae(f)

    conv_half(zhalf=True)
    d = dbg_out("zs", [D, 2051], BF16)
    if d is not None:
        kb.wait("ae", "ve", "ae")
        def f(a, d=d):
            for c in range(8):
                kb.inc(a.dma_start(out=d[128 * c:128 * (c + 1), :],
                                   in_=a2[:, c * NL:c * NL + 2051]), "qa", 16)
            a.wait_ge(kb.sem("qa"), kb.n["qa"])
        kb.ae(f)
    kb.wait("ae", "qa")
    conv_half(zhalf=False)
    kb.wait("ve", "ae", "qa")
    def f(v):
        v.tensor_copy(xln[:, 0:8 * NL], a2[:, 0:8 * NL])
        kb.incl(v, "ve")
    kb.ve(f)

    d = dbg_out("xs", [D, 2051], BF16)
    if d is not None:
        kb.wait("ae", "ve")
        def f(a, d=d):
            for c in range(8):
                kb.inc(a.dma_start(out=d[128 * c:128 * (c + 1), :],
                                   in_=xln[:, c * NL:c * NL + 2051]), "qa", 16)
            a.wait_ge(kb.sem("qa"), kb.n["qa"])
        kb.ae(f)

    XSOFF = 3

    def xs_moving(k, n0, nw):
        return xln[:, k * NL + XSOFF + n0:k * NL + XSOFF + n0 + nw]

    def emit_qk(which):
        w_ap = w["L0_attn_W_q"] if which == "q" else w["L0_attn_W_k"]
        row0 = 0 if which == "q" else 768
        kb.wait("ve", "qa")

        def evac(col, pp, n0, nw, tem):
            def f(v, col=col, pp=pp, n0=n0, nw=nw, tem=tem):
                v.wait_ge(kb.sem(tem[0]), tem[1])
                v.tensor_copy(a1[:, col * NL + n0:col * NL + n0 + nw], pp[0:128, 0:nw])
                kb.incl(v, "ve")
            kb.ve(f)
            return ("ve", kb.n["ve"])

        emit_proj(w_ap, 8, 6, SPAN, xs_moving, evac)
        kb.wait("ae", "ve")
        def f(a):
            for c in range(6):
                a.activation(a2[:, c * NL:c * NL + SPAN],
                             a1[:, c * NL:c * NL + SPAN], AF.Square)
            kb.incl(a, "ae")
        kb.ae(f)
        seg = {0: [(0, CB_FULL), (1, CB_TOP)], 1: [(1, CB_BOT), (2, CB_FULL)],
               2: [(3, CB_FULL), (4, CB_TOP)], 3: [(4, CB_BOT), (5, CB_FULL)]}
        kb.wait("te", "ae")
        for h in range(4):
            for ti, (n0, nw) in enumerate(_ntiles(SPAN, 512)):
                def f(t, h=h, n0=n0, nw=nw):
                    for si, (c, p) in enumerate(seg[h]):
                        t.matmul(P4[0:1, 0:nw], cbb[:, p:p + 1],
                                 a2[:, c * NL + n0:c * NL + n0 + nw],
                                 start=(si == 0), stop=(si == 1))
                    kb.incl(t, "te")
                kb.te(f)
                tev = kb.n["te"]
                def f(v, n0=n0, nw=nw, tev=tev):
                    v.wait_ge(kb.sem("te"), tev)
                    v.tensor_scalar(sc[0:1, SC_NRM + n0:SC_NRM + n0 + nw],
                                    P4[0:1, 0:nw], 1e-6, None, ALU.add)
                    kb.incl(v, "ve")
                kb.ve(f)
                kb.wait("te", "ve")
            kb.wait("ae", "ve")
            def f(a):
                a.activation(sc[0:1, SC_NRM:SC_NRM + SPAN],
                             sc[0:1, SC_NRM:SC_NRM + SPAN], AF.Sqrt)
                kb.incl(a, "ae")
            kb.ae(f)
            kb.wait("ve", "ae")
            def f(v):
                v.reciprocal(sc[0:1, SC_NRM:SC_NRM + SPAN], sc[0:1, SC_NRM:SC_NRM + SPAN])
                kb.incl(v, "ve")
            kb.ve(f)
            kb.wait("ae", "ve")
            def f(a, h=h):
                kb.inc(a.dma_start(out=rows_hbm[h:h + 1, 0:SPAN],
                                   in_=sc[0:1, SC_NRM:SC_NRM + SPAN]), "qnr", 16)
            kb.ae(f)
            kb.wait("se", "qnr")
            def f(s, h=h):
                kb.inc(s.dma_start(out=mrep[:, 0:SPAN],
                                   in_=rows_hbm[h:h + 1, 0:SPAN].broadcast_to((128, SPAN))),
                       "qnb", 16)
            kb.se(f)
            kb.wait("ve", "qnb")
            for (c, p) in seg[h]:
                pr = (slice(0, 128) if p == CB_FULL else
                      (slice(0, 64) if p == CB_TOP else slice(64, 128)))
                def f(v, c=c, pr=pr):
                    v.tensor_tensor(a1[pr, c * NL:c * NL + SPAN],
                                    a1[pr, c * NL:c * NL + SPAN],
                                    mrep[pr, 0:SPAN], ALU.mult)
                    kb.incl(v, "ve")
                kb.ve(f)
            kb.wait("se", "ve")
        kb.wait("ae", "ve")
        def f(a, row0=row0):
            for c in range(6):
                kb.inc(a.dma_start(out=ag1b_i[row0 + 128 * c:row0 + 128 * (c + 1), :],
                                   in_=a1[:, c * NL:c * NL + SPAN]), "qa", 16)
        kb.ae(f)

    emit_qk("q")
    emit_qk("k")

    def emit_v():
        def evac(col, pp, n0, nw, tem):
            sl = (col * 4 + n0 // 512) % 2
            kb.wait("ve", "qa")
            def f(v, pp=pp, nw=nw, tem=tem, sl=sl):
                v.wait_ge(kb.sem(tem[0]), tem[1])
                v.tensor_copy(scb[:, 1024 * sl:1024 * sl + nw], pp[0:128, 0:nw])
                kb.incl(v, "ve")
            kb.ve(f)
            vev = kb.n["ve"]
            def f(a, col=col, n0=n0, nw=nw, vev=vev, sl=sl):
                a.wait_ge(kb.sem("ve"), vev)
                kb.inc(a.dma_start(
                    out=ag1b_i[1536 + 128 * col:1536 + 128 * col + 128, n0:n0 + nw],
                    in_=scb[:, 1024 * sl:1024 * sl + nw]), "qa", 16)
            kb.ae(f)
            return ("ve", vev)
        emit_proj(w["L0_attn_W_v"], 8, 12, SPAN, xs_moving, evac)

    emit_v()

    def emit_gkbeta():
        kb.wait("se", "ve", "qa")
        def f(s):
            for k in range(8):
                kb.inc(s.dma_start(out=wst[:, 4 * k:4 * k + 4],
                                   in_=w["L0_attn_W_gk"][128 * k:128 * k + 128, :]),
                       "qs", 16)
                kb.inc(s.dma_start(out=wst[:, 32 + 4 * k:32 + 4 * k + 4],
                                   in_=w["L0_attn_W_b"][128 * k:128 * k + 128, :]),
                       "qs", 16)
        kb.se(f)
        kb.wait("ve", "qs")
        def f(v):
            v.tensor_copy(scb[:, 0:64], wst[:, 0:64])
            kb.incl(v, "ve")
        kb.ve(f)
        kb.wait("te", "ve")
        for ti, (n0, nw) in enumerate(_ntiles(SPAN, 512)):
            def f(t, n0=n0, nw=nw):
                for k in range(8):
                    t.matmul(P0[0:4, 0:nw], scb[:, 4 * k:4 * k + 4],
                             xs_moving(k, n0, nw), start=(k == 0), stop=(k == 7))
                for k in range(8):
                    t.matmul(P1[0:4, 0:nw], scb[:, 32 + 4 * k:32 + 4 * k + 4],
                             xs_moving(k, n0, nw), start=(k == 0), stop=(k == 7))
                kb.incl(t, "te")
            kb.te(f)
            tev = kb.n["te"]
            def f(a, n0=n0, nw=nw, tev=tev):
                a.wait_ge(kb.sem("te"), tev)
                a.activation(sc[0:4, SC_NRM + n0:SC_NRM + n0 + nw], P0[0:4, 0:nw],
                             AF.Identity, bias=sc[0:4, SC_DTB:SC_DTB + 1])
                a.activation(sc[32:36, SC_NRM + n0:SC_NRM + n0 + nw], P1[0:4, 0:nw],
                             AF.Sigmoid, bias=sc[0:4, SC_BB:SC_BB + 1])
                kb.incl(a, "ae")
            kb.ae(f)
            kb.wait("te", "ae")
        kb.wait("ve", "ae")
        def f(v):
            v.tensor_relu(mrep[0:4, 0:SPAN], sc[0:4, SC_NRM:SC_NRM + SPAN])
            kb.incl(v, "ve")
        kb.ve(f)
        kb.wait("ae", "ve")
        def f(a):
            u = sc[0:4, SC_NRM:SC_NRM + SPAN]
            a.activation(u, u, AF.Abs)
            a.activation(u, u, AF.Exp, scale=-1.0)
            a.activation(u, u, AF.Ln, bias=1.0)
            kb.incl(a, "ae")
        kb.ae(f)
        kb.wait("ve", "ae")
        def f(v):
            u = sc[0:4, SC_NRM:SC_NRM + SPAN]
            v.tensor_tensor(u, u, mrep[0:4, 0:SPAN], ALU.add)
            v.tensor_scalar(u, u, sc[0:4, SC_EAL:SC_EAL + 1], -1.0, ALU.mult, ALU.mult)
            kb.incl(v, "ve")
        kb.ve(f)
        kb.wait("ae", "ve")
        def f(a):
            kb.inc(a.dma_start(out=ag1f_i[0:4, :], in_=sc[0:4, SC_NRM:SC_NRM + SPAN]),
                   "qa", 16)
            kb.inc(a.dma_start(out=ag1f_i[4:8, :], in_=sc[32:36, SC_NRM:SC_NRM + SPAN]),
                   "qa", 16)
        kb.ae(f)

    emit_gkbeta()

    def emit_g():
        def evac(col, pp, n0, nw, tem):
            sl = (col * 5 + n0 // 512) % 2
            kb.wait("ve", "qa")
            def f(v, pp=pp, n0=n0, nw=nw, tem=tem, sl=sl):
                v.wait_ge(kb.sem(tem[0]), tem[1])
                v.tensor_copy(scb[:, 1024 * sl:1024 * sl + nw], pp[0:128, 0:nw])
                kb.incl(v, "ve")
            kb.ve(f)
            vev = kb.n["ve"]
            def f(a, col=col, n0=n0, nw=nw, vev=vev, sl=sl):
                a.wait_ge(kb.sem("ve"), vev)
                kb.inc(a.dma_start(out=g_hbm[128 * col:128 * col + 128, n0:n0 + nw],
                                   in_=scb[:, 1024 * sl:1024 * sl + nw]), "qa", 16)
            kb.ae(f)
            return ("ve", vev)

        def g_moving(k, n0, nw):
            return xln[:, k * NL + n0:k * NL + n0 + nw]

        emit_proj(w["L0_attn_W_g"], 8, 12, 2051, g_moving, evac)

    emit_g()

    d = dbg_out("ag1b", [3072, SPAN], BF16)
    if d is not None:
        kb.wait("ae", "qa", "ve")
        def f(a, d=d):
            kb.inc(a.dma_start(out=d, in_=ag1b_i), "qa", 16)
            a.wait_ge(kb.sem("qa"), kb.n["qa"])
        kb.ae(f)
    d = dbg_out("ag1f", [8, SPAN], F32)
    if d is not None:
        kb.wait("ae", "qa", "ve")
        def f(a, d=d):
            kb.inc(a.dma_start(out=d, in_=ag1f_i), "qa", 16)
            a.wait_ge(kb.sem("qa"), kb.n["qa"])
        kb.ae(f)

    kb.wait("ge", "qa", "qs", "ve", "ae", "te")
    def f(g):
        kb.inc(g.collective_compute("AllGather", ALU.bypass,
                                    replica_groups=[list(range(8))],
                                    ins=[ag1b_i.opt()], outs=[ag1b_o.opt()]), "cc")
        kb.inc(g.collective_compute("AllGather", ALU.bypass,
                                    replica_groups=[list(range(8))],
                                    ins=[ag1f_i.opt()], outs=[ag1f_o.opt()]), "cc")
    kb.ge(f)

    GX_GB = 0
    GX_SLAB = 4
    GX_OC = 32
    GX_OH = 44
    GX_L1 = 56
    GX_B1 = 80
    GX_OE = 84

    A1_QH, A1_QL, A1_KH, A1_KL = 0, 2048, 4096, 6144
    A1_V0, A1_V1, A1_V2 = 8192, 10240, 12288
    A1_MT, A1_DT, A1_NT, A1_DD = 14336, 14464, 14592, 14720
    A1_GT, A1_PW, A1_PWT, A1_TF = 14848, 14976, 15104, 15232
    A1_TT, A1_LT, A1_TS = 15360, 15488, 15616
    A1_GB = 16000
    A2_KTM, A2_VTM, A2_VT, A2_KBAR, A2_KTIL = 0, 192, 576, 960, 1152
    A2_QBH, A2_QBL, A2_WHI, A2_WLO = 1344, 1472, 1600, 1728
    A2_EMT, A2_EDT, A2_ENT, A2_ELT, A2_ES = 1856, 1984, 2112, 2240, 2368
    A2_SBH, A2_SBL, A2_OST = 2496, 2880, 3264
    SD_C, SD_CST, SD_CEND, SD_TIL, SD_BET, SD_BEC = 0, 16, 32, 48, 64, 80

    def delta_rowprep_L0(j):
        kb.sync_step("ge")
        def f(g):
            kb.inc(g.indirect_dma_start(
                out=mrep[0:64, 0:SPAN], out_offset=None, in_=ag1f_o,
                in_offset=bass.IndirectOffsetOnAxis(ap=gix[0:64, GX_GB + j:GX_GB + j + 1], axis=0)),
                "qgB", 16)
        kb.ge(f)
        kb.sync_step("ve")
        def f(v):
            v.tensor_tensor_scan(sc[0:1, SC_NRM:SC_NRM + SPAN], mrep[0:1, 0:SPAN],
                                 mrep[0:1, 0:SPAN], 0.0, ALU.add, ALU.bypass)
            v.tensor_copy(sc[0:1, SC_STAT:SC_STAT + 16],
                          bass.AP(sc.tensor if hasattr(sc, "tensor") else None, 0, []) if False else
                          sc[0:1, SC_NRM + 127:SC_NRM + SPAN].rearrange("a (b c) -> a (b c)") if False else
                          sc[0:1, SC_NRM:SC_NRM + SPAN][:, 127::128])
            v.memset(sc[0:1, SC_STAT + 16:SC_STAT + 17], 0.0)
            v.tensor_copy(sc[0:1, SC_STAT + 17:SC_STAT + 32],
                          sc[0:1, SC_STAT:SC_STAT + 15])
            v.tensor_copy(mrep[0:1, 0:SPAN], mrep[32:33, 0:SPAN])
            kb.incl(v, "ve")
        kb.ve(f)
        kb.sync_step("ae")
        def f(a):
            a.activation(mrep[0:1, 0:SPAN], mrep[0:1, 0:SPAN], AF.Ln)
            kb.incl(a, "ae")
        kb.ae(f)
        kb.sync_step("ve")
        def f(v):
            v.tensor_scalar(mrep[0:1, 0:SPAN], mrep[0:1, 0:SPAN], -1e4, None, ALU.max)
            v.tensor_tensor(mrep[0:1, 0:SPAN], mrep[0:1, 0:SPAN],
                            sc[0:1, SC_NRM:SC_NRM + SPAN], ALU.add)
            kb.incl(v, "ve")
        kb.ve(f)
        kb.sync_step("ae")
        def f(a):
            kb.inc(a.dma_start(out=rows_hbm[0:1, 0:SPAN], in_=sc[0:1, SC_NRM:SC_NRM + SPAN]), "qrw", 16)
            kb.inc(a.dma_start(out=rows_hbm[1:2, 0:SPAN], in_=mrep[0:1, 0:SPAN]), "qrw", 16)
            kb.inc(a.dma_start(out=rows_hbm[2:3, 0:16], in_=sc[0:1, SC_STAT:SC_STAT + 16]), "qrw", 16)
            kb.inc(a.dma_start(out=rows_hbm[3:4, 0:16], in_=sc[0:1, SC_STAT + 16:SC_STAT + 32]), "qrw", 16)
            kb.inc(a.dma_start(out=rows_hbm[4:5, 0:SPAN], in_=mrep[32:33, 0:SPAN]), "qrw", 16)
        kb.ae(f)
        kb.sync_step("se")
        def f(s):
            rh = rows_hbm.tensor
            LL = rows_hbm.shape[1] if hasattr(rows_hbm, "shape") else L
            kb.inc(s.dma_start(out=mrep[:, 0:SPAN],
                               in_=bass.AP(rh, 0, [[0, 128], [1, SPAN]])), "qbc", 16)
            kb.inc(s.dma_start(out=sc[:, SC_NRM:SC_NRM + SPAN],
                               in_=bass.AP(rh, L, [[0, 128], [1, SPAN]])), "qbc", 16)
            kb.inc(s.dma_start(out=wst[0:16, 1024:1152],
                               in_=bass.AP(rh, 0, [[128, 16], [1, 128]])), "qbc", 16)
            kb.inc(s.dma_start(out=sd[:, SD_CEND:SD_CEND + 16],
                               in_=bass.AP(rh, 2 * L, [[0, 128], [1, 16]])), "qbc", 16)
            kb.inc(s.dma_start(out=sd[:, SD_CST:SD_CST + 16],
                               in_=bass.AP(rh, 3 * L, [[0, 128], [1, 16]])), "qbc", 16)
            kb.inc(s.dma_start(out=wst[0:16, 1152:1280],
                               in_=bass.AP(rh, 4 * L, [[128, 16], [1, 128]])), "qbc", 16)
        kb.se(f)
        kb.sync_step("te")
        def f(t):
            t.transpose(P2[0:128, 0:16], wst[0:16, 1024:1152], identF[0:16, 0:16])
            t.transpose(P3[0:128, 0:16], wst[0:16, 1152:1280], identF[0:16, 0:16])
            kb.incl(t, "te")
        kb.te(f)
        kb.sync_step("ve")
        def f(v):
            v.tensor_copy(sd[:, SD_C:SD_C + 16], P2[0:128, 0:16])
            v.tensor_copy(sd[:, SD_BET:SD_BET + 16], P3[0:128, 0:16])
            kb.incl(v, "ve")
        kb.ve(f)
        kb.sync_step("ve")
        def f(v):
            v.tensor_tensor(sd[:, SD_TIL:SD_TIL + 16], sd[:, SD_CEND:SD_CEND + 16],
                            sd[:, SD_C:SD_C + 16], ALU.subtract)
            v.tensor_tensor(sd[:, SD_BEC:SD_BEC + 16], sd[:, SD_C:SD_C + 16],
                            sd[:, SD_CST:SD_CST + 16], ALU.subtract)
            kb.incl(v, "ve")
        kb.ve(f)
        kb.sync_step("ae")
        def f(a):
            a.activation(sd[:, SD_TIL:SD_TIL + 16], sd[:, SD_TIL:SD_TIL + 16], AF.Exp)
            a.activation(sd[:, SD_BEC:SD_BEC + 16], sd[:, SD_BEC:SD_BEC + 16], AF.Exp)
            kb.incl(a, "ae")
        kb.ae(f)
        kb.sync_step("ve")
        def f(v):
            v.tensor_tensor(sd[:, SD_CEND:SD_CEND + 16], sd[:, SD_CEND:SD_CEND + 16],
                            sd[:, SD_CST:SD_CST + 16], ALU.subtract)
            v.tensor_tensor(sd[:, SD_BEC:SD_BEC + 16], sd[:, SD_BEC:SD_BEC + 16],
                            sd[:, SD_BET:SD_BET + 16], ALU.mult)
            kb.incl(v, "ve")
        kb.ve(f)
        kb.sync_step("ae")
        def f(a):
            a.activation(sd[:, SD_CEND:SD_CEND + 16], sd[:, SD_CEND:SD_CEND + 16], AF.Exp)
            kb.incl(a, "ae")
        kb.ae(f)

    def delta_slabs_L0(j):
        kb.sync_step("ge")
        def f(g):
            for t, (col, n_p) in enumerate([(A1_QH, 128), (A1_QL, 64), (A1_KH, 128),
                                            (A1_KL, 64), (A1_V0, 128), (A1_V1, 128),
                                            (A1_V2, 128)]):
                kb.inc(g.indirect_dma_start(
                    out=a1[0:n_p, col:col + SPAN], out_offset=None, in_=ag1b_o,
                    in_offset=bass.IndirectOffsetOnAxis(
                        ap=gix[0:n_p, GX_SLAB + 7 * j + t:GX_SLAB + 7 * j + t + 1], axis=0)),
                    "qgB", 16)
        kb.ge(f)

    def delta_chunk_L0(j, c):
        tok0 = 128 * c
        cS = lambda off: sd[:, off + c:off + c + 1]
        kb.sync_step("se")
        def f(s):
            kb.inc(s.dma_start(out=a2[:, A2_KTM:A2_KTM + 128],
                               in_=a1[:, A1_KH + tok0:A1_KH + tok0 + 128], transpose=True), "qt0", 16)
            kb.inc(s.dma_start(out=a2[:, A2_KTM + 128:A2_KTM + 192],
                               in_=a1[0:64, A1_KL + tok0:A1_KL + tok0 + 128], transpose=True), "qt0", 16)
            for t, col in enumerate((A1_V0, A1_V1, A1_V2)):
                kb.inc(s.dma_start(out=a2[:, A2_VTM + 128 * t:A2_VTM + 128 * (t + 1)],
                                   in_=a1[:, col + tok0:col + tok0 + 128], transpose=True), "qt0", 16)
        kb.se(f)
        kb.sync_step("te")
        def f(t):
            t.matmul(P2[0:128, 0:128], a1[:, A1_KH + tok0:A1_KH + tok0 + 128],
                     a1[:, A1_KH + tok0:A1_KH + tok0 + 128], start=True, stop=False)
            t.matmul(P2[0:128, 0:128], a1[0:64, A1_KL + tok0:A1_KL + tok0 + 128],
                     a1[0:64, A1_KL + tok0:A1_KL + tok0 + 128], start=False, stop=True)
            t.matmul(P3[0:128, 0:128], a1[:, A1_KH + tok0:A1_KH + tok0 + 128],
                     a1[:, A1_QH + tok0:A1_QH + tok0 + 128], start=True, stop=False)
            t.matmul(P3[0:128, 0:128], a1[0:64, A1_KL + tok0:A1_KL + tok0 + 128],
                     a1[0:64, A1_QL + tok0:A1_QL + tok0 + 128], start=False, stop=True)
            kb.incl(t, "te")
        kb.te(f)
        kb.sync_step("ve")
        def f(v):
            cln = sc[:, SC_NRM + tok0:SC_NRM + tok0 + 128]
            crp = mrep[:, tok0:tok0 + 128]
            v.scalar_tensor_tensor(wst[:, 0:128], crp, cS(SD_C), cfm(CF_MB_UP_INCL),
                                   ALU.subtract, ALU.add)
            v.scalar_tensor_tensor(wst[:, 128:256], cln, cS(SD_C), cfm(CF_MB_BD),
                                   ALU.subtract, ALU.add)
            v.scalar_tensor_tensor(wst[:, 1280:1408], cln, cS(SD_C), cfm(CF_MB_BELOW),
                                   ALU.subtract, ALU.add)
            kb.incl(v, "ve")
        kb.ve(f)
        kb.sync_step("ae")
        def f(a):
            a.activation(a2[:, A2_ELT:A2_ELT + 128], wst[:, 0:128], AF.Exp)
            a.activation(a2[:, A2_EDT:A2_EDT + 128], wst[:, 128:256], AF.Exp)
            a.activation(a2[:, A2_ENT:A2_ENT + 128], wst[:, 1280:1408], AF.Exp)
            kb.incl(a, "ae")
        kb.ae(f)
        kb.sync_step("ve")
        def f(v):
            v.tensor_tensor(a1[:, A1_DT:A1_DT + 128], P2[0:128, 0:128],
                            a2[:, A2_EDT:A2_EDT + 128], ALU.mult)
            v.tensor_tensor(a1[:, A1_NT:A1_NT + 128], P2[0:128, 0:128],
                            a2[:, A2_ENT:A2_ENT + 128], ALU.mult)
            v.tensor_tensor(a1[:, A1_LT:A1_LT + 128], P3[0:128, 0:128],
                            a2[:, A2_ELT:A2_ELT + 128], ALU.mult)
            v.transpose(a1[:, A1_DD:A1_DD + 128], a1[:, A1_DT:A1_DT + 128])
            v.scalar_tensor_tensor(a1[:, A1_GT:A1_GT + 128], a1[:, A1_DT:A1_DT + 128],
                                   -1.0, identB, ALU.mult, ALU.add)
            kb.incl(v, "ve")
        kb.ve(f)
        kb.sync_step("te")
        def f(t):
            t.matmul(P4[0:128, 0:128], a1[:, A1_DD:A1_DD + 128],
                     a1[:, A1_DT:A1_DT + 128], start=True, stop=True)
            kb.incl(t, "te")
        kb.te(f)
        for lvl in range(4):
            kb.sync_step("ve")
            def f(v):
                v.tensor_copy(a1[:, A1_PW:A1_PW + 128], P4[0:128, 0:128])
                v.transpose(a1[:, A1_PWT:A1_PWT + 128], a1[:, A1_PW:A1_PW + 128])
                kb.incl(v, "ve")
            kb.ve(f)
            kb.sync_step("te")
            def f(t, lvl=lvl):
                t.matmul(P5[0:128, 0:128], a1[:, A1_PWT:A1_PWT + 128],
                         a1[:, A1_GT:A1_GT + 128], start=True, stop=True)
                if lvl < 3:
                    t.matmul(P4[0:128, 0:128], a1[:, A1_PWT:A1_PWT + 128],
                             a1[:, A1_PW:A1_PW + 128], start=True, stop=True)
                kb.incl(t, "te")
            kb.te(f)
            kb.sync_step("ve")
            def f(v, lvl=lvl):
                v.tensor_tensor(a1[:, A1_GT:A1_GT + 128], a1[:, A1_GT:A1_GT + 128],
                                P5[0:128, 0:128], ALU.add)
                if lvl == 3:
                    v.transpose(a1[:, A1_GB:A1_GB + 128], a1[:, A1_GT:A1_GT + 128])
                    v.tensor_copy(a1[:, A1_TF:A1_TF + 128], a1[:, A1_GB:A1_GB + 128])
                kb.incl(v, "ve")
            kb.ve(f)
        for it in range(3):
            kb.sync_step("te")
            def f(t):
                t.matmul(P4[0:128, 0:128], a1[:, A1_NT:A1_NT + 128],
                         a1[:, A1_TF:A1_TF + 128], start=True, stop=True)
                kb.incl(t, "te")
            kb.te(f)
            kb.sync_step("ve")
            def f(v):
                v.tensor_copy(a1[0:128, A1_TS:A1_TS + 128], P4[0:128, 0:128])
                kb.incl(v, "ve")
            kb.ve(f)
            kb.sync_step("te")
            def f(t):
                t.matmul(P5[0:128, 0:128], a1[:, A1_GT:A1_GT + 128],
                         a1[:, A1_TS:A1_TS + 128], start=True, stop=True)
                kb.incl(t, "te")
            kb.te(f)
            kb.sync_step("ve")
            def f(v):
                v.tensor_tensor(a1[:, A1_TF:A1_TF + 128], a1[:, A1_GB:A1_GB + 128],
                                P5[0:128, 0:128], ALU.subtract)
                kb.incl(v, "ve")
            kb.ve(f)
        kb.sync_step("te")
        def f(t):
            t.transpose(PT[:, 0:128], a1[:, A1_TF:A1_TF + 128], identB)
            kb.incl(t, "te")
        kb.te(f)
        kb.sync_step("ve")
        def f(v):
            v.tensor_copy(a1[:, A1_TT:A1_TT + 128], PT[:, 0:128])
            v.tensor_scalar(a2[:, A2_VT:A2_VT + 384], a2[:, A2_VTM:A2_VTM + 384],
                            cS(SD_BET), None, ALU.mult)
            v.tensor_scalar(a2[:, A2_KBAR:A2_KBAR + 192], a2[:, A2_KTM:A2_KTM + 192],
                            cS(SD_BEC), None, ALU.mult)
            v.tensor_scalar(a2[:, A2_KTIL:A2_KTIL + 192], a2[:, A2_KTM:A2_KTM + 192],
                            cS(SD_TIL), None, ALU.mult)
            v.tensor_scalar(wst[:, 0:128], mrep[:, tok0:tok0 + 128], cS(SD_CST),
                            None, ALU.subtract)
            kb.incl(v, "ve")
        kb.ve(f)
        kb.sync_step("ae")
        def f(a):
            a.activation(a2[:, A2_ES:A2_ES + 128], wst[:, 0:128], AF.Exp)
            kb.incl(a, "ae")
        kb.ae(f)
        kb.sync_step("ve")
        def f(v):
            v.tensor_tensor(a2[:, A2_QBH:A2_QBH + 128], a1[:, A1_QH + tok0:A1_QH + tok0 + 128],
                            a2[:, A2_ES:A2_ES + 128], ALU.mult)
            v.tensor_tensor(a2[0:64, A2_QBL:A2_QBL + 128], a1[0:64, A1_QL + tok0:A1_QL + tok0 + 128],
                            a2[0:64, A2_ES:A2_ES + 128], ALU.mult)
            kb.incl(v, "ve")
        kb.ve(f)
        kb.sync_step("te")
        def f(t):
            t.matmul(P4[0:128, 0:384], a1[:, A1_TT:A1_TT + 128],
                     a2[:, A2_VT:A2_VT + 384], start=True, stop=True)
            t.matmul(P5[0:128, 0:128], a2[:, A2_KBAR:A2_KBAR + 128],
                     a1[:, A1_TT:A1_TT + 128], start=True, stop=True)
            t.matmul(P5[0:64, 128:256], a2[:, A2_KBAR + 128:A2_KBAR + 192],
                     a1[:, A1_TT:A1_TT + 128], start=True, stop=True)
            kb.incl(t, "te")
        kb.te(f)
        kb.sync_step("ve")
        def f(v):
            v.tensor_copy(a1[:, A1_TS:A1_TS + 384] if False else a2[:, A2_WHI:A2_WHI + 128],
                          P5[0:128, 0:128])
            v.tensor_copy(a2[0:64, A2_WLO:A2_WLO + 128], P5[0:64, 128:256])
            kb.incl(v, "ve")
        kb.ve(f)
        kb.sync_step("ve")
        def f(v):
            v.tensor_copy(a1[:, A1_TS:A1_TS + 384], P4[0:128, 0:384])
            kb.incl(v, "ve")
        kb.ve(f)
        kb.sync_step("te")
        def f(t):
            t.matmul(P4[0:128, 0:384], a2[:, A2_WHI:A2_WHI + 128],
                     a2[:, A2_SBH:A2_SBH + 384], start=True, stop=False)
            t.matmul(P4[0:128, 0:384], a2[0:64, A2_WLO:A2_WLO + 128],
                     a2[0:64, A2_SBL:A2_SBL + 384], start=False, stop=True)
            kb.incl(t, "te")
        kb.te(f)
        kb.sync_step("ve")
        def f(v):
            v.tensor_tensor(a2[:, A2_VT:A2_VT + 384], a1[:, A1_TS:A1_TS + 384],
                            P4[0:128, 0:384], ALU.subtract)
            kb.incl(v, "ve")
        kb.ve(f)
        kb.sync_step("te")
        def f(t):
            for ds, pp in enumerate((P6, P0, P1)):
                t.matmul(pp[0:128, 0:128], a2[:, A2_SBH + 128 * ds:A2_SBH + 128 * (ds + 1)],
                         a2[:, A2_QBH:A2_QBH + 128], start=True, stop=False)
                t.matmul(pp[0:128, 0:128], a2[0:64, A2_SBL + 128 * ds:A2_SBL + 128 * (ds + 1)],
                         a2[0:64, A2_QBL:A2_QBL + 128], start=False, stop=False)
                t.matmul(pp[0:128, 0:128], a2[:, A2_VT + 128 * ds:A2_VT + 128 * (ds + 1)],
                         a1[:, A1_LT:A1_LT + 128], start=False, stop=True)
            kb.incl(t, "te")
        kb.te(f)
        kb.sync_step("ve")
        def f(v):
            for ds, pp in enumerate((P6, P0, P1)):
                v.tensor_copy(a2[:, A2_OST + 128 * ds:A2_OST + 128 * (ds + 1)], pp[0:128, 0:128])
            kb.incl(v, "ve")
        kb.ve(f)
        kb.sync_step("ae")
        def f(a):
            for ds in range(3):
                kb.inc(a.dma_start(out=ag2_i[128 * ds:128 * (ds + 1),
                                             SPAN * j + tok0:SPAN * j + tok0 + 128],
                                   in_=a2[:, A2_OST + 128 * ds:A2_OST + 128 * (ds + 1)]),
                       "qo", 16)
        kb.ae(f)
        kb.sync_step("te")
        def f(t):
            t.matmul(P4[0:128, 0:384], a2[:, A2_KTIL:A2_KTIL + 128],
                     a2[:, A2_VT:A2_VT + 384], start=True, stop=True)
            t.matmul(P5[0:64, 0:384], a2[:, A2_KTIL + 128:A2_KTIL + 192],
                     a2[:, A2_VT:A2_VT + 384], start=True, stop=True)
            kb.incl(t, "te")
        kb.te(f)
        kb.sync_step("ve")
        def f(v):
            v.scalar_tensor_tensor(wst[:, 256:640], wst[:, 256:640], cS(SD_CEND),
                                   P4[0:128, 0:384], ALU.mult, ALU.add)
            v.scalar_tensor_tensor(wst[0:64, 640:1024], wst[0:64, 640:1024],
                                   sd[0:64, SD_CEND + c:SD_CEND + c + 1],
                                   P5[0:64, 0:384], ALU.mult, ALU.add)
            v.tensor_copy(a2[:, A2_SBH:A2_SBH + 384], wst[:, 256:640])
            v.tensor_copy(a2[0:64, A2_SBL:A2_SBL + 384], wst[0:64, 640:1024])
            kb.incl(v, "ve")
        kb.ve(f)

    def emit_delta_L0():
        kb.sync_step("ve")
        def f(v):
            v.memset(wst[:, 256:640], 0.0)
            v.memset(wst[0:64, 640:1024], 0.0)
            v.memset(a2[:, A2_SBH:A2_SBH + 384], 0.0)
            v.memset(a2[0:64, A2_SBL:A2_SBL + 384], 0.0)
            kb.incl(v, "ve")
        kb.ve(f)
        for j in range(4):
            delta_slabs_L0(j)
            delta_rowprep_L0(j)
            for c in range(16):
                delta_chunk_L0(j, c)

    kb.wait("ge", "cc")
    emit_delta_L0()

    d = dbg_out("oft", [DV0, L], BF16)
    if d is not None:
        kb.wait("ae", "qo", "ve", "te")
        def f(a, d=d):
            kb.inc(a.dma_start(out=d, in_=ag2_i), "qa", 16)
            a.wait_ge(kb.sem("qa"), kb.n["qa"])
        kb.ae(f)

    kb.wait("ge", "qo", "ve", "ae", "te")
    def f(g):
        kb.inc(g.collective_compute("AllGather", ALU.bypass,
                                    replica_groups=[list(range(8))],
                                    ins=[ag2_i.opt()], outs=[ag2_o.opt()]), "cc")
    kb.ge(f)

    kb.wait("ge", "cc")

    def ogch(i):
        return (a1, i * NL) if i < 8 else (a2, (i - 8) * NL)

    def emit_oread_L0():
        kb.sync_step("ge")
        def f(g):
            inmain = bass.AP(ag2_o.tensor, 0, [[2048, 4 * 3072], [1, 2048]])
            for cc in range(12):
                tns, off = ogch(cc)
                kb.inc(g.indirect_dma_start(
                    out=tns[:, off + 3:off + 2051], out_offset=None, in_=inmain,
                    in_offset=bass.IndirectOffsetOnAxis(
                        ap=gix[:, GX_OC + cc:GX_OC + cc + 1], axis=0)), "qgB", 16)
            inhalo = bass.AP(ag2_o.tensor, 0, [[64, 128 * 3072], [1, 64]])
            for cc in range(12):
                kb.inc(g.indirect_dma_start(
                    out=scb[:, 64 * cc:64 * cc + 64], out_offset=None, in_=inhalo,
                    in_offset=bass.IndirectOffsetOnAxis(
                        ap=gix[:, GX_OH + cc:GX_OH + cc + 1], axis=0)), "qgB", 16)
        kb.ge(f)
        kb.sync_step("ve")
        def f(v):
            for cc in range(12):
                tns, off = ogch(cc)
                v.tensor_copy(tns[:, off:off + 3], scb[:, 64 * cc + 61:64 * cc + 64])
            kb.incl(v, "ve")
        kb.ve(f)

    def emit_onorm_gate_L0():
        tiles = _ntiles(2051, 512)
        for hh in range(4):
            for ti, (n0, nw) in enumerate(tiles):
                kb.sync_step("ae")
                def f(a, hh=hh, n0=n0, nw=nw):
                    for ds in range(3):
                        tns, off = ogch(3 * hh + ds)
                        a.activation(scb[0:128, 512 * ds:512 * ds + nw],
                                     tns[:, off + n0:off + n0 + nw], AF.Square)
                    kb.incl(a, "ae")
                kb.ae(f)
                kb.sync_step("te")
                def f(t, n0=n0, nw=nw):
                    for ds in range(3):
                        t.matmul(P4[0:1, 0:nw], cbb[:, CB_ONES:CB_ONES + 1],
                                 scb[0:128, 512 * ds:512 * ds + nw],
                                 start=(ds == 0), stop=(ds == 2))
                    kb.incl(t, "te")
                kb.te(f)
                kb.sync_step("ve")
                def f(v, n0=n0, nw=nw):
                    v.tensor_scalar(mrep[0:1, n0:n0 + nw], P4[0:1, 0:nw],
                                    1.0 / 384.0, 1e-5, ALU.mult, ALU.add)
                    kb.incl(v, "ve")
                kb.ve(f)
            kb.sync_step("ae")
            def f(a, hh=hh):
                kb.inc(a.dma_start(out=rows_hbm[8 + hh:9 + hh, 0:2051],
                                   in_=mrep[0:1, 0:2051]), "qrw", 16)
            kb.ae(f)
            kb.sync_step("ve")
        kb.sync_step("se")
        def f(s):
            kb.inc(s.dma_start(out=mrep[0:4, 0:2051], in_=rows_hbm[8:12, 0:2051]), "qbc", 16)
        kb.se(f)
        kb.sync_step("ae")
        def f(a):
            a.activation(mrep[0:4, 0:2051], mrep[0:4, 0:2051], AF.Sqrt)
            kb.incl(a, "ae")
        kb.ae(f)
        kb.sync_step("ve")
        def f(v):
            v.reciprocal(mrep[0:4, 0:2051], mrep[0:4, 0:2051])
            kb.incl(v, "ve")
        kb.ve(f)
        kb.sync_step("ae")
        def f(a):
            kb.inc(a.dma_start(out=rows_hbm[8:12, 0:2051], in_=mrep[0:4, 0:2051]), "qrw", 16)
        kb.ae(f)
        for hh in range(4):
            kb.sync_step("ge")
            def f(g, hh=hh):
                kb.inc(g.dma_start(out=rrep[:, 0:2051],
                                   in_=rows_hbm[8 + hh:9 + hh, 0:2051].broadcast_to((128, 2051))),
                       "qbg", 16)
            kb.ge(f)
            for ds in range(3):
                cc = 3 * hh + ds
                tns, off = ogch(cc)
                kb.sync_step("se")
                def f(s, cc=cc):
                    kb.inc(s.dma_start(out=scb[:, 0:2048],
                                       in_=g_hbm[128 * cc:128 * cc + 128, 0:2048]), "qt0", 16)
                    kb.inc(s.dma_start(out=rrep[:, 2051:2054],
                                       in_=g_hbm[128 * cc:128 * cc + 128, 2048:2051]), "qt0", 16)
                kb.se(f)
                kb.sync_step("ve")
                def f(v, cc=cc, tns=tns, off=off, ds=ds):
                    v.scalar_tensor_tensor(tns[:, off:off + 2051], tns[:, off:off + 2051],
                                           sc[:, SC_GNW + ds:SC_GNW + ds + 1],
                                           rrep[:, 0:2051], ALU.mult, ALU.mult)
                    kb.incl(v, "ve")
                kb.ve(f)
                kb.sync_step("ae")
                def f(a, cc=cc):
                    a.activation(scb[:, 0:2048], scb[:, 0:2048], AF.Silu)
                    a.activation(rrep[:, 2051:2054], rrep[:, 2051:2054], AF.Silu)
                    kb.incl(a, "ae")
                kb.ae(f)
                kb.sync_step("ve")
                def f(v, tns=tns, off=off):
                    v.tensor_tensor(tns[:, off:off + 2048], tns[:, off:off + 2048],
                                    scb[:, 0:2048], ALU.mult)
                    v.tensor_tensor(tns[:, off + 2048:off + 2051], tns[:, off + 2048:off + 2051],
                                    rrep[:, 2051:2054], ALU.mult)
                    kb.incl(v, "ve")
                kb.ve(f)

    def emit_wso_wout():
        def og_moving(k, n0, nw):
            tns, off = ogch(k)
            return tns[:, off + n0:off + n0 + nw]

        def evac_y(col, pp, n0, nw, tem):
            def f(v, col=col, pp=pp, n0=n0, nw=nw, tem=tem):
                v.wait_ge(kb.sem(tem[0]), tem[1])
                v.tensor_copy(xln[:, col * NL + n0:col * NL + n0 + nw], pp[0:128, 0:nw])
                kb.incl(v, "ve")
            kb.ve(f)
            return ("ve", kb.n["ve"])

        emit_proj(w["L0_attn_W_so"], 12, 8, 2051, og_moving, evac_y)
        kb.wait("se", "ve", "te")
        def f(s):
            for c in range(8):
                kb.inc(s.dma_start(out=a1[:, c * NL:c * NL + 2051],
                                   in_=zs_hbm[128 * c:128 * (c + 1), :]), "qt0", 16)
        kb.se(f)
        kb.wait("ve", "qt0")
        def yz_moving(k, n0, nw):
            if k < 8:
                return xln[:, k * NL + n0:k * NL + n0 + nw]
            return a1[:, (k - 8) * NL + n0:(k - 8) * NL + n0 + nw]

        def evac_res(col, pp, n0, nw, tem):
            def f(v, col=col, pp=pp, n0=n0, nw=nw, tem=tem):
                v.wait_ge(kb.sem(tem[0]), tem[1])
                v.tensor_tensor(res[:, col * NL + 1 + n0:col * NL + 1 + n0 + nw],
                                res[:, col * NL + 1 + n0:col * NL + 1 + n0 + nw],
                                pp[0:128, 0:nw], ALU.add)
                kb.incl(v, "ve")
            kb.ve(f)
            return ("ve", kb.n["ve"])

        emit_proj(w["L0_attn_W_out"], 16, 8, 2051, yz_moving, evac_res)

    def emit_routing(ncols, col0):
        kb.wait("ve", "qa", "qs")
        kb.wait("se", "ve", "te", "ae")
        def f(s):
            kb.inc(s.dma_start(out=sc[0:8, SC_NRM:SC_NRM + 32], in_=w["L1_ffn_tile_sigs"]), "qs", 16)
        kb.se(f)
        kb.wait("ae", "qs")
        def f(a):
            a.activation(sc[0:8, SC_NRM:SC_NRM + 32], sc[0:8, SC_NRM:SC_NRM + 32], AF.Sign)
            kb.incl(a, "ae")
        kb.ae(f)
        kb.wait("te", "ae")
        def f(t):
            t.transpose(P4[0:32, 0:8], sc[0:8, SC_NRM:SC_NRM + 32], identF[0:8, 0:8])
            kb.incl(t, "te")
        kb.te(f)
        kb.wait("ve", "te")
        def f(v):
            v.tensor_copy(scb[0:32, 0:8], P4[0:32, 0:8])
            kb.incl(v, "ve")
        kb.ve(f)
        kb.wait("ge", "ve")
        def f(g):
            kb.inc(g.dma_start(out=rrep[0:32, 0:ncols], in_=pe_in[:, col0:col0 + ncols]), "qg", 16)
        kb.ge(f)
        kb.wait("te", "ge", "qg", "ve")
        for ti, (t0, tw) in enumerate(_ntiles(ncols, 128)):
            def f(t, t0=t0, tw=tw):
                t.matmul(P4[0:tw, 0:8], rrep[0:32, t0:t0 + tw], scb[0:32, 0:8],
                         start=True, stop=True)
                kb.incl(t, "te")
            kb.te(f)
            kb.sync_step("ve")
            def f(v, tw=tw):
                v.reduce_max(sc[0:tw, SC_STAT:SC_STAT + 1], P4[0:tw, 0:8], axis=AX.X)
                v.tensor_scalar(sc[0:tw, SC_STAT + 1:SC_STAT + 9], P4[0:tw, 0:8],
                                sc[0:tw, SC_STAT:SC_STAT + 1], None, ALU.is_ge)
                kb.incl(v, "ve")
            kb.ve(f)
            kb.sync_step("te")
            def f(t, tw=tw):
                t.transpose(P5[0:8, 0:tw], sc[0:tw, SC_STAT + 1:SC_STAT + 9],
                            identF[0:tw, 0:tw])
                kb.incl(t, "te")
            kb.te(f)
            kb.sync_step("ve")
            def f(v, tw=tw):
                v.tensor_copy(sc[0:8, SC_NRM + 100:SC_NRM + 100 + tw], P5[0:8, 0:tw])
                kb.incl(v, "ve")
            kb.ve(f)
            kb.sync_step("ae")
            def f(a, t0=t0, tw=tw):
                kb.inc(a.dma_start(out=onehot_hbm[:, t0:t0 + tw],
                                   in_=sc[0:8, SC_NRM + 100:SC_NRM + 100 + tw]), "qrw", 16)
            kb.ae(f)

    def emit_ffn1(ncols, col0):
        emit_routing(ncols, col0)
        kb.wait("ge", "ve", "qrw")
        def f(g):
            kb.inc(g.dma_start(out=scb[0:8, 0:1024], in_=w["L1_ffn_b"]), "qg", 16)
        kb.ge(f)
        kb.wait("ge", "qrw", "ve", "te")
        def f(g):
            kb.inc(g.dma_start(out=a1[0:8, 0:ncols], in_=onehot_hbm[:, 0:ncols]), "qg", 16)
        kb.ge(f)
        tiles = _ntiles(ncols, 512)
        kb.wait("te", "qg", "ge")
        bias_free = {}
        for col in range(8):
            for ti, (n0, nw) in enumerate(tiles):
                gi = col * len(tiles) + ti
                pp = P0 if gi % 2 == 0 else P1
                if gi >= 2:
                    kb.waitm("te", [bias_free[gi - 2]])
                def f(t, col=col, n0=n0, nw=nw, pp=pp):
                    t.matmul(pp[0:128, 0:nw], scb[0:8, 128 * col:128 * col + 128],
                             a1[0:8, n0:n0 + nw], start=True, stop=True)
                    kb.incl(t, "te")
                kb.te(f)
                tev = kb.n["te"]
                def f(v, col=col, n0=n0, nw=nw, pp=pp, tev=tev):
                    v.wait_ge(kb.sem("te"), tev)
                    v.scalar_tensor_tensor(res[:, col * NL + col0 + n0:col * NL + col0 + n0 + nw],
                                           pp[0:128, 0:nw], 0.5,
                                           res[:, col * NL + col0 + n0:col * NL + col0 + n0 + nw],
                                           ALU.mult, ALU.add)
                    kb.incl(v, "ve")
                kb.ve(f)
                bias_free[gi] = ("ve", kb.n["ve"])
        for e in range(8):
            kb.wait("ge", "ve", "te")
            def f(g, e=e):
                kb.inc(g.dma_start(out=rrep[:, 0:ncols],
                                   in_=onehot_hbm[e:e + 1, 0:ncols].broadcast_to((128, ncols))),
                       "qbg", 16)
            kb.ge(f)
            kb.wait("ve", "qbg")
            def f(v):
                for c in range(8):
                    v.tensor_tensor(a2[:, c * NL:c * NL + ncols],
                                    xln[:, c * NL:c * NL + ncols], rrep[:, 0:ncols],
                                    ALU.mult)
                kb.incl(v, "ve")
            kb.ve(f)

            def xt_moving(k, n0, nw):
                return a2[:, k * NL + n0:k * NL + n0 + nw]

            def evac_acc(col, pp, n0, nw, tem):
                def f(v, col=col, pp=pp, n0=n0, nw=nw, tem=tem):
                    v.wait_ge(kb.sem(tem[0]), tem[1])
                    v.scalar_tensor_tensor(res[:, col * NL + col0 + n0:col * NL + col0 + n0 + nw],
                                           pp[0:128, 0:nw], 0.5,
                                           res[:, col * NL + col0 + n0:col * NL + col0 + n0 + nw],
                                           ALU.mult, ALU.add)
                    kb.incl(v, "ve")
                kb.ve(f)
                return ("ve", kb.n["ve"])

            emit_proj(bass.AP(wT_hbm.tensor, 1024 * 1024 * e, [[1024, 1024], [1, 1024]]),
                      8, 8, ncols, xt_moving, evac_acc, bf16src=True)

    def emit_wT_prep():
        for e in range(8):
            emit_trans_prep(
                bass.AP(w["L1_ffn_W"].tensor, 1024 * 1024 * e, [[1024, 1024], [1, 1024]]),
                bass.AP(wT_hbm.tensor, 1024 * 1024 * e, [[1024, 1024], [1, 1024]]),
                8, 8)

    def emit_l1_proj(which):
        w_ap = {"q": w["L1_attn_q_w"], "k": w["L1_attn_k_w"], "v": w["L1_attn_v_w"]}[which]
        kb.wait("ae", "qa")
        kb.wait("ve", "qa")
        tap = {"q": SC_CQW, "k": SC_CKW, "v": SC_CVW}[which]
        row0 = {"q": 0, "k": 1024, "v": 2048}[which]

        def evac(col, pp, n0, nw, tem):
            def f(v, col=col, pp=pp, n0=n0, nw=nw, tem=tem):
                v.wait_ge(kb.sem(tem[0]), tem[1])
                v.tensor_copy(a1[:, col * NL + n0:col * NL + n0 + nw], pp[0:128, 0:nw])
                kb.incl(v, "ve")
            kb.ve(f)
            return ("ve", kb.n["ve"])

        emit_proj(w_ap, 8, 8, 2051, xln_moving, evac)
        kb.wait("ge", "te", "ve")
        def f(g):
            kb.inc(g.dma_start(out=rrep[:, 0:NL], in_=hmask.broadcast_to((128, NL))), "qbg", 16)
        kb.ge(f)
        kb.wait("ve", "qbg")
        for c in range(8):
            def f(v, c=c):
                u = a1[:, c * NL:c * NL + 2051]
                v.tensor_tensor(u[:, 0:3], u[:, 0:3], rrep[:, 1:4], ALU.mult)
                v.tensor_scalar(mrep[:, 0:SPAN], u[:, 0:SPAN],
                                sc[:, tap + 4 * c:tap + 4 * c + 1], None, ALU.mult)
                for jj in range(1, 4):
                    v.scalar_tensor_tensor(mrep[:, 0:SPAN], u[:, jj:jj + SPAN],
                                           sc[:, tap + 4 * c + jj:tap + 4 * c + jj + 1],
                                           mrep[:, 0:SPAN], ALU.mult, ALU.add)
                kb.incl(v, "ve")
            kb.ve(f)
            kb.sync_step("ae")
            def f(a, c=c):
                a.activation(a2[:, c * NL:c * NL + SPAN], mrep[:, 0:SPAN], AF.Silu)
                kb.incl(a, "ae")
            kb.ae(f)
            kb.sync_step("ve")
        if which == "v":
            kb.wait("ae", "ve", "ae")
            def f(a, row0=row0):
                for c in range(8):
                    kb.inc(a.dma_start(out=ag3b_i[row0 + 128 * c:row0 + 128 * (c + 1), :],
                                       in_=a2[:, c * NL:c * NL + SPAN]), "qa", 16)
            kb.ae(f)
            return
        for c in range(8):
            for ti, (n0, nw) in enumerate(_ntiles(SPAN, 512)):
                kb.sync_step("ae")
                def f(a, c=c, n0=n0, nw=nw):
                    a.activation(scb[:, 0:nw], a2[:, c * NL + n0:c * NL + n0 + nw], AF.Square)
                    kb.incl(a, "ae")
                kb.ae(f)
                kb.sync_step("te")
                def f(t, n0=n0, nw=nw):
                    t.matmul(P4[0:1, 0:nw], cbb[:, CB_ONES:CB_ONES + 1], scb[:, 0:nw],
                             start=True, stop=True)
                    kb.incl(t, "te")
                kb.te(f)
                kb.sync_step("ve")
                def f(v, n0=n0, nw=nw):
                    v.tensor_scalar(mrep[0:1, n0:n0 + nw], P4[0:1, 0:nw], 1e-12, None, ALU.add)
                    kb.incl(v, "ve")
                kb.ve(f)
            kb.sync_step("ae")
            def f(a, c=c):
                kb.inc(a.dma_start(out=rows_hbm[16 + c:17 + c, 0:SPAN], in_=mrep[0:1, 0:SPAN]), "qrw", 16)
            kb.ae(f)
            kb.sync_step("ve")
        kb.sync_step("se")
        def f(s):
            kb.inc(s.dma_start(out=mrep[0:8, 0:SPAN], in_=rows_hbm[16:24, 0:SPAN]), "qbc", 16)
        kb.se(f)
        kb.sync_step("ae")
        def f(a):
            a.activation(mrep[0:8, 0:SPAN], mrep[0:8, 0:SPAN], AF.Sqrt)
            kb.incl(a, "ae")
        kb.ae(f)
        kb.sync_step("ve")
        def f(v):
            v.reciprocal(mrep[0:8, 0:SPAN], mrep[0:8, 0:SPAN])
            kb.incl(v, "ve")
        kb.ve(f)
        kb.sync_step("ae")
        def f(a):
            kb.inc(a.dma_start(out=rows_hbm[16:24, 0:SPAN], in_=mrep[0:8, 0:SPAN]), "qrw", 16)
        kb.ae(f)
        for c in range(8):
            kb.sync_step("ge")
            def f(g, c=c):
                kb.inc(g.dma_start(out=rrep[:, 0:SPAN],
                                   in_=rows_hbm[16 + c:17 + c, 0:SPAN].broadcast_to((128, SPAN))),
                       "qbg", 16)
            kb.ge(f)
            kb.sync_step("ve")
            def f(v, c=c):
                v.tensor_tensor(a2[:, c * NL:c * NL + SPAN], a2[:, c * NL:c * NL + SPAN],
                                rrep[:, 0:SPAN], ALU.mult)
                kb.incl(v, "ve")
            kb.ve(f)
            kb.sync_step("ae")
            def f(a, c=c, row0=row0):
                kb.inc(a.dma_start(out=ag3b_i[row0 + 128 * c:row0 + 128 * (c + 1), :],
                                   in_=a2[:, c * NL:c * NL + SPAN]), "qa", 16)
            kb.ae(f)
            kb.sync_step("ge")

    def emit_l1_small():
        kb.wait("se", "ve", "te")
        def f(s):
            for k in range(8):
                kb.inc(s.dma_start(out=wst[:, 8 * k:8 * k + 8],
                                   in_=w["L1_attn_b_w"][128 * k:128 * k + 128, :]), "qs", 16)
                kb.inc(s.dma_start(out=wst[:, 64 + 8 * k:64 + 8 * k + 8],
                                   in_=w["L1_attn_mix_w"][128 * k:128 * k + 128, :]), "qs", 16)
        kb.se(f)
        kb.wait("ve", "qs")
        def f(v):
            v.tensor_copy(scb[:, 0:128], wst[:, 0:128])
            kb.incl(v, "ve")
        kb.ve(f)
        kb.wait("te", "ve")
        for ti, (n0, nw) in enumerate(_ntiles(SPAN, 512)):
            def f(t, n0=n0, nw=nw):
                for k in range(8):
                    t.matmul(P0[0:8, 0:nw], scb[:, 8 * k:8 * k + 8],
                             xln[:, k * NL + 3 + n0:k * NL + 3 + n0 + nw],
                             start=(k == 0), stop=(k == 7))
                for k in range(8):
                    t.matmul(P1[0:8, 0:nw], scb[:, 64 + 8 * k:64 + 8 * k + 8],
                             xln[:, k * NL + 3 + n0:k * NL + 3 + n0 + nw],
                             start=(k == 0), stop=(k == 7))
                kb.incl(t, "te")
            kb.te(f)
            tev = kb.n["te"]
            def f(a, n0=n0, nw=nw, tev=tev):
                a.wait_ge(kb.sem("te"), tev)
                a.activation(sc[0:8, SC_NRM + n0:SC_NRM + n0 + nw], P0[0:8, 0:nw], AF.Sigmoid)
                a.activation(sc[32:40, SC_NRM + n0:SC_NRM + n0 + nw], P1[0:8, 0:nw],
                             AF.Sigmoid, bias=sc[0:8, SC_MXB:SC_MXB + 1])
                kb.incl(a, "ae")
            kb.ae(f)
            kb.wait("te", "ae")
        kb.wait("ae", "ae")
        def f(a):
            kb.inc(a.dma_start(out=ag3f_i, in_=sc[0:8, SC_NRM:SC_NRM + SPAN]), "qa", 16)
            kb.inc(a.dma_start(out=g1_hbm, in_=sc[32:40, SC_NRM:SC_NRM + SPAN]), "qa", 16)
        kb.ae(f)

    emit_oread_L0()
    emit_onorm_gate_L0()
    emit_wso_wout()
    d = dbg_out("x2", [D, NL])
    if d is not None:
        kb.wait("ae", "ve")
        def f(a, d=d):
            for c in range(8):
                kb.inc(a.dma_start(out=d[128 * c:128 * (c + 1), :],
                                   in_=res[:, c * NL:(c + 1) * NL]), "qa", 16)
            a.wait_ge(kb.sem("qa"), kb.n["qa"])
        kb.ae(f)
    emit_ln(ncols=2051, col0=1, quant=True)
    emit_ffn0(2051, col0=1)
    emit_wT_prep()
    emit_ln(ncols=2051, col0=1)
    emit_ffn1(2051, col0=1)
    d = dbg_out("x3b", [D, NL])
    if d is not None:
        kb.wait("ae", "ve")
        def f(a, d=d):
            for c in range(8):
                kb.inc(a.dma_start(out=d[128 * c:128 * (c + 1), :],
                                   in_=res[:, c * NL:(c + 1) * NL]), "qa", 16)
            a.wait_ge(kb.sem("qa"), kb.n["qa"])
        kb.ae(f)
    emit_ln(ncols=2051, col0=1)
    emit_l1_proj("q")
    emit_l1_proj("k")
    emit_l1_proj("v")
    emit_l1_small()
    d = dbg_out("ag3b", [3072, SPAN], BF16)
    if d is not None:
        kb.wait("ae", "qa", "ve")
        def f(a, d=d):
            kb.inc(a.dma_start(out=d, in_=ag3b_i), "qa", 16)
            a.wait_ge(kb.sem("qa"), kb.n["qa"])
        kb.ae(f)
    d = dbg_out("ag3f", [8, SPAN], F32)
    if d is not None:
        kb.wait("ae", "qa", "ve")
        def f(a, d=d):
            kb.inc(a.dma_start(out=d, in_=ag3f_i), "qa", 16)
            a.wait_ge(kb.sem("qa"), kb.n["qa"])
        kb.ae(f)
    kb.wait("ge", "qa", "qs", "ve", "ae", "te", "qrw")
    def f(g):
        kb.inc(g.collective_compute("AllGather", ALU.bypass,
                                    replica_groups=[list(range(8))],
                                    ins=[ag3b_i.opt()], outs=[ag3b_o.opt()]), "cc")
        kb.inc(g.collective_compute("AllGather", ALU.bypass,
                                    replica_groups=[list(range(8))],
                                    ins=[ag3f_i.opt()], outs=[ag3f_o.opt()]), "cc")
    kb.ge(f)

    def delta_chunk_L1(u, j, c):
        tok0 = 128 * c
        kb.sync_step("se")
        def f(s):
            kb.inc(s.dma_start(out=a2[:, A2_KTM:A2_KTM + 128],
                               in_=a1[:, A1_KH + tok0:A1_KH + tok0 + 128], transpose=True), "qt0", 16)
            kb.inc(s.dma_start(out=a2[:, A2_VTM:A2_VTM + 128],
                               in_=a1[:, A1_V0 + tok0:A1_V0 + tok0 + 128], transpose=True), "qt0", 16)
        kb.se(f)
        kb.sync_step("te")
        def f(t):
            t.matmul(P2[0:128, 0:128], a1[:, A1_KH + tok0:A1_KH + tok0 + 128],
                     a1[:, A1_KL + tok0:A1_KL + tok0 + 128], start=True, stop=True)
            t.matmul(P3[0:128, 0:128], a1[:, A1_KH + tok0:A1_KH + tok0 + 128],
                     a1[:, A1_QH + tok0:A1_QH + tok0 + 128], start=True, stop=True)
            kb.incl(t, "te")
        kb.te(f)
        kb.sync_step("ve")
        def f(v):
            v.tensor_tensor(a1[:, A1_DT:A1_DT + 128], P2[0:128, 0:128],
                            cfm(CF_MSK_BD), ALU.mult)
            v.tensor_tensor(a1[:, A1_NT:A1_NT + 128], P2[0:128, 0:128],
                            cfm(CF_MSK_BELOW), ALU.mult)
            v.tensor_tensor(a1[:, A1_LT:A1_LT + 128], P3[0:128, 0:128],
                            cfm(CF_MSK_UP_INCL), ALU.mult)
            v.transpose(a1[:, A1_DD:A1_DD + 128], a1[:, A1_DT:A1_DT + 128])
            v.scalar_tensor_tensor(a1[:, A1_GT:A1_GT + 128], a1[:, A1_DT:A1_DT + 128],
                                   -1.0, identB, ALU.mult, ALU.add)
            kb.incl(v, "ve")
        kb.ve(f)
        kb.sync_step("te")
        def f(t):
            t.matmul(P4[0:128, 0:128], a1[:, A1_DD:A1_DD + 128],
                     a1[:, A1_DT:A1_DT + 128], start=True, stop=True)
            kb.incl(t, "te")
        kb.te(f)
        for lvl in range(4):
            kb.sync_step("ve")
            def f(v):
                v.tensor_copy(a1[:, A1_PW:A1_PW + 128], P4[0:128, 0:128])
                v.transpose(a1[:, A1_PWT:A1_PWT + 128], a1[:, A1_PW:A1_PW + 128])
                kb.incl(v, "ve")
            kb.ve(f)
            kb.sync_step("te")
            def f(t, lvl=lvl):
                t.matmul(P5[0:128, 0:128], a1[:, A1_PWT:A1_PWT + 128],
                         a1[:, A1_GT:A1_GT + 128], start=True, stop=True)
                if lvl < 3:
                    t.matmul(P4[0:128, 0:128], a1[:, A1_PWT:A1_PWT + 128],
                             a1[:, A1_PW:A1_PW + 128], start=True, stop=True)
                kb.incl(t, "te")
            kb.te(f)
            kb.sync_step("ve")
            def f(v, lvl=lvl):
                v.tensor_tensor(a1[:, A1_GT:A1_GT + 128], a1[:, A1_GT:A1_GT + 128],
                                P5[0:128, 0:128], ALU.add)
                if lvl == 3:
                    v.transpose(a1[:, A1_GB:A1_GB + 128], a1[:, A1_GT:A1_GT + 128])
                    v.tensor_copy(a1[:, A1_TF:A1_TF + 128], a1[:, A1_GB:A1_GB + 128])
                kb.incl(v, "ve")
            kb.ve(f)
        for it in range(3):
            kb.sync_step("te")
            def f(t):
                t.matmul(P4[0:128, 0:128], a1[:, A1_NT:A1_NT + 128],
                         a1[:, A1_TF:A1_TF + 128], start=True, stop=True)
                kb.incl(t, "te")
            kb.te(f)
            kb.sync_step("ve")
            def f(v):
                v.tensor_copy(a1[0:128, A1_TS:A1_TS + 128], P4[0:128, 0:128])
                kb.incl(v, "ve")
            kb.ve(f)
            kb.sync_step("te")
            def f(t):
                t.matmul(P5[0:128, 0:128], a1[:, A1_GT:A1_GT + 128],
                         a1[:, A1_TS:A1_TS + 128], start=True, stop=True)
                kb.incl(t, "te")
            kb.te(f)
            kb.sync_step("ve")
            def f(v):
                v.tensor_tensor(a1[:, A1_TF:A1_TF + 128], a1[:, A1_GB:A1_GB + 128],
                                P5[0:128, 0:128], ALU.subtract)
                kb.incl(v, "ve")
            kb.ve(f)
        kb.sync_step("te")
        def f(t):
            t.transpose(PT[:, 0:128], a1[:, A1_TF:A1_TF + 128], identB)
            kb.incl(t, "te")
        kb.te(f)
        kb.sync_step("ve")
        def f(v):
            v.tensor_copy(a1[:, A1_TT:A1_TT + 128], PT[:, 0:128])
            v.tensor_scalar(a2[:, A2_VT:A2_VT + 128], a2[:, A2_VTM:A2_VTM + 128],
                            sd[:, SD_BET + c:SD_BET + c + 1], None, ALU.mult)
            v.tensor_scalar(a2[:, A2_KBAR:A2_KBAR + 128], a2[:, A2_KTM:A2_KTM + 128],
                            sd[:, SD_BET + c:SD_BET + c + 1], None, ALU.mult)
            kb.incl(v, "ve")
        kb.ve(f)
        kb.sync_step("te")
        def f(t):
            t.matmul(P4[0:128, 0:128], a1[:, A1_TT:A1_TT + 128],
                     a2[:, A2_VT:A2_VT + 128], start=True, stop=True)
            t.matmul(P5[0:128, 0:128], a2[:, A2_KBAR:A2_KBAR + 128],
                     a1[:, A1_TT:A1_TT + 128], start=True, stop=True)
            kb.incl(t, "te")
        kb.te(f)
        kb.sync_step("ve")
        def f(v):
            v.tensor_copy(a2[:, A2_WHI:A2_WHI + 128], P5[0:128, 0:128])
            kb.incl(v, "ve")
        kb.ve(f)
        kb.sync_step("ve")
        def f(v):
            v.tensor_copy(a1[:, A1_TS:A1_TS + 128], P4[0:128, 0:128])
            kb.incl(v, "ve")
        kb.ve(f)
        kb.sync_step("te")
        def f(t):
            t.matmul(P4[0:128, 0:128], a2[:, A2_WHI:A2_WHI + 128],
                     a2[:, A2_SBH:A2_SBH + 128], start=True, stop=True)
            kb.incl(t, "te")
        kb.te(f)
        kb.sync_step("ve")
        def f(v):
            v.tensor_tensor(a2[:, A2_VT:A2_VT + 128], a1[:, A1_TS:A1_TS + 128],
                            P4[0:128, 0:128], ALU.subtract)
            kb.incl(v, "ve")
        kb.ve(f)
        kb.sync_step("te")
        def f(t):
            t.matmul(P6[0:128, 0:128], a2[:, A2_SBH:A2_SBH + 128],
                     a1[:, A1_QH + tok0:A1_QH + tok0 + 128], start=True, stop=False)
            t.matmul(P6[0:128, 0:128], a2[:, A2_VT:A2_VT + 128],
                     a1[:, A1_LT:A1_LT + 128], start=False, stop=True)
            kb.incl(t, "te")
        kb.te(f)
        kb.sync_step("ve")
        def f(v):
            v.tensor_copy(a2[:, A2_OST:A2_OST + 128], P6[0:128, 0:128])
            kb.incl(v, "ve")
        kb.ve(f)
        kb.sync_step("ae")
        def f(a, u=u, j=j, c=c):
            kb.inc(a.dma_start(out=ag4_i[128 * u:128 * (u + 1),
                                         SPAN * j + tok0:SPAN * j + tok0 + 128],
                               in_=a2[:, A2_OST:A2_OST + 128]), "qo", 16)
        kb.ae(f)
        kb.sync_step("te")
        def f(t):
            t.matmul(P4[0:128, 0:128], a2[:, A2_KTM:A2_KTM + 128],
                     a2[:, A2_VT:A2_VT + 128], start=True, stop=True)
            kb.incl(t, "te")
        kb.te(f)
        kb.sync_step("ve")
        def f(v):
            v.tensor_tensor(wst[:, 256:384], wst[:, 256:384], P4[0:128, 0:128], ALU.add)
            v.tensor_copy(a2[:, A2_SBH:A2_SBH + 128], wst[:, 256:384])
            kb.incl(v, "ve")
        kb.ve(f)

    def emit_delta_L1():
        for u in range(2):
            kb.sync_step("ve")
            def f(v):
                v.memset(wst[:, 256:384], 0.0)
                v.memset(a2[:, A2_SBH:A2_SBH + 128], 0.0)
                kb.incl(v, "ve")
            kb.ve(f)
            for j in range(4):
                kb.sync_step("ge")
                def f(g, u=u, j=j):
                    for t_, col in enumerate((A1_QH, A1_KH, A1_V0)):
                        gc = GX_L1 + u * 12 + j * 3 + t_
                        kb.inc(g.indirect_dma_start(
                            out=a1[:, col:col + SPAN], out_offset=None, in_=ag3b_o,
                            in_offset=bass.IndirectOffsetOnAxis(
                                ap=gix[:, gc:gc + 1], axis=0)), "qgB", 16)
                    kb.inc(g.indirect_dma_start(
                        out=mrep[0:64, 0:SPAN], out_offset=None, in_=ag3f_o,
                        in_offset=bass.IndirectOffsetOnAxis(
                            ap=gix[0:64, GX_B1 + j:GX_B1 + j + 1], axis=0)), "qgB", 16)
                kb.ge(f)
                kb.sync_step("ae")
                def f(a, u=u):
                    kb.inc(a.dma_start(out=rows_hbm[4:5, 0:SPAN],
                                       in_=mrep[32 * u:32 * u + 1, 0:SPAN]), "qrw", 16)
                kb.ae(f)
                kb.sync_step("se")
                def f(s):
                    rh = rows_hbm.tensor
                    kb.inc(s.dma_start(out=wst[0:16, 1152:1280],
                                       in_=bass.AP(rh, 4 * L, [[128, 16], [1, 128]])), "qbc", 16)
                kb.se(f)
                kb.sync_step("ge")
                def f(g):
                    kb.inc(g.dma_start(out=rrep[:, 0:SPAN],
                                       in_=rows_hbm[4:5, 0:SPAN].broadcast_to((128, SPAN))),
                           "qbg", 16)
                kb.ge(f)
                kb.sync_step("te")
                def f(t):
                    t.transpose(P3[0:128, 0:16], wst[0:16, 1152:1280], identF[0:16, 0:16])
                    kb.incl(t, "te")
                kb.te(f)
                kb.sync_step("ve")
                def f(v):
                    v.tensor_copy(sd[:, SD_BET:SD_BET + 16], P3[0:128, 0:16])
                    v.tensor_tensor(a1[:, A1_KL:A1_KL + SPAN], a1[:, A1_KH:A1_KH + SPAN],
                                    rrep[:, 0:SPAN], ALU.mult)
                    kb.incl(v, "ve")
                kb.ve(f)
                for c in range(16):
                    delta_chunk_L1(u, j, c)

    kb.wait("ge", "cc")
    emit_delta_L1()
    d = dbg_out("oft1", [2 * DV1, L], BF16)
    if d is not None:
        kb.wait("ae", "qo", "ve", "te")
        def f(a, d=d):
            kb.inc(a.dma_start(out=d, in_=ag4_i), "qa", 16)
            a.wait_ge(kb.sem("qa"), kb.n["qa"])
        kb.ae(f)
    kb.wait("ge", "qo", "ve", "ae", "te")
    def f(g):
        kb.inc(g.collective_compute("AllGather", ALU.bypass,
                                    replica_groups=[list(range(8))],
                                    ins=[ag4_i.opt()], outs=[ag4_o.opt()]), "cc")
    kb.ge(f)

    kb.wait("ge", "cc")
    kb.sync_step("ge")
    def f(g):
        inw = bass.AP(ag4_o.tensor, 0, [[2048, 4 * 2048], [1, 2048]])
        for hh in range(8):
            kb.inc(g.indirect_dma_start(
                out=a1[:, hh * NL:hh * NL + SPAN], out_offset=None, in_=inw,
                in_offset=bass.IndirectOffsetOnAxis(
                    ap=gix[:, GX_OE + hh:GX_OE + hh + 1], axis=0)), "qgB", 16)
    kb.ge(f)
    kb.sync_step("se")
    def f(s):
        for hh in range(8):
            kb.inc(s.dma_start(out=a2[:, hh * NL:hh * NL + SPAN],
                               in_=ag3b_i[2048 + 128 * hh:2048 + 128 * (hh + 1), :]), "qt0", 16)
    kb.se(f)
    for hh in range(8):
        kb.sync_step("ge")
        def f(g, hh=hh):
            kb.inc(g.dma_start(out=rrep[:, 0:SPAN],
                               in_=g1_hbm[hh:hh + 1, :].broadcast_to((128, SPAN))), "qbg", 16)
        kb.ge(f)
        kb.sync_step("ve")
        def f(v, hh=hh):
            o = a1[:, hh * NL:hh * NL + SPAN]
            vt = a2[:, hh * NL:hh * NL + SPAN]
            v.tensor_tensor(o, o, vt, ALU.subtract)
            v.tensor_tensor(o, o, rrep[:, 0:SPAN], ALU.mult)
            v.tensor_tensor(o, o, vt, ALU.add)
            kb.incl(v, "ve")
        kb.ve(f)
        for ti, (n0, nw) in enumerate(_ntiles(SPAN, 512)):
            kb.sync_step("ae")
            def f(a, hh=hh, n0=n0, nw=nw):
                a.activation(scb[:, 0:nw], a1[:, hh * NL + n0:hh * NL + n0 + nw], AF.Square)
                kb.incl(a, "ae")
            kb.ae(f)
            kb.sync_step("te")
            def f(t, n0=n0, nw=nw):
                t.matmul(P4[0:1, 0:nw], cbb[:, CB_ONES:CB_ONES + 1], scb[:, 0:nw],
                         start=True, stop=True)
                kb.incl(t, "te")
            kb.te(f)
            kb.sync_step("ve")
            def f(v, n0=n0, nw=nw):
                v.tensor_scalar(mrep[0:1, n0:n0 + nw], P4[0:1, 0:nw],
                                1.0 / 128.0, 1e-5, ALU.mult, ALU.add)
                kb.incl(v, "ve")
            kb.ve(f)
        kb.sync_step("ae")
        def f(a, hh=hh):
            kb.inc(a.dma_start(out=rows_hbm[24 + hh:25 + hh, 0:SPAN], in_=mrep[0:1, 0:SPAN]), "qrw", 16)
        kb.ae(f)
        kb.sync_step("ve")
    kb.sync_step("se")
    def f(s):
        kb.inc(s.dma_start(out=mrep[0:8, 0:SPAN], in_=rows_hbm[24:32, 0:SPAN]), "qbc", 16)
    kb.se(f)
    kb.sync_step("ae")
    def f(a):
        a.activation(mrep[0:8, 0:SPAN], mrep[0:8, 0:SPAN], AF.Sqrt)
        kb.incl(a, "ae")
    kb.ae(f)
    kb.sync_step("ve")
    def f(v):
        v.reciprocal(mrep[0:8, 0:SPAN], mrep[0:8, 0:SPAN])
        kb.incl(v, "ve")
    kb.ve(f)
    kb.sync_step("ae")
    def f(a):
        kb.inc(a.dma_start(out=rows_hbm[24:32, 0:SPAN], in_=mrep[0:8, 0:SPAN]), "qrw", 16)
    kb.ae(f)
    for hh in range(8):
        kb.sync_step("ge")
        def f(g, hh=hh):
            kb.inc(g.dma_start(out=rrep[:, 0:SPAN],
                               in_=rows_hbm[24 + hh:25 + hh, 0:SPAN].broadcast_to((128, SPAN))),
                   "qbg", 16)
        kb.ge(f)
        kb.sync_step("ve")
        def f(v, hh=hh):
            v.scalar_tensor_tensor(a1[:, hh * NL:hh * NL + SPAN], a1[:, hh * NL:hh * NL + SPAN],
                                   sc[:, SC_ONW:SC_ONW + 1], rrep[:, 0:SPAN],
                                   ALU.mult, ALU.mult)
            kb.incl(v, "ve")
        kb.ve(f)
        kb.sync_step("ge")
    def on_moving(k, n0, nw):
        return a1[:, k * NL + n0:k * NL + n0 + nw]

    def evac_res4(col, pp, n0, nw, tem):
        def f(v, col=col, pp=pp, n0=n0, nw=nw, tem=tem):
            v.wait_ge(kb.sem(tem[0]), tem[1])
            v.tensor_tensor(res[:, col * NL + 4 + n0:col * NL + 4 + n0 + nw],
                            res[:, col * NL + 4 + n0:col * NL + 4 + n0 + nw],
                            pp[0:128, 0:nw], ALU.add)
            kb.incl(v, "ve")
        kb.ve(f)
        return ("ve", kb.n["ve"])

    emit_proj(w["L1_attn_o_w"], 8, 8, SPAN, on_moving, evac_res4)
    d = dbg_out("x5", [D, NL])
    if d is not None:
        kb.wait("ae", "ve")
        def f(a, d=d):
            for c in range(8):
                kb.inc(a.dma_start(out=d[128 * c:128 * (c + 1), :],
                                   in_=res[:, c * NL:(c + 1) * NL]), "qa", 16)
            a.wait_ge(kb.sem("qa"), kb.n["qa"])
        kb.ae(f)
    emit_ln(ncols=SPAN, col0=4)
    emit_ffn1(SPAN, col0=4)

    kb.wait("se", "ve", "te", "ae", "ge", "cc", "qa", "qo")
    def f(s):
        for c in range(8):
            kb.inc(s.dma_start(out=y_out[128 * c:128 * (c + 1), :],
                               in_=res[:, c * NL + HL:c * NL + HL + SPAN]), "qs", 16)
        s.wait_ge(kb.sem("qs"), kb.n["qs"])
    kb.se(f)

    es.close()
    return nc, sorted(dbg.keys())



def _gidx_table(r):
    b, qd = divmod(r, 4)
    h = r % 4
    t = np.zeros((128, 96), np.uint32)
    p = np.arange(128)
    for j in range(4):
        rank = 4 * b + j
        col = np.zeros(128, np.uint32)
        col[:32] = 8 * rank + h
        col[32:64] = 8 * rank + 4 + h
        t[:, 0 + j] = col
        offs = [192 * h, 192 * h + 128, 768 + 192 * h, 768 + 192 * h + 128,
                1536 + 384 * h, 1536 + 384 * h + 128, 1536 + 384 * h + 256]
        for tt_, off in enumerate(offs):
            t[:, 4 + 7 * j + tt_] = 3072 * rank + off + np.minimum(p, 127)
    for cc in range(12):
        hh, ds = divmod(cc, 3)
        row = 384 * (4 * b + hh) + 128 * ds + p
        t[:, 32 + cc] = row * 4 + qd
        t[:, 44 + cc] = np.maximum(row * 128 + 32 * qd - 1, 0)
    for u in range(2):
        hh = 2 * qd + u if False else (2 * r + u) % 8
        for j in range(4):
            rank = 4 * b + j
            for tt_, off in enumerate((0, 1024, 2048)):
                t[:, 56 + u * 12 + j * 3 + tt_] = 3072 * rank + off + 128 * hh + p
    for j in range(4):
        rank = 4 * b + j
        col = np.zeros(128, np.uint32)
        col[:32] = 8 * rank + (2 * r) % 8
        col[32:64] = 8 * rank + (2 * r + 1) % 8
        t[:, 80 + j] = col
    for hh in range(8):
        u = 8 * b + hh
        row = 256 * (u // 2) + 128 * (u % 2) + p
        t[:, 84 + hh] = row * 4 + qd
    return t


def _host_layout(inputs):
    x = np.asarray(inputs["L0_attn_x"])
    cf, cbm = _host_consts()
    pos_all = np.arange(L, dtype=np.float32)
    div = np.exp(np.arange(0, 32, 2, dtype=np.float32) * (-math.log(10000.0) / 32))
    pe_full = np.zeros((L, 32), np.float32)
    ang = pos_all[:, None] * div[None, :]
    pe_full[:, 0::2] = np.sin(ang)
    pe_full[:, 1::2] = np.cos(ang)

    maps = []
    for r in range(NCORE):
        b, qd = divmod(r, 4)
        s0 = qd * SPAN
        lo, hi = s0 - HL, s0 + SPAN + HR
        clo, chi = max(lo, 0), min(hi, L)
        xs = np.zeros((NL, D), np.float32)
        xs[clo - lo:chi - lo] = x[b, clo:chi]
        pes = np.zeros((NL, 32), np.float32)
        pes[clo - lo:chi - lo] = pe_full[clo:chi]
        g = np.arange(lo, hi)
        hm = ((g >= 0) & (g < L)).astype(np.float32)[None, :]
        m = {
            "x_fm": np.ascontiguousarray(xs.T),
            "pe_fm": np.ascontiguousarray(pes.T),
            "hmask": np.ascontiguousarray(hm),
            "consts_f": cf,
            "consts_b": cbm,
            "gidx": _gidx_table(r),
        }
        for name in ["L0_attn_W_in", "L0_attn_W_q", "L0_attn_W_k", "L0_attn_W_v",
                     "L0_attn_W_g", "L0_attn_W_gk", "L0_attn_W_b", "L0_attn_W_so",
                     "L0_attn_W_out", "L0_ffn_weight_fp8", "L1_attn_q_w",
                     "L1_attn_k_w", "L1_attn_v_w", "L1_attn_b_w", "L1_attn_mix_w",
                     "L1_attn_o_w", "L1_ffn_tile_sigs"]:
            m[name] = np.ascontiguousarray(np.asarray(inputs[name], dtype=np.float32))
        m["L1_ffn_W"] = np.ascontiguousarray(
            np.asarray(inputs["L1_ffn_W"], np.float32).reshape(8 * 1024, 1024))
        m["L1_ffn_b"] = np.ascontiguousarray(np.asarray(inputs["L1_ffn_b"], np.float32))
        for name in ["L0_attn_conv_x_w", "L0_attn_conv_z_w", "L1_attn_conv_q_w",
                     "L1_attn_conv_k_w", "L1_attn_conv_v_w"]:
            m[name] = np.ascontiguousarray(
                np.asarray(inputs[name], np.float32).reshape(1024, 4))
        for name in ["L0_attn_conv_x_b", "L0_attn_conv_z_b", "L0_attn_b_b",
                     "L0_attn_A_log", "L0_attn_dt_bias", "L0_attn_g_norm_w",
                     "L0_ffn_bias", "L0_ffn_input_scale", "L0_ffn_weight_scale",
                     "L1_attn_mix_b", "L1_attn_mix_bias", "L1_attn_o_norm_w"]:
            m[name] = np.ascontiguousarray(
                np.asarray(inputs[name], np.float32).reshape(-1, 1))
        maps.append(m)
    return maps


def kernel(**inputs):
    maps = _host_layout(inputs)
    nc, dbg_names = build_program()
    res = run_bass_kernel_spmd(nc, maps, core_ids=list(range(NCORE)))
    out = np.zeros((B, L, D), np.float32)
    for r in range(NCORE):
        b, qd = divmod(r, 4)
        out[b, qd * SPAN:(qd + 1) * SPAN, :] = np.asarray(
            res.results[r]["y_fm"], np.float32).T
    if DEBUG:
        kernel.debug = [{n: np.asarray(res.results[r]["dbg_" + n]) for n in dbg_names}
                        for r in range(NCORE)]
    return out


# version 08 (final)
# speedup vs baseline: 1.3050x; 1.0105x over previous
import contextlib
import math
import numpy as np

import concourse.bass as bass
import concourse.mybir as mybir
from concourse.bass_utils import run_bass_kernel_spmd

F32 = mybir.dt.float32
BF16 = mybir.dt.bfloat16
FP8 = mybir.dt.float8e4
ALU = mybir.AluOpType
AF = mybir.ActivationFunctionType
AX = mybir.AxisListType

B, L, D = 2, 8192, 1024
NCORE = 8
SPAN = 2048
HL, HR = 4, 2
NL = SPAN + HL + HR
NEG = -1e30

H0, DK0, DV0 = 4, 192, 384
H1, DK1, DV1 = 8, 128, 128
CC = 128

DEBUG = ()


def _ntiles(n, t=512):
    out, c = [], 0
    while c < n:
        out.append((c, min(t, n - c)))
        c += t
    return out



def _host_consts():
    iv = np.arange(128)
    col, row = iv[None, :], iv[:, None]
    up_strict = col > row
    up_incl = col >= row
    same_blk = col // 32 == row // 32
    mb = lambda m: np.where(m, 0.0, NEG).astype(np.float32)
    mk = lambda m: m.astype(np.float32)
    ident = np.eye(128, dtype=np.float32)
    cols = [np.ones((128, 1), np.float32)]
    for v in (1e-5, 1e-6, 1e-12, 0.5, 1.0 / 1024.0, 1.0 / 384.0, 1.0 / 128.0):
        cols.append(np.full((128, 1), v, np.float32))
    cf = np.concatenate(cols + [
        ident,
        mb(up_strict), mb(up_incl),
        mb(up_strict & same_blk), mb(up_strict & ~same_blk),
        mk(up_strict), mk(up_incl),
        mk(up_strict & same_blk), mk(up_strict & ~same_blk),
    ], axis=1)
    top64 = (iv < 64).astype(np.float32)[:, None]
    cbm = np.concatenate([
        np.eye(128, dtype=np.float32),
        np.ones((128, 1), np.float32),
        np.ones((128, 1), np.float32), top64, 1.0 - top64,
    ], axis=1)
    import ml_dtypes
    return cf, cbm.astype(ml_dtypes.bfloat16)

CF_ONES = 0
CF_EPS5, CF_EPS6, CF_EPS12, CF_HALF, CF_RD, CF_RDV, CF_R128 = 1, 2, 3, 4, 5, 6, 7
CF_ID = 8
CF_MB_UP_STRICT, CF_MB_UP_INCL, CF_MB_BD, CF_MB_BELOW = 136, 264, 392, 520
CF_MSK_UP_STRICT, CF_MSK_UP_INCL, CF_MSK_BD, CF_MSK_BELOW = 648, 776, 904, 1032
CB_ID, CB_ONES, CB_FULL, CB_TOP, CB_BOT = 0, 128, 129, 130, 131

SC_CXW, SC_CZW, SC_CQW, SC_CKW, SC_CVW = 0, 32, 64, 96, 128
SC_CXB, SC_CZB = 160, 168
SC_AL, SC_EAL, SC_DTB, SC_BB = 176, 177, 178, 179
SC_GNW = 180
SC_ONW = 183
SC_MXB = 184
SC_SIN, SC_RSIN, SC_WS = 186, 187, 188
SC_F0B = 190
SC_STAT = 200
SC_NRM = 260
SC_END = 260 + 2048


class _Last:
    _SKIP = {"wait_ge", "nop", "maybe_nop_then_inc", "drain"}

    def __init__(self, eng, autodrain):
        object.__setattr__(self, "_e", eng)
        object.__setattr__(self, "last", None)
        object.__setattr__(self, "_ad", autodrain)

    def __getattr__(self, name):
        f = getattr(self._e, name)
        if callable(f):
            skip = name in _Last._SKIP
            def g(*a, **k):
                if (not skip) and self._ad:
                    self._e.drain()
                r = f(*a, **k)
                if (not skip) and r is not None and hasattr(r, "then_inc"):
                    object.__setattr__(self, "last", r)
                return r
            return g
        return f


class KB:
    def __init__(self):
        self.nc = bass.Bass("TRN2", target_bir_lowering=False)
        self.es = contextlib.ExitStack()
        self.block = None
        self._sems = {}
        self.n = {}
        self.seen = {}

    def sem(self, name):
        if name not in self._sems:
            self._sems[name] = self.es.enter_context(self.nc.semaphore(name))
            self.n[name] = 0
        return self._sems[name]

    def inc(self, instr, name, k=1):
        instr.then_inc(self.sem(name), k)
        self.n[name] += k
        return self.n[name]

    ENG = {"te": "tensor", "ve": "vector", "ae": "scalar", "ge": "gpsimd",
           "se": "sync"}

    def te(self, fn): self.block.tensor(lambda e: fn(_Last(e, False)))
    def ve(self, fn): self.block.vector(lambda e: fn(_Last(e, True)))
    def ae(self, fn): self.block.scalar(lambda e: fn(_Last(e, True)))
    def ge(self, fn): self.block.gpsimd(lambda e: fn(_Last(e, True)))
    def se(self, fn): self.block.sync(lambda e: fn(_Last(e, False)))

    def incl(self, proxy, name, k=1):
        assert proxy.last is not None, "no instruction to attach inc to"
        return self.inc(proxy.last, name, k)

    def wait(self, eng, *sems):
        vals = [(self.sem(s), self.n[s]) for s in sems if self.n.get(s, 0) > 0]
        if not vals:
            return
        def f(e):
            for h, v in vals:
                e.wait_ge(h, v)
        getattr(self.block, self.ENG[eng])(f)

    def sync_step(self, eng, sems=("te", "ve", "ae", "ge", "qgB", "qt0", "qt1",
                                   "qo", "qrw", "qbc", "qbg", "qg", "qa", "qs")):
        seen = self.seen.setdefault(eng, {})
        vals = []
        for s in sems:
            n = self.n.get(s, 0)
            if n > seen.get(s, 0):
                vals.append((self.sem(s), n))
                seen[s] = n
        if vals:
            def f(e):
                for h, v in vals:
                    e.wait_ge(h, v)
            getattr(self.block, self.ENG[eng])(f)

    def waitm(self, eng, marks):
        vals = [(self.sem(s), v) for (s, v) in marks if v > 0]
        if not vals:
            return
        def f(e):
            for h, v in vals:
                e.wait_ge(h, v)
        getattr(self.block, self.ENG[eng])(f)


def build_program():
    kb = KB()
    nc, es = kb.nc, kb.es

    def din(name, shape, dtype=F32):
        return nc.dram_tensor(name, shape, dtype, kind="ExternalInput").ap()

    x_in = din("x_fm", [D, NL])
    gidx_in = din("gidx", [128, 96], mybir.dt.uint32)
    pe_in = din("pe_fm", [32, NL])
    hmask = din("hmask", [1, NL])
    consts_f = din("consts_f", [128, 1160])
    consts_b = din("consts_b", [128, 132], BF16)

    w = {}
    for name, shape in [
        ("L0_attn_W_in", [1024, 2048]), ("L0_attn_conv_x_w", [1024, 4]),
        ("L0_attn_conv_x_b", [1024, 1]), ("L0_attn_conv_z_w", [1024, 4]),
        ("L0_attn_conv_z_b", [1024, 1]), ("L0_attn_W_q", [1024, 768]),
        ("L0_attn_W_k", [1024, 768]), ("L0_attn_W_v", [1024, 1536]),
        ("L0_attn_W_g", [1024, 1536]), ("L0_attn_W_gk", [1024, 4]),
        ("L0_attn_W_b", [1024, 4]), ("L0_attn_b_b", [4, 1]),
        ("L0_attn_A_log", [4, 1]), ("L0_attn_dt_bias", [4, 1]),
        ("L0_attn_g_norm_w", [384, 1]), ("L0_attn_W_so", [1536, 1024]),
        ("L0_attn_W_out", [2048, 1024]), ("L0_ffn_weight_fp8", [1024, 1024]),
        ("L0_ffn_bias", [1024, 1]), ("L0_ffn_input_scale", [1, 1]),
        ("L0_ffn_weight_scale", [1, 1]), ("L1_attn_q_w", [1024, 1024]),
        ("L1_attn_k_w", [1024, 1024]), ("L1_attn_v_w", [1024, 1024]),
        ("L1_attn_conv_q_w", [1024, 4]), ("L1_attn_conv_k_w", [1024, 4]),
        ("L1_attn_conv_v_w", [1024, 4]), ("L1_attn_b_w", [1024, 8]),
        ("L1_attn_mix_w", [1024, 8]), ("L1_attn_mix_b", [8, 1]),
        ("L1_attn_mix_bias", [8, 1]), ("L1_attn_o_norm_w", [128, 1]),
        ("L1_attn_o_w", [1024, 1024]), ("L1_ffn_tile_sigs", [8, 32]),
        ("L1_ffn_W", [8 * 1024, 1024]), ("L1_ffn_b", [8, 1024]),
    ]:
        w[name] = din(name, shape)

    y_out = nc.dram_tensor("y_fm", [D, SPAN], F32, kind="ExternalOutput").ap()

    dbg = {}

    def dbg_out(name, shape, dtype=F32):
        if name in DEBUG and name not in dbg:
            dbg[name] = nc.dram_tensor("dbg_" + name, shape, dtype,
                                       kind="ExternalOutput").ap()
        return dbg.get(name)

    def dint(name, shape, dtype):
        return nc.dram_tensor(name, shape, dtype).ap()

    ag1b_i = dint("ag1b_i", [3072, SPAN], BF16)
    ag1b_o = dint("ag1b_o", [8 * 3072, SPAN], BF16)
    ag1f_i = dint("ag1f_i", [8, SPAN], F32)
    ag1f_o = dint("ag1f_o", [64, SPAN], F32)
    ag2_i = dint("ag2_i", [DV0, L], BF16)
    ag2_o = dint("ag2_o", [8 * DV0, L], BF16)
    ag3b_i = dint("ag3b_i", [3072, SPAN], BF16)
    ag3b_o = dint("ag3b_o", [8 * 3072, SPAN], BF16)
    ag3f_i = dint("ag3f_i", [8, SPAN], F32)
    ag3f_o = dint("ag3f_o", [64, SPAN], F32)
    ag4_i = dint("ag4_i", [2 * DV1, L], BF16)
    ag4_o = dint("ag4_o", [16 * DV1, L], BF16)
    zs_hbm = dint("zs_hbm", [1024, 2051], BF16)
    g_hbm = dint("g_hbm", [1536, 2051], BF16)
    stats_hbm = dint("stats_hbm", [60, 128], F32)
    rows_hbm = dint("rows_hbm", [64, L], F32)
    wf0T_hbm = dint("wf0T_hbm", [1024, 1024], BF16)
    onehot_hbm = dint("onehot_hbm", [8, NL], F32)
    g1_hbm = dint("g1_hbm", [8, SPAN], F32)
    wT_hbm = dint("wT_hbm", [8 * 1024, 1024], BF16)

    sb = lambda name, shape, dt=F32: es.enter_context(nc.sbuf_tensor(name, shape, dt))
    res = sb("res", [128, 8 * NL])
    xln = sb("xln", [128, 8 * NL], BF16)
    a1 = sb("a1", [128, 8 * NL], BF16)
    a2 = sb("a2", [128, 8 * NL], BF16)
    wr = sb("wr", [128, 2 * 2048], BF16)
    wst = sb("wst", [128, 2048])
    mrep = sb("mrep", [128, NL])
    rrep = sb("rrep", [128, NL], BF16)
    sc = sb("sc", [128, SC_END])
    scb = sb("scb", [128, 2048], BF16)
    q8t = sb("q8t", [128, 512], FP8)
    sd = sb("sd", [128, 96])
    gix = sb("gix", [128, 96], mybir.dt.uint32)

    ps = lambda name, shape, dt=F32: es.enter_context(nc.psum_tensor(name, shape, dt))
    P0 = ps("P0", [128, 512])
    P1 = ps("P1", [128, 512])
    P2 = ps("P2", [128, 512])
    P3 = ps("P3", [128, 512])
    P4 = ps("P4", [128, 512])
    P5 = ps("P5", [128, 512])
    P6 = ps("P6", [128, 512])
    PT = ps("PT", [128, 1024], BF16)

    kb.block = es.enter_context(nc.Block())

    cfb = sb("cfb", [128, 1160])
    cbb = sb("cbb", [128, 132], BF16)
    identF = cfb[:, CF_ID:CF_ID + 128]
    identB = cbb[:, CB_ID:CB_ID + 128]

    def cfm(off):
        return cfb[:, off:off + 128]

    def f(s):
        i1 = s.dma_start(out=cfb[:], in_=consts_f)
        kb.inc(i1, "qs", 16)
        kb.inc(s.dma_start(out=gix[:], in_=gidx_in), "qs", 16)
        i2 = s.dma_start(out=cbb[:], in_=consts_b)
        kb.inc(i2, "qs", 16)
        for c in range(8):
            kb.inc(s.dma_start(out=res[:, c * NL:(c + 1) * NL],
                               in_=x_in[128 * c:128 * (c + 1), :]), "qs", 16)
    kb.se(f)

    def f(s):
        for cget, dst in ((w["L0_attn_conv_x_w"], SC_CXW), (w["L0_attn_conv_z_w"], SC_CZW),
                          (w["L1_attn_conv_q_w"], SC_CQW), (w["L1_attn_conv_k_w"], SC_CKW),
                          (w["L1_attn_conv_v_w"], SC_CVW)):
            for c in range(8):
                kb.inc(s.dma_start(out=sc[:, dst + 4 * c:dst + 4 * c + 4],
                                   in_=cget[128 * c:128 * (c + 1), :]), "qa", 16)
        for cget, dst in ((w["L0_attn_conv_x_b"], SC_CXB), (w["L0_attn_conv_z_b"], SC_CZB),
                          (w["L0_ffn_bias"], SC_F0B)):
            kb.inc(s.dma_start(out=sc[:, dst:dst + 8],
                               in_=cget.rearrange("(a b) c -> b (a c)", b=128),
                               allow_slow_non_contiguous=True), "qa", 16)
        kb.inc(s.dma_start(out=sc[0:4, SC_AL:SC_AL + 1], in_=w["L0_attn_A_log"]), "qa", 16)
        kb.inc(s.dma_start(out=sc[0:4, SC_DTB:SC_DTB + 1], in_=w["L0_attn_dt_bias"]), "qa", 16)
        kb.inc(s.dma_start(out=sc[0:4, SC_BB:SC_BB + 1], in_=w["L0_attn_b_b"]), "qa", 16)
        kb.inc(s.dma_start(out=sc[:, SC_GNW:SC_GNW + 3],
                           in_=w["L0_attn_g_norm_w"].rearrange("(a b) c -> b (a c)", b=128),
                           allow_slow_non_contiguous=True), "qa", 16)
        kb.inc(s.dma_start(out=sc[:, SC_ONW:SC_ONW + 1], in_=w["L1_attn_o_norm_w"]), "qa", 16)
        kb.inc(s.dma_start(out=sc[0:8, SC_MXB:SC_MXB + 1], in_=w["L1_attn_mix_b"]), "qa", 16)
        kb.inc(s.dma_start(out=sc[0:8, SC_MXB + 1:SC_MXB + 2], in_=w["L1_attn_mix_bias"]), "qa", 16)
        kb.inc(s.dma_start(out=sc[:, SC_SIN:SC_SIN + 1],
                           in_=w["L0_ffn_input_scale"].broadcast_to((128, 1))), "qa", 16)
        kb.inc(s.dma_start(out=sc[:, SC_WS:SC_WS + 1],
                           in_=w["L0_ffn_weight_scale"].broadcast_to((128, 1))), "qa", 16)
    kb.ae(f)
    kb.wait("ve", "qa")
    def f(v):
        v.memset(sc[:, SC_STAT:SC_STAT + 60], 0.0)
        v.reciprocal(sc[:, SC_RSIN:SC_RSIN + 1], sc[:, SC_SIN:SC_SIN + 1])
        v.tensor_add(sc[0:8, SC_MXB:SC_MXB + 1], sc[0:8, SC_MXB:SC_MXB + 1],
                     sc[0:8, SC_MXB + 1:SC_MXB + 2])
        kb.incl(v, "ve")
    kb.ve(f)
    kb.wait("ae", "qa")
    def f(a):
        a.activation(sc[0:4, SC_EAL:SC_EAL + 1], sc[0:4, SC_AL:SC_AL + 1], AF.Exp)
        kb.incl(a, "ae")
    kb.ae(f)

    def emit_ln(ncols=NL, eps_off=CF_EPS5, quant=False, col0=0):
        tiles = _ntiles(ncols, 128)
        nt = len(tiles)
        kb.wait("te", "ve", "ae", "qs")
        extract_marks = {}
        for ti, (t0, tw) in enumerate(tiles):
            pp = P2 if ti % 2 == 0 else P3
            pq = P4 if ti % 2 == 0 else P5
            if ti >= 2:
                kb.waitm("te", [extract_marks[ti - 2]])
            def f(t, t0=t0, tw=tw, pp=pp, pq=pq):
                for c in range(8):
                    xt = res[:, c * NL + col0 + t0:c * NL + col0 + t0 + tw]
                    t.matmul(pp[0:tw, 0:tw], xt, xt, start=(c == 0), stop=(c == 7))
                    t.matmul(pq[0:tw, 0:1], xt, cfb[:, CF_ONES:CF_ONES + 1],
                             start=(c == 0), stop=(c == 7))
                kb.incl(t, "te")
            kb.te(f)
            tev = kb.n["te"]
            def f(v, ti=ti, tw=tw, pp=pp, pq=pq, tev=tev):
                v.wait_ge(kb.sem("te"), tev)
                v.tensor_tensor(sc[0:tw, SC_NRM:SC_NRM + tw], pp[0:tw, 0:tw],
                                identF[0:tw, 0:tw], ALU.mult)
                v.reduce_sum(sc[0:tw, SC_STAT + ti:SC_STAT + ti + 1],
                             sc[0:tw, SC_NRM:SC_NRM + tw], axis=AX.X)
                v.tensor_copy(sc[0:tw, SC_STAT + 20 + ti:SC_STAT + 21 + ti],
                              pq[0:tw, 0:1])
                kb.incl(v, "ve")
            kb.ve(f)
            extract_marks[ti] = ("ve", kb.n["ve"])
        def f(v):
            st = sc[:, SC_STAT:SC_STAT + 60]
            v.tensor_scalar(st[:, 20:20 + nt], st[:, 20:20 + nt], 1.0 / 1024.0, None, ALU.mult)
            v.tensor_scalar(st[:, 0:nt], st[:, 0:nt], 1.0 / 1024.0, None, ALU.mult)
            kb.incl(v, "ve")
        kb.ve(f)
        kb.wait("ae", "ve")
        def f(a):
            st = sc[:, SC_STAT:SC_STAT + 60]
            a.activation(st[:, 40:40 + nt], st[:, 20:20 + nt], AF.Square)
            kb.incl(a, "ae")
        kb.ae(f)
        kb.wait("ve", "ae")
        def f(v):
            st = sc[:, SC_STAT:SC_STAT + 60]
            v.tensor_tensor(st[:, 0:nt], st[:, 0:nt], st[:, 40:40 + nt], ALU.subtract)
            kb.incl(v, "ve")
        kb.ve(f)
        kb.wait("ae", "ve")
        def f(a):
            st = sc[:, SC_STAT:SC_STAT + 60]
            a.activation(st[:, 0:nt], st[:, 0:nt], AF.Sqrt,
                         bias=cfb[:, eps_off:eps_off + 1])
            kb.incl(a, "ae")
        kb.ae(f)
        kb.wait("ve", "ae")
        def f(v):
            st = sc[:, SC_STAT:SC_STAT + 60]
            v.reciprocal(st[:, 0:nt], st[:, 0:nt])
            if quant:
                v.tensor_scalar(st[:, 40:40 + nt], st[:, 0:nt],
                                sc[:, SC_RSIN:SC_RSIN + 1], None, ALU.mult)
            kb.incl(v, "ve")
        kb.ve(f)
        kb.wait("te", "ve")
        def f(t):
            t.transpose(P2[0:60, 0:128], sc[:, SC_STAT:SC_STAT + 60], identF)
            kb.incl(t, "te")
        kb.te(f)
        kb.wait("ve", "te")
        def f(v):
            v.tensor_copy(sc[0:60, SC_NRM:SC_NRM + 128], P2[0:60, 0:128])
            kb.incl(v, "ve")
        kb.ve(f)
        kb.wait("ae", "ve")
        def f(a):
            kb.inc(a.dma_start(out=stats_hbm, in_=sc[0:60, SC_NRM:SC_NRM + 128]), "qln", 16)
        kb.ae(f)
        kb.wait("ge", "qln")
        def f(g):
            flat_r = bass.AP(stats_hbm.tensor, 0, [[0, 128], [1, ncols]])
            flat_m = bass.AP(stats_hbm.tensor, 20 * 128, [[0, 128], [1, ncols]])
            kb.inc(g.dma_start(out=rrep[:, 0:ncols], in_=flat_r), "qg", 16)
            kb.inc(g.dma_start(out=mrep[:, 0:ncols], in_=flat_m), "qg", 16)
        kb.ge(f)
        if not quant:
            kb.wait("ve", "qg")
            def f(v):
                for c in range(8):
                    v.tensor_tensor(xln[:, c * NL:c * NL + ncols],
                                    res[:, c * NL + col0:c * NL + col0 + ncols],
                                    mrep[:, 0:ncols], ALU.subtract)
                    v.tensor_tensor(xln[:, c * NL:c * NL + ncols],
                                    xln[:, c * NL:c * NL + ncols], rrep[:, 0:ncols],
                                    ALU.mult)
                kb.incl(v, "ve")
            kb.ve(f)
        else:
            kb.wait("se", "te", "ve", "qln")
            for w0, ww in _ntiles(ncols, 2048):
                if w0 > 0:
                    kb.wait("se", "ve")
                def f(s, w0=w0, ww=ww):
                    flat_rp = bass.AP(stats_hbm.tensor, 40 * 128 + w0, [[0, 128], [1, ww]])
                    kb.inc(s.dma_start(out=wst[:, 0:ww], in_=flat_rp), "qln2", 16)
                kb.se(f)
                kb.wait("ve", "qln2", "qg")
                def f(v, w0=w0, ww=ww):
                    for c in range(8):
                        t = sc[:, SC_NRM:SC_NRM + ww]
                        v.tensor_tensor(t, res[:, c * NL + col0 + w0:c * NL + col0 + w0 + ww],
                                        mrep[:, w0:w0 + ww], ALU.subtract)
                        v.tensor_tensor(t, t, wst[:, 0:ww], ALU.mult)
                        for h0, hw in _ntiles(ww, 512):
                            v.tensor_scalar(q8t[:, 0:hw], t[:, h0:h0 + hw],
                                            448.0, -448.0, ALU.min, ALU.max)
                            v.tensor_copy(a2[:, c * NL + w0 + h0:c * NL + w0 + h0 + hw],
                                          q8t[:, 0:hw])
                        v.tensor_scalar(xln[:, c * NL + w0:c * NL + w0 + ww], t,
                                        sc[:, SC_SIN:SC_SIN + 1], None, ALU.mult)
                    kb.incl(v, "ve")
                kb.ve(f)

    def emit_proj(w_ap, kchunks, colchunks, ncols, moving, evac, bf16src=False):
        tiles = _ntiles(ncols, 512)
        ready = {}
        mm_done = {}
        psum_free = {}
        kb.wait("te", "ve", "ae")
        kb.wait("se", "te")
        for col in range(colchunks):
            slot = col % 2
            qw = "qw%d" % slot
            if bf16src:
                marks = [mm_done[col - 2]] if col >= 2 else []
                kb.waitm("se", marks)
                def f(s, col=col, slot=slot, qw=qw):
                    for k in range(kchunks):
                        kb.inc(s.dma_start(
                            out=wr[:, 2048 * slot + 128 * k:2048 * slot + 128 * k + 128],
                            in_=w_ap[128 * k:128 * (k + 1), 128 * col:128 * col + 128]),
                            qw, 16)
                kb.se(f)
                ready[col] = (qw, kb.n[qw])
            else:
                marks = [ready[col - 1]] if col >= 1 else []
                if col >= 2:
                    marks.append(mm_done[col - 2])
                kb.waitm("se", marks)
                def f(s, col=col, qw=qw):
                    for k in range(kchunks):
                        kb.inc(s.dma_start(
                            out=wst[:, 128 * k:128 * k + 128],
                            in_=w_ap[128 * k:128 * (k + 1), 128 * col:128 * col + 128]),
                            qw, 16)
                kb.se(f)
                marks = [(qw, kb.n[qw])]
                if col >= 2:
                    marks.append(mm_done[col - 2])
                kb.waitm("ve", marks)
                def f(v, slot=slot, kchunks=kchunks):
                    v.tensor_copy(wr[:, 2048 * slot:2048 * slot + 128 * kchunks],
                                  wst[:, 0:128 * kchunks])
                    kb.incl(v, "ve")
                kb.ve(f)
                ready[col] = ("ve", kb.n["ve"])
            kb.waitm("te", [ready[col]])
            for ti, (n0, nw) in enumerate(tiles):
                gi = col * len(tiles) + ti
                pp = P0 if gi % 2 == 0 else P1
                if gi >= 2:
                    kb.waitm("te", [psum_free[gi - 2]])
                def f(t, col=col, slot=slot, n0=n0, nw=nw, pp=pp, kchunks=kchunks):
                    for k in range(kchunks):
                        mv = moving(k, n0, nw)
                        t.matmul(pp[0:128, 0:nw],
                                 wr[0:mv.shape[0],
                                    2048 * slot + 128 * k:2048 * slot + 128 * k + 128],
                                 mv, start=(k == 0), stop=(k == kchunks - 1))
                    kb.incl(t, "te")
                kb.te(f)
                psum_free[gi] = evac(col, pp, n0, nw, ("te", kb.n["te"]))
            mm_done[col] = ("te", kb.n["te"])

    def xln_moving(k, n0, nw):
        return xln[:, k * NL + n0:k * NL + n0 + nw]

    def emit_trans_prep(src_ap, dst_ap, kchunks, colchunks, scale=None):
        for col in range(colchunks):
            kb.wait("se", "ve", "qa")
            def f(s, col=col):
                for k in range(kchunks):
                    kb.inc(s.dma_start(
                        out=wst[:, 128 * k:128 * k + 128],
                        in_=src_ap[128 * col:128 * col + 128,
                                   128 * k:128 * k + 128]), "qs", 16)
            kb.se(f)
            kb.wait("ve", "qs", "qa")
            if scale is not None:
                def f(v, kchunks=kchunks):
                    v.tensor_scalar(scb[:, 0:128 * kchunks], wst[:, 0:128 * kchunks],
                                    scale, None, ALU.mult)
                    kb.incl(v, "ve")
            else:
                def f(v, kchunks=kchunks):
                    v.tensor_copy(scb[:, 0:128 * kchunks], wst[:, 0:128 * kchunks])
                    kb.incl(v, "ve")
            kb.ve(f)
            kb.wait("te", "ve")
            def f(t, kchunks=kchunks):
                for k in range(kchunks):
                    t.transpose(PT[:, 128 * k:128 * k + 128],
                                scb[:, 128 * k:128 * k + 128], identB)
                kb.incl(t, "te")
            kb.te(f)
            kb.wait("ve", "te")
            def f(v, kchunks=kchunks):
                v.tensor_copy(scb[:, 0:128 * kchunks], PT[:, 0:128 * kchunks])
                kb.incl(v, "ve")
            kb.ve(f)
            kb.wait("ae", "ve")
            def f(a, col=col, kchunks=kchunks):
                for k in range(kchunks):
                    kb.inc(a.dma_start(
                        out=dst_ap[128 * k:128 * k + 128, 128 * col:128 * col + 128],
                        in_=scb[:, 128 * k:128 * k + 128]), "qa", 16)
            kb.ae(f)

    def emit_wf0_prep():
        emit_trans_prep(w["L0_ffn_weight_fp8"], wf0T_hbm, 8, 8,
                        scale=sc[:, SC_WS:SC_WS + 1])

    def emit_ffn0(ncols, col0=0):
        def moving(k, n0, nw):
            return a2[:, k * NL + n0:k * NL + n0 + nw]

        sl_state = [0]
        slot_dve = {0: None, 1: None}

        def evac(col, pp, n0, nw, tem):
            sl = sl_state[0] % 2
            sl_state[0] += 1
            prev = slot_dve[sl]
            def f(a, col=col, nw=nw, pp=pp, tem=tem, sl=sl, prev=prev):
                a.wait_ge(kb.sem(tem[0]), tem[1])
                if prev is not None:
                    a.wait_ge(kb.sem(prev[0]), prev[1])
                a.activation(sc[:, SC_NRM + 600 * sl:SC_NRM + 600 * sl + nw],
                             pp[0:128, 0:nw], AF.Identity,
                             bias=sc[:, SC_F0B + col:SC_F0B + col + 1],
                             scale=sc[:, SC_SIN:SC_SIN + 1])
                kb.incl(a, "ae")
            kb.ae(f)
            aev = kb.n["ae"]
            def f(v, col=col, n0=n0, nw=nw, aev=aev, sl=sl):
                v.wait_ge(kb.sem("ae"), aev)
                v.scalar_tensor_tensor(res[:, col * NL + col0 + n0:col * NL + col0 + n0 + nw],
                                       sc[:, SC_NRM + 600 * sl:SC_NRM + 600 * sl + nw],
                                       0.5, res[:, col * NL + col0 + n0:col * NL + col0 + n0 + nw],
                                       ALU.mult, ALU.add)
                kb.incl(v, "ve")
            kb.ve(f)
            slot_dve[sl] = ("ve", kb.n["ve"])
            return ("ae", aev)

        kb.wait("se", "qa")
        emit_proj(wf0T_hbm, 8, 8, ncols, moving, evac, bf16src=True)
        kb.wait("ae", "ve")

    emit_wf0_prep()
    emit_ln(quant=True)
    emit_ffn0(NL)

    d = dbg_out("x1", [D, NL])
    if d is not None:
        kb.wait("ae", "ve")
        def f(a, d=d):
            for c in range(8):
                kb.inc(a.dma_start(out=d[128 * c:128 * (c + 1), :],
                                   in_=res[:, c * NL:(c + 1) * NL]), "qa", 16)
            a.wait_ge(kb.sem("qa"), kb.n["qa"])
        kb.ae(f)

    emit_ln()
    d = dbg_out("xln2", [D, NL], BF16)
    if d is not None:
        kb.wait("ae", "ve")
        def f(a, d=d):
            for c in range(8):
                kb.inc(a.dma_start(out=d[128 * c:128 * (c + 1), :],
                                   in_=xln[:, c * NL:(c + 1) * NL]), "qa", 16)
            a.wait_ge(kb.sem("qa"), kb.n["qa"])
        kb.ae(f)

    kb.wait("ge", "ve")
    def f(g):
        kb.inc(g.dma_start(out=rrep[:, 0:NL], in_=hmask.broadcast_to((128, NL))),
               "qg", 16)
    kb.ge(f)
    kb.wait("ve", "qg")

    def conv_half(zhalf):
        col_off = 8 if zhalf else 0
        store = {}

        def evac(col, pp, n0, nw, tem):
            def f(v, col=col, pp=pp, n0=n0, nw=nw, tem=tem):
                v.wait_ge(kb.sem(tem[0]), tem[1])
                v.tensor_copy(a1[:, col * NL + n0:col * NL + n0 + nw], pp[0:128, 0:nw])
                kb.incl(v, "ve")
            kb.ve(f)
            return ("ve", kb.n["ve"])

        emit_proj(w["L0_attn_W_in"][:, col_off * 128:(col_off + 8) * 128],
                  8, 8, NL, xln_moving, evac)
        tapoff = SC_CZW if zhalf else SC_CXW
        boff = SC_CZB if zhalf else SC_CXB
        kb.wait("ve", "te")
        for c in range(8):
            def f(v, c=c):
                xc = a1[:, c * NL:(c + 1) * NL]
                v.tensor_tensor(xc[:, 0:4], xc[:, 0:4], rrep[:, 0:4], ALU.mult)
                v.tensor_tensor(xc[:, NL - 2:NL], xc[:, NL - 2:NL],
                                rrep[:, NL - 2:NL], ALU.mult)
                v.tensor_scalar(mrep[:, 0:2051], xc[:, 0:2051],
                                sc[:, tapoff + 4 * c:tapoff + 4 * c + 1], None, ALU.mult)
                for j in range(1, 4):
                    v.scalar_tensor_tensor(mrep[:, 0:2051], xc[:, j:j + 2051],
                                           sc[:, tapoff + 4 * c + j:tapoff + 4 * c + j + 1],
                                           mrep[:, 0:2051], ALU.mult, ALU.add)
                kb.incl(v, "ve")
            kb.ve(f)
            vev = kb.n["ve"]
            def f(a, c=c, vev=vev):
                a.wait_ge(kb.sem("ve"), vev)
                a.activation(a2[:, c * NL:c * NL + 2051], mrep[:, 0:2051], AF.Silu,
                             bias=sc[:, boff + c:boff + c + 1])
                kb.incl(a, "ae")
            kb.ae(f)
            kb.wait("ve", "ae")
        if zhalf:
            kb.wait("ae", "ae")
            def f(a):
                for c in range(8):
                    kb.inc(a.dma_start(out=zs_hbm[128 * c:128 * (c + 1), :],
                                       in_=a2[:, c * NL:c * NL + 2051]), "qa", 16)
            kb.ae(f)

    conv_half(zhalf=True)
    d = dbg_out("zs", [D, 2051], BF16)
    if d is not None:
        kb.wait("ae", "ve", "ae")
        def f(a, d=d):
            for c in range(8):
                kb.inc(a.dma_start(out=d[128 * c:128 * (c + 1), :],
                                   in_=a2[:, c * NL:c * NL + 2051]), "qa", 16)
            a.wait_ge(kb.sem("qa"), kb.n["qa"])
        kb.ae(f)
    kb.wait("ae", "qa")
    conv_half(zhalf=False)
    kb.wait("ve", "ae", "qa")
    def f(v):
        v.tensor_copy(xln[:, 0:8 * NL], a2[:, 0:8 * NL])
        kb.incl(v, "ve")
    kb.ve(f)

    d = dbg_out("xs", [D, 2051], BF16)
    if d is not None:
        kb.wait("ae", "ve")
        def f(a, d=d):
            for c in range(8):
                kb.inc(a.dma_start(out=d[128 * c:128 * (c + 1), :],
                                   in_=xln[:, c * NL:c * NL + 2051]), "qa", 16)
            a.wait_ge(kb.sem("qa"), kb.n["qa"])
        kb.ae(f)

    XSOFF = 3

    def xs_moving(k, n0, nw):
        return xln[:, k * NL + XSOFF + n0:k * NL + XSOFF + n0 + nw]

    def emit_qk(which):
        w_ap = w["L0_attn_W_q"] if which == "q" else w["L0_attn_W_k"]
        row0 = 0 if which == "q" else 768
        kb.wait("ve", "qa")

        def evac(col, pp, n0, nw, tem):
            def f(v, col=col, pp=pp, n0=n0, nw=nw, tem=tem):
                v.wait_ge(kb.sem(tem[0]), tem[1])
                v.tensor_copy(a1[:, col * NL + n0:col * NL + n0 + nw], pp[0:128, 0:nw])
                kb.incl(v, "ve")
            kb.ve(f)
            return ("ve", kb.n["ve"])

        emit_proj(w_ap, 8, 6, SPAN, xs_moving, evac)
        kb.wait("ae", "ve")
        def f(a):
            for c in range(6):
                a.activation(a2[:, c * NL:c * NL + SPAN],
                             a1[:, c * NL:c * NL + SPAN], AF.Square)
            kb.incl(a, "ae")
        kb.ae(f)
        seg = {0: [(0, CB_FULL), (1, CB_TOP)], 1: [(1, CB_BOT), (2, CB_FULL)],
               2: [(3, CB_FULL), (4, CB_TOP)], 3: [(4, CB_BOT), (5, CB_FULL)]}
        kb.wait("te", "ae")
        for h in range(4):
            for ti, (n0, nw) in enumerate(_ntiles(SPAN, 512)):
                def f(t, h=h, n0=n0, nw=nw):
                    for si, (c, p) in enumerate(seg[h]):
                        t.matmul(P4[0:1, 0:nw], cbb[:, p:p + 1],
                                 a2[:, c * NL + n0:c * NL + n0 + nw],
                                 start=(si == 0), stop=(si == 1))
                    kb.incl(t, "te")
                kb.te(f)
                tev = kb.n["te"]
                def f(v, n0=n0, nw=nw, tev=tev):
                    v.wait_ge(kb.sem("te"), tev)
                    v.tensor_scalar(sc[0:1, SC_NRM + n0:SC_NRM + n0 + nw],
                                    P4[0:1, 0:nw], 1e-6, None, ALU.add)
                    kb.incl(v, "ve")
                kb.ve(f)
                kb.wait("te", "ve")
            kb.wait("ae", "ve")
            def f(a):
                a.activation(sc[0:1, SC_NRM:SC_NRM + SPAN],
                             sc[0:1, SC_NRM:SC_NRM + SPAN], AF.Sqrt)
                kb.incl(a, "ae")
            kb.ae(f)
            kb.wait("ve", "ae")
            def f(v):
                v.reciprocal(sc[0:1, SC_NRM:SC_NRM + SPAN], sc[0:1, SC_NRM:SC_NRM + SPAN])
                kb.incl(v, "ve")
            kb.ve(f)
            kb.wait("ae", "ve")
            def f(a, h=h):
                kb.inc(a.dma_start(out=rows_hbm[h:h + 1, 0:SPAN],
                                   in_=sc[0:1, SC_NRM:SC_NRM + SPAN]), "qnr", 16)
            kb.ae(f)
            kb.wait("se", "qnr")
            def f(s, h=h):
                kb.inc(s.dma_start(out=mrep[:, 0:SPAN],
                                   in_=rows_hbm[h:h + 1, 0:SPAN].broadcast_to((128, SPAN))),
                       "qnb", 16)
            kb.se(f)
            kb.wait("ve", "qnb")
            for (c, p) in seg[h]:
                pr = (slice(0, 128) if p == CB_FULL else
                      (slice(0, 64) if p == CB_TOP else slice(64, 128)))
                def f(v, c=c, pr=pr):
                    v.tensor_tensor(a1[pr, c * NL:c * NL + SPAN],
                                    a1[pr, c * NL:c * NL + SPAN],
                                    mrep[pr, 0:SPAN], ALU.mult)
                    kb.incl(v, "ve")
                kb.ve(f)
            kb.wait("se", "ve")
        kb.wait("ae", "ve")
        def f(a, row0=row0):
            for c in range(6):
                kb.inc(a.dma_start(out=ag1b_i[row0 + 128 * c:row0 + 128 * (c + 1), :],
                                   in_=a1[:, c * NL:c * NL + SPAN]), "qa", 16)
        kb.ae(f)

    emit_qk("q")
    emit_qk("k")

    def emit_v():
        def evac(col, pp, n0, nw, tem):
            sl = (col * 4 + n0 // 512) % 2
            kb.wait("ve", "qa")
            def f(v, pp=pp, nw=nw, tem=tem, sl=sl):
                v.wait_ge(kb.sem(tem[0]), tem[1])
                v.tensor_copy(scb[:, 1024 * sl:1024 * sl + nw], pp[0:128, 0:nw])
                kb.incl(v, "ve")
            kb.ve(f)
            vev = kb.n["ve"]
            def f(a, col=col, n0=n0, nw=nw, vev=vev, sl=sl):
                a.wait_ge(kb.sem("ve"), vev)
                kb.inc(a.dma_start(
                    out=ag1b_i[1536 + 128 * col:1536 + 128 * col + 128, n0:n0 + nw],
                    in_=scb[:, 1024 * sl:1024 * sl + nw]), "qa", 16)
            kb.ae(f)
            return ("ve", vev)
        emit_proj(w["L0_attn_W_v"], 8, 12, SPAN, xs_moving, evac)

    emit_v()

    def emit_gkbeta():
        kb.wait("se", "ve", "qa")
        def f(s):
            for k in range(8):
                kb.inc(s.dma_start(out=wst[:, 4 * k:4 * k + 4],
                                   in_=w["L0_attn_W_gk"][128 * k:128 * k + 128, :]),
                       "qs", 16)
                kb.inc(s.dma_start(out=wst[:, 32 + 4 * k:32 + 4 * k + 4],
                                   in_=w["L0_attn_W_b"][128 * k:128 * k + 128, :]),
                       "qs", 16)
        kb.se(f)
        kb.wait("ve", "qs")
        def f(v):
            v.tensor_copy(scb[:, 0:64], wst[:, 0:64])
            kb.incl(v, "ve")
        kb.ve(f)
        kb.wait("te", "ve")
        for ti, (n0, nw) in enumerate(_ntiles(SPAN, 512)):
            def f(t, n0=n0, nw=nw):
                for k in range(8):
                    t.matmul(P0[0:4, 0:nw], scb[:, 4 * k:4 * k + 4],
                             xs_moving(k, n0, nw), start=(k == 0), stop=(k == 7))
                for k in range(8):
                    t.matmul(P1[0:4, 0:nw], scb[:, 32 + 4 * k:32 + 4 * k + 4],
                             xs_moving(k, n0, nw), start=(k == 0), stop=(k == 7))
                kb.incl(t, "te")
            kb.te(f)
            tev = kb.n["te"]
            def f(a, n0=n0, nw=nw, tev=tev):
                a.wait_ge(kb.sem("te"), tev)
                a.activation(sc[0:4, SC_NRM + n0:SC_NRM + n0 + nw], P0[0:4, 0:nw],
                             AF.Identity, bias=sc[0:4, SC_DTB:SC_DTB + 1])
                a.activation(sc[32:36, SC_NRM + n0:SC_NRM + n0 + nw], P1[0:4, 0:nw],
                             AF.Sigmoid, bias=sc[0:4, SC_BB:SC_BB + 1])
                kb.incl(a, "ae")
            kb.ae(f)
            kb.wait("te", "ae")
        kb.wait("ve", "ae")
        def f(v):
            v.tensor_relu(mrep[0:4, 0:SPAN], sc[0:4, SC_NRM:SC_NRM + SPAN])
            kb.incl(v, "ve")
        kb.ve(f)
        kb.wait("ae", "ve")
        def f(a):
            u = sc[0:4, SC_NRM:SC_NRM + SPAN]
            a.activation(u, u, AF.Abs)
            a.activation(u, u, AF.Exp, scale=-1.0)
            a.activation(u, u, AF.Ln, bias=1.0)
            kb.incl(a, "ae")
        kb.ae(f)
        kb.wait("ve", "ae")
        def f(v):
            u = sc[0:4, SC_NRM:SC_NRM + SPAN]
            v.tensor_tensor(u, u, mrep[0:4, 0:SPAN], ALU.add)
            v.tensor_scalar(u, u, sc[0:4, SC_EAL:SC_EAL + 1], -1.0, ALU.mult, ALU.mult)
            kb.incl(v, "ve")
        kb.ve(f)
        kb.wait("ae", "ve")
        def f(a):
            kb.inc(a.dma_start(out=ag1f_i[0:4, :], in_=sc[0:4, SC_NRM:SC_NRM + SPAN]),
                   "qa", 16)
            kb.inc(a.dma_start(out=ag1f_i[4:8, :], in_=sc[32:36, SC_NRM:SC_NRM + SPAN]),
                   "qa", 16)
        kb.ae(f)

    emit_gkbeta()

    def emit_g():
        def evac(col, pp, n0, nw, tem):
            sl = (col * 5 + n0 // 512) % 2
            kb.wait("ve", "qa")
            def f(v, pp=pp, n0=n0, nw=nw, tem=tem, sl=sl):
                v.wait_ge(kb.sem(tem[0]), tem[1])
                v.tensor_copy(scb[:, 1024 * sl:1024 * sl + nw], pp[0:128, 0:nw])
                kb.incl(v, "ve")
            kb.ve(f)
            vev = kb.n["ve"]
            def f(a, col=col, n0=n0, nw=nw, vev=vev, sl=sl):
                a.wait_ge(kb.sem("ve"), vev)
                kb.inc(a.dma_start(out=g_hbm[128 * col:128 * col + 128, n0:n0 + nw],
                                   in_=scb[:, 1024 * sl:1024 * sl + nw]), "qa", 16)
            kb.ae(f)
            return ("ve", vev)

        def g_moving(k, n0, nw):
            return xln[:, k * NL + n0:k * NL + n0 + nw]

        emit_proj(w["L0_attn_W_g"], 8, 12, 2051, g_moving, evac)

    emit_g()

    d = dbg_out("ag1b", [3072, SPAN], BF16)
    if d is not None:
        kb.wait("ae", "qa", "ve")
        def f(a, d=d):
            kb.inc(a.dma_start(out=d, in_=ag1b_i), "qa", 16)
            a.wait_ge(kb.sem("qa"), kb.n["qa"])
        kb.ae(f)
    d = dbg_out("ag1f", [8, SPAN], F32)
    if d is not None:
        kb.wait("ae", "qa", "ve")
        def f(a, d=d):
            kb.inc(a.dma_start(out=d, in_=ag1f_i), "qa", 16)
            a.wait_ge(kb.sem("qa"), kb.n["qa"])
        kb.ae(f)

    kb.wait("ge", "qa", "qs", "ve", "ae", "te")
    def f(g):
        kb.inc(g.collective_compute("AllGather", ALU.bypass,
                                    replica_groups=[list(range(8))],
                                    ins=[ag1b_i.opt()], outs=[ag1b_o.opt()]), "cc")
        kb.inc(g.collective_compute("AllGather", ALU.bypass,
                                    replica_groups=[list(range(8))],
                                    ins=[ag1f_i.opt()], outs=[ag1f_o.opt()]), "cc")
    kb.ge(f)

    GX_GB = 0
    GX_SLAB = 4
    GX_OC = 32
    GX_OH = 44
    GX_L1 = 56
    GX_B1 = 80
    GX_OE = 84

    A1_QH, A1_QL, A1_KH, A1_KL = 0, 2048, 4096, 6144
    A1_V0, A1_V1, A1_V2 = 8192, 10240, 12288
    A1_MT, A1_DT, A1_NT, A1_DD = 14336, 14464, 14592, 14720
    A1_GT, A1_PW, A1_PWT, A1_TF = 14848, 14976, 15104, 15232
    A1_TT, A1_LT, A1_TS = 15360, 15488, 15616
    A1_GB = 16000
    A2_KTM, A2_VTM, A2_VT, A2_KBAR, A2_KTIL = 0, 192, 576, 960, 1152
    A2_QBH, A2_QBL, A2_WHI, A2_WLO = 1344, 1472, 1600, 1728
    A2_EMT, A2_EDT, A2_ENT, A2_ELT, A2_ES = 1856, 1984, 2112, 2240, 2368
    A2_SBH, A2_SBL, A2_OST = 2496, 2880, 3264
    SD_C, SD_CST, SD_CEND, SD_TIL, SD_BET, SD_BEC = 0, 16, 32, 48, 64, 80

    def delta_rowprep_L0(j):
        kb.sync_step("ge")
        def f(g):
            kb.inc(g.indirect_dma_start(
                out=mrep[0:64, 0:SPAN], out_offset=None, in_=ag1f_o,
                in_offset=bass.IndirectOffsetOnAxis(ap=gix[0:64, GX_GB + j:GX_GB + j + 1], axis=0)),
                "qgB", 16)
        kb.ge(f)
        kb.sync_step("ve")
        def f(v):
            v.tensor_tensor_scan(sc[0:1, SC_NRM:SC_NRM + SPAN], mrep[0:1, 0:SPAN],
                                 mrep[0:1, 0:SPAN], 0.0, ALU.add, ALU.bypass)
            v.tensor_copy(sc[0:1, SC_STAT:SC_STAT + 16],
                          bass.AP(sc.tensor if hasattr(sc, "tensor") else None, 0, []) if False else
                          sc[0:1, SC_NRM + 127:SC_NRM + SPAN].rearrange("a (b c) -> a (b c)") if False else
                          sc[0:1, SC_NRM:SC_NRM + SPAN][:, 127::128])
            v.memset(sc[0:1, SC_STAT + 16:SC_STAT + 17], 0.0)
            v.tensor_copy(sc[0:1, SC_STAT + 17:SC_STAT + 32],
                          sc[0:1, SC_STAT:SC_STAT + 15])
            v.tensor_copy(mrep[0:1, 0:SPAN], mrep[32:33, 0:SPAN])
            kb.incl(v, "ve")
        kb.ve(f)
        kb.sync_step("ae")
        def f(a):
            a.activation(mrep[0:1, 0:SPAN], mrep[0:1, 0:SPAN], AF.Ln)
            kb.incl(a, "ae")
        kb.ae(f)
        kb.sync_step("ve")
        def f(v):
            v.tensor_scalar(mrep[0:1, 0:SPAN], mrep[0:1, 0:SPAN], -1e4, None, ALU.max)
            v.tensor_tensor(mrep[0:1, 0:SPAN], mrep[0:1, 0:SPAN],
                            sc[0:1, SC_NRM:SC_NRM + SPAN], ALU.add)
            kb.incl(v, "ve")
        kb.ve(f)
        kb.sync_step("ae")
        def f(a):
            kb.inc(a.dma_start(out=rows_hbm[0:1, 0:SPAN], in_=sc[0:1, SC_NRM:SC_NRM + SPAN]), "qrw", 16)
            kb.inc(a.dma_start(out=rows_hbm[1:2, 0:SPAN], in_=mrep[0:1, 0:SPAN]), "qrw", 16)
            kb.inc(a.dma_start(out=rows_hbm[2:3, 0:16], in_=sc[0:1, SC_STAT:SC_STAT + 16]), "qrw", 16)
            kb.inc(a.dma_start(out=rows_hbm[3:4, 0:16], in_=sc[0:1, SC_STAT + 16:SC_STAT + 32]), "qrw", 16)
            kb.inc(a.dma_start(out=rows_hbm[4:5, 0:SPAN], in_=mrep[32:33, 0:SPAN]), "qrw", 16)
        kb.ae(f)
        kb.sync_step("se")
        def f(s):
            rh = rows_hbm.tensor
            LL = rows_hbm.shape[1] if hasattr(rows_hbm, "shape") else L
            kb.inc(s.dma_start(out=mrep[:, 0:SPAN],
                               in_=bass.AP(rh, 0, [[0, 128], [1, SPAN]])), "qbc", 16)
            kb.inc(s.dma_start(out=sc[:, SC_NRM:SC_NRM + SPAN],
                               in_=bass.AP(rh, L, [[0, 128], [1, SPAN]])), "qbc", 16)
            kb.inc(s.dma_start(out=wst[0:16, 1024:1152],
                               in_=bass.AP(rh, 0, [[128, 16], [1, 128]])), "qbc", 16)
            kb.inc(s.dma_start(out=sd[:, SD_CEND:SD_CEND + 16],
                               in_=bass.AP(rh, 2 * L, [[0, 128], [1, 16]])), "qbc", 16)
            kb.inc(s.dma_start(out=sd[:, SD_CST:SD_CST + 16],
                               in_=bass.AP(rh, 3 * L, [[0, 128], [1, 16]])), "qbc", 16)
            kb.inc(s.dma_start(out=wst[0:16, 1152:1280],
                               in_=bass.AP(rh, 4 * L, [[128, 16], [1, 128]])), "qbc", 16)
        kb.se(f)
        kb.sync_step("te")
        def f(t):
            t.transpose(P2[0:128, 0:16], wst[0:16, 1024:1152], identF[0:16, 0:16])
            t.transpose(P3[0:128, 0:16], wst[0:16, 1152:1280], identF[0:16, 0:16])
            kb.incl(t, "te")
        kb.te(f)
        kb.sync_step("ve")
        def f(v):
            v.tensor_copy(sd[:, SD_C:SD_C + 16], P2[0:128, 0:16])
            v.tensor_copy(sd[:, SD_BET:SD_BET + 16], P3[0:128, 0:16])
            kb.incl(v, "ve")
        kb.ve(f)
        kb.sync_step("ve")
        def f(v):
            v.tensor_tensor(sd[:, SD_TIL:SD_TIL + 16], sd[:, SD_CEND:SD_CEND + 16],
                            sd[:, SD_C:SD_C + 16], ALU.subtract)
            v.tensor_tensor(sd[:, SD_BEC:SD_BEC + 16], sd[:, SD_C:SD_C + 16],
                            sd[:, SD_CST:SD_CST + 16], ALU.subtract)
            kb.incl(v, "ve")
        kb.ve(f)
        kb.sync_step("ae")
        def f(a):
            a.activation(sd[:, SD_TIL:SD_TIL + 16], sd[:, SD_TIL:SD_TIL + 16], AF.Exp)
            a.activation(sd[:, SD_BEC:SD_BEC + 16], sd[:, SD_BEC:SD_BEC + 16], AF.Exp)
            kb.incl(a, "ae")
        kb.ae(f)
        kb.sync_step("ve")
        def f(v):
            v.tensor_tensor(sd[:, SD_CEND:SD_CEND + 16], sd[:, SD_CEND:SD_CEND + 16],
                            sd[:, SD_CST:SD_CST + 16], ALU.subtract)
            v.tensor_tensor(sd[:, SD_BEC:SD_BEC + 16], sd[:, SD_BEC:SD_BEC + 16],
                            sd[:, SD_BET:SD_BET + 16], ALU.mult)
            kb.incl(v, "ve")
        kb.ve(f)
        kb.sync_step("ae")
        def f(a):
            a.activation(sd[:, SD_CEND:SD_CEND + 16], sd[:, SD_CEND:SD_CEND + 16], AF.Exp)
            kb.incl(a, "ae")
        kb.ae(f)

    def delta_slabs_L0(j):
        kb.sync_step("ge")
        def f(g):
            for t, (col, n_p) in enumerate([(A1_QH, 128), (A1_QL, 64), (A1_KH, 128),
                                            (A1_KL, 64), (A1_V0, 128), (A1_V1, 128),
                                            (A1_V2, 128)]):
                kb.inc(g.indirect_dma_start(
                    out=a1[0:n_p, col:col + SPAN], out_offset=None, in_=ag1b_o,
                    in_offset=bass.IndirectOffsetOnAxis(
                        ap=gix[0:n_p, GX_SLAB + 7 * j + t:GX_SLAB + 7 * j + t + 1], axis=0)),
                    "qgB", 16)
        kb.ge(f)

    def delta_chunk_L0(j, c):
        tok0 = 128 * c
        cS = lambda off: sd[:, off + c:off + c + 1]
        kb.sync_step("se")
        def f(s):
            kb.inc(s.dma_start(out=a2[:, A2_KTM:A2_KTM + 128],
                               in_=a1[:, A1_KH + tok0:A1_KH + tok0 + 128], transpose=True), "qt0", 16)
            kb.inc(s.dma_start(out=a2[:, A2_KTM + 128:A2_KTM + 192],
                               in_=a1[0:64, A1_KL + tok0:A1_KL + tok0 + 128], transpose=True), "qt0", 16)
            for t, col in enumerate((A1_V0, A1_V1, A1_V2)):
                kb.inc(s.dma_start(out=a2[:, A2_VTM + 128 * t:A2_VTM + 128 * (t + 1)],
                                   in_=a1[:, col + tok0:col + tok0 + 128], transpose=True), "qt0", 16)
        kb.se(f)
        kb.sync_step("te")
        def f(t):
            t.matmul(P2[0:128, 0:128], a1[:, A1_KH + tok0:A1_KH + tok0 + 128],
                     a1[:, A1_KH + tok0:A1_KH + tok0 + 128], start=True, stop=False)
            t.matmul(P2[0:128, 0:128], a1[0:64, A1_KL + tok0:A1_KL + tok0 + 128],
                     a1[0:64, A1_KL + tok0:A1_KL + tok0 + 128], start=False, stop=True)
            t.matmul(P3[0:128, 0:128], a1[:, A1_KH + tok0:A1_KH + tok0 + 128],
                     a1[:, A1_QH + tok0:A1_QH + tok0 + 128], start=True, stop=False)
            t.matmul(P3[0:128, 0:128], a1[0:64, A1_KL + tok0:A1_KL + tok0 + 128],
                     a1[0:64, A1_QL + tok0:A1_QL + tok0 + 128], start=False, stop=True)
            kb.incl(t, "te")
        kb.te(f)
        kb.sync_step("ve")
        def f(v):
            cln = sc[:, SC_NRM + tok0:SC_NRM + tok0 + 128]
            crp = mrep[:, tok0:tok0 + 128]
            v.scalar_tensor_tensor(wst[:, 0:128], crp, cS(SD_C), cfm(CF_MB_UP_INCL),
                                   ALU.subtract, ALU.add)
            v.scalar_tensor_tensor(wst[:, 128:256], cln, cS(SD_C), cfm(CF_MB_BD),
                                   ALU.subtract, ALU.add)
            v.scalar_tensor_tensor(wst[:, 1280:1408], cln, cS(SD_C), cfm(CF_MB_BELOW),
                                   ALU.subtract, ALU.add)
            v.tensor_scalar(wst[:, 1408:1536], crp, cS(SD_CST),
                            None, ALU.subtract)
            kb.incl(v, "ve")
        kb.ve(f)
        kb.sync_step("ae")
        def f(a):
            a.activation(a2[:, A2_ELT:A2_ELT + 128], wst[:, 0:128], AF.Exp)
            a.activation(a2[:, A2_EDT:A2_EDT + 128], wst[:, 128:256], AF.Exp)
            a.activation(a2[:, A2_ENT:A2_ENT + 128], wst[:, 1280:1408], AF.Exp)
            a.activation(a2[:, A2_ES:A2_ES + 128], wst[:, 1408:1536], AF.Exp)
            kb.incl(a, "ae")
        kb.ae(f)
        kb.sync_step("ve")
        def f(v):
            v.tensor_tensor(a1[:, A1_DT:A1_DT + 128], P2[0:128, 0:128],
                            a2[:, A2_EDT:A2_EDT + 128], ALU.mult)
            v.tensor_tensor(a1[:, A1_NT:A1_NT + 128], P2[0:128, 0:128],
                            a2[:, A2_ENT:A2_ENT + 128], ALU.mult)
            v.tensor_tensor(a1[:, A1_LT:A1_LT + 128], P3[0:128, 0:128],
                            a2[:, A2_ELT:A2_ELT + 128], ALU.mult)
            v.transpose(a1[:, A1_DD:A1_DD + 128], a1[:, A1_DT:A1_DT + 128])
            v.scalar_tensor_tensor(a1[:, A1_GT:A1_GT + 128], a1[:, A1_DT:A1_DT + 128],
                                   -1.0, identB, ALU.mult, ALU.add)
            kb.incl(v, "ve")
        kb.ve(f)
        kb.sync_step("te")
        def f(t):
            t.matmul(P4[0:128, 0:128], a1[:, A1_DD:A1_DD + 128],
                     a1[:, A1_DT:A1_DT + 128], start=True, stop=True)
            kb.incl(t, "te")
        kb.te(f)
        for lvl in range(4):
            kb.sync_step("ve")
            def f(v):
                v.tensor_copy(a1[:, A1_PW:A1_PW + 128], P4[0:128, 0:128])
                v.transpose(a1[:, A1_PWT:A1_PWT + 128], a1[:, A1_PW:A1_PW + 128])
                kb.incl(v, "ve")
            kb.ve(f)
            kb.sync_step("te")
            def f(t, lvl=lvl):
                t.matmul(P5[0:128, 0:128], a1[:, A1_PWT:A1_PWT + 128],
                         a1[:, A1_GT:A1_GT + 128], start=True, stop=True)
                if lvl < 3:
                    t.matmul(P4[0:128, 0:128], a1[:, A1_PWT:A1_PWT + 128],
                             a1[:, A1_PW:A1_PW + 128], start=True, stop=True)
                kb.incl(t, "te")
            kb.te(f)
            kb.sync_step("ve")
            def f(v, lvl=lvl):
                v.tensor_tensor(a1[:, A1_GT:A1_GT + 128], a1[:, A1_GT:A1_GT + 128],
                                P5[0:128, 0:128], ALU.add)
                if lvl == 3:
                    v.transpose(a1[:, A1_GB:A1_GB + 128], a1[:, A1_GT:A1_GT + 128])
                    v.tensor_copy(a1[:, A1_TF:A1_TF + 128], a1[:, A1_GB:A1_GB + 128])
                kb.incl(v, "ve")
            kb.ve(f)
        for it in range(3):
            kb.sync_step("te")
            def f(t):
                t.matmul(P4[0:128, 0:128], a1[:, A1_NT:A1_NT + 128],
                         a1[:, A1_TF:A1_TF + 128], start=True, stop=True)
                kb.incl(t, "te")
            kb.te(f)
            kb.sync_step("ve")
            def f(v):
                v.tensor_copy(a1[0:128, A1_TS:A1_TS + 128], P4[0:128, 0:128])
                kb.incl(v, "ve")
            kb.ve(f)
            kb.sync_step("te")
            def f(t):
                t.matmul(P5[0:128, 0:128], a1[:, A1_GT:A1_GT + 128],
                         a1[:, A1_TS:A1_TS + 128], start=True, stop=True)
                kb.incl(t, "te")
            kb.te(f)
            kb.sync_step("ve")
            def f(v):
                v.tensor_tensor(a1[:, A1_TF:A1_TF + 128], a1[:, A1_GB:A1_GB + 128],
                                P5[0:128, 0:128], ALU.subtract)
                kb.incl(v, "ve")
            kb.ve(f)
        kb.sync_step("te")
        def f(t):
            t.transpose(PT[:, 0:128], a1[:, A1_TF:A1_TF + 128], identB)
            kb.incl(t, "te")
        kb.te(f)
        kb.sync_step("ve")
        def f(v):
            v.tensor_copy(a1[:, A1_TT:A1_TT + 128], PT[:, 0:128])
            v.tensor_scalar(a2[:, A2_VT:A2_VT + 384], a2[:, A2_VTM:A2_VTM + 384],
                            cS(SD_BET), None, ALU.mult)
            v.tensor_scalar(a2[:, A2_KBAR:A2_KBAR + 192], a2[:, A2_KTM:A2_KTM + 192],
                            cS(SD_BEC), None, ALU.mult)
            v.tensor_scalar(a2[:, A2_KTIL:A2_KTIL + 192], a2[:, A2_KTM:A2_KTM + 192],
                            cS(SD_TIL), None, ALU.mult)
            v.tensor_tensor(a2[:, A2_QBH:A2_QBH + 128], a1[:, A1_QH + tok0:A1_QH + tok0 + 128],
                            a2[:, A2_ES:A2_ES + 128], ALU.mult)
            v.tensor_tensor(a2[0:64, A2_QBL:A2_QBL + 128], a1[0:64, A1_QL + tok0:A1_QL + tok0 + 128],
                            a2[0:64, A2_ES:A2_ES + 128], ALU.mult)
            kb.incl(v, "ve")
        kb.ve(f)
        kb.sync_step("te")
        def f(t):
            t.matmul(P4[0:128, 0:384], a1[:, A1_TT:A1_TT + 128],
                     a2[:, A2_VT:A2_VT + 384], start=True, stop=True)
            t.matmul(P5[0:128, 0:128], a2[:, A2_KBAR:A2_KBAR + 128],
                     a1[:, A1_TT:A1_TT + 128], start=True, stop=True)
            t.matmul(P5[0:64, 128:256], a2[:, A2_KBAR + 128:A2_KBAR + 192],
                     a1[:, A1_TT:A1_TT + 128], start=True, stop=True)
            kb.incl(t, "te")
        kb.te(f)
        kb.sync_step("ve")
        def f(v):
            v.tensor_copy(a1[:, A1_TS:A1_TS + 384] if False else a2[:, A2_WHI:A2_WHI + 128],
                          P5[0:128, 0:128])
            v.tensor_copy(a2[0:64, A2_WLO:A2_WLO + 128], P5[0:64, 128:256])
            kb.incl(v, "ve")
        kb.ve(f)
        kb.sync_step("ve")
        def f(v):
            v.tensor_copy(a1[:, A1_TS:A1_TS + 384], P4[0:128, 0:384])
            kb.incl(v, "ve")
        kb.ve(f)
        kb.sync_step("te")
        def f(t):
            t.matmul(P4[0:128, 0:384], a2[:, A2_WHI:A2_WHI + 128],
                     a2[:, A2_SBH:A2_SBH + 384], start=True, stop=False)
            t.matmul(P4[0:128, 0:384], a2[0:64, A2_WLO:A2_WLO + 128],
                     a2[0:64, A2_SBL:A2_SBL + 384], start=False, stop=True)
            kb.incl(t, "te")
        kb.te(f)
        kb.sync_step("ve")
        def f(v):
            v.tensor_tensor(a2[:, A2_VT:A2_VT + 384], a1[:, A1_TS:A1_TS + 384],
                            P4[0:128, 0:384], ALU.subtract)
            kb.incl(v, "ve")
        kb.ve(f)
        kb.sync_step("te")
        def f(t):
            for ds, pp in enumerate((P6, P0, P1)):
                t.matmul(pp[0:128, 0:128], a2[:, A2_SBH + 128 * ds:A2_SBH + 128 * (ds + 1)],
                         a2[:, A2_QBH:A2_QBH + 128], start=True, stop=False)
                t.matmul(pp[0:128, 0:128], a2[0:64, A2_SBL + 128 * ds:A2_SBL + 128 * (ds + 1)],
                         a2[0:64, A2_QBL:A2_QBL + 128], start=False, stop=False)
                t.matmul(pp[0:128, 0:128], a2[:, A2_VT + 128 * ds:A2_VT + 128 * (ds + 1)],
                         a1[:, A1_LT:A1_LT + 128], start=False, stop=True)
            kb.incl(t, "te")
        kb.te(f)
        kb.sync_step("ve")
        def f(v):
            for ds, pp in enumerate((P6, P0, P1)):
                v.tensor_copy(a2[:, A2_OST + 128 * ds:A2_OST + 128 * (ds + 1)], pp[0:128, 0:128])
            kb.incl(v, "ve")
        kb.ve(f)
        kb.sync_step("ae")
        def f(a):
            for ds in range(3):
                kb.inc(a.dma_start(out=ag2_i[128 * ds:128 * (ds + 1),
                                             SPAN * j + tok0:SPAN * j + tok0 + 128],
                                   in_=a2[:, A2_OST + 128 * ds:A2_OST + 128 * (ds + 1)]),
                       "qo", 16)
        kb.ae(f)
        kb.sync_step("te")
        def f(t):
            t.matmul(P4[0:128, 0:384], a2[:, A2_KTIL:A2_KTIL + 128],
                     a2[:, A2_VT:A2_VT + 384], start=True, stop=True)
            t.matmul(P5[0:64, 0:384], a2[:, A2_KTIL + 128:A2_KTIL + 192],
                     a2[:, A2_VT:A2_VT + 384], start=True, stop=True)
            kb.incl(t, "te")
        kb.te(f)
        kb.sync_step("ve")
        def f(v):
            v.scalar_tensor_tensor(wst[:, 256:640], wst[:, 256:640], cS(SD_CEND),
                                   P4[0:128, 0:384], ALU.mult, ALU.add)
            v.scalar_tensor_tensor(wst[0:64, 640:1024], wst[0:64, 640:1024],
                                   sd[0:64, SD_CEND + c:SD_CEND + c + 1],
                                   P5[0:64, 0:384], ALU.mult, ALU.add)
            v.tensor_copy(a2[:, A2_SBH:A2_SBH + 384], wst[:, 256:640])
            v.tensor_copy(a2[0:64, A2_SBL:A2_SBL + 384], wst[0:64, 640:1024])
            kb.incl(v, "ve")
        kb.ve(f)

    def emit_delta_L0():
        kb.sync_step("ve")
        def f(v):
            v.memset(wst[:, 256:640], 0.0)
            v.memset(wst[0:64, 640:1024], 0.0)
            v.memset(a2[:, A2_SBH:A2_SBH + 384], 0.0)
            v.memset(a2[0:64, A2_SBL:A2_SBL + 384], 0.0)
            kb.incl(v, "ve")
        kb.ve(f)
        for j in range(4):
            delta_slabs_L0(j)
            delta_rowprep_L0(j)
            for c in range(16):
                delta_chunk_L0(j, c)

    kb.wait("ge", "cc")
    emit_delta_L0()

    d = dbg_out("oft", [DV0, L], BF16)
    if d is not None:
        kb.wait("ae", "qo", "ve", "te")
        def f(a, d=d):
            kb.inc(a.dma_start(out=d, in_=ag2_i), "qa", 16)
            a.wait_ge(kb.sem("qa"), kb.n["qa"])
        kb.ae(f)

    kb.wait("ge", "qo", "ve", "ae", "te")
    def f(g):
        kb.inc(g.collective_compute("AllGather", ALU.bypass,
                                    replica_groups=[list(range(8))],
                                    ins=[ag2_i.opt()], outs=[ag2_o.opt()]), "cc")
    kb.ge(f)

    kb.wait("ge", "cc")

    def ogch(i):
        return (a1, i * NL) if i < 8 else (a2, (i - 8) * NL)

    def emit_oread_L0():
        kb.sync_step("ge")
        def f(g):
            inmain = bass.AP(ag2_o.tensor, 0, [[2048, 4 * 3072], [1, 2048]])
            for cc in range(12):
                tns, off = ogch(cc)
                kb.inc(g.indirect_dma_start(
                    out=tns[:, off + 3:off + 2051], out_offset=None, in_=inmain,
                    in_offset=bass.IndirectOffsetOnAxis(
                        ap=gix[:, GX_OC + cc:GX_OC + cc + 1], axis=0)), "qgB", 16)
            inhalo = bass.AP(ag2_o.tensor, 0, [[64, 128 * 3072], [1, 64]])
            for cc in range(12):
                kb.inc(g.indirect_dma_start(
                    out=scb[:, 64 * cc:64 * cc + 64], out_offset=None, in_=inhalo,
                    in_offset=bass.IndirectOffsetOnAxis(
                        ap=gix[:, GX_OH + cc:GX_OH + cc + 1], axis=0)), "qgB", 16)
        kb.ge(f)
        kb.sync_step("ve")
        def f(v):
            for cc in range(12):
                tns, off = ogch(cc)
                v.tensor_copy(tns[:, off:off + 3], scb[:, 64 * cc + 61:64 * cc + 64])
            kb.incl(v, "ve")
        kb.ve(f)

    def emit_onorm_gate_L0():
        tiles = _ntiles(2051, 512)
        for hh in range(4):
            for ti, (n0, nw) in enumerate(tiles):
                kb.sync_step("ae")
                def f(a, hh=hh, n0=n0, nw=nw):
                    for ds in range(3):
                        tns, off = ogch(3 * hh + ds)
                        a.activation(scb[0:128, 512 * ds:512 * ds + nw],
                                     tns[:, off + n0:off + n0 + nw], AF.Square)
                    kb.incl(a, "ae")
                kb.ae(f)
                kb.sync_step("te")
                def f(t, n0=n0, nw=nw):
                    for ds in range(3):
                        t.matmul(P4[0:1, 0:nw], cbb[:, CB_ONES:CB_ONES + 1],
                                 scb[0:128, 512 * ds:512 * ds + nw],
                                 start=(ds == 0), stop=(ds == 2))
                    kb.incl(t, "te")
                kb.te(f)
                kb.sync_step("ve")
                def f(v, n0=n0, nw=nw):
                    v.tensor_scalar(mrep[0:1, n0:n0 + nw], P4[0:1, 0:nw],
                                    1.0 / 384.0, 1e-5, ALU.mult, ALU.add)
                    kb.incl(v, "ve")
                kb.ve(f)
            kb.sync_step("ae")
            def f(a, hh=hh):
                kb.inc(a.dma_start(out=rows_hbm[8 + hh:9 + hh, 0:2051],
                                   in_=mrep[0:1, 0:2051]), "qrw", 16)
            kb.ae(f)
            kb.sync_step("ve")
        kb.sync_step("se")
        def f(s):
            kb.inc(s.dma_start(out=mrep[0:4, 0:2051], in_=rows_hbm[8:12, 0:2051]), "qbc", 16)
        kb.se(f)
        kb.sync_step("ae")
        def f(a):
            a.activation(mrep[0:4, 0:2051], mrep[0:4, 0:2051], AF.Sqrt)
            kb.incl(a, "ae")
        kb.ae(f)
        kb.sync_step("ve")
        def f(v):
            v.reciprocal(mrep[0:4, 0:2051], mrep[0:4, 0:2051])
            kb.incl(v, "ve")
        kb.ve(f)
        kb.sync_step("ae")
        def f(a):
            kb.inc(a.dma_start(out=rows_hbm[8:12, 0:2051], in_=mrep[0:4, 0:2051]), "qrw", 16)
        kb.ae(f)
        for hh in range(4):
            kb.sync_step("ge")
            def f(g, hh=hh):
                kb.inc(g.dma_start(out=rrep[:, 0:2051],
                                   in_=rows_hbm[8 + hh:9 + hh, 0:2051].broadcast_to((128, 2051))),
                       "qbg", 16)
            kb.ge(f)
            for ds in range(3):
                cc = 3 * hh + ds
                tns, off = ogch(cc)
                kb.sync_step("se")
                def f(s, cc=cc):
                    kb.inc(s.dma_start(out=scb[:, 0:2048],
                                       in_=g_hbm[128 * cc:128 * cc + 128, 0:2048]), "qt0", 16)
                    kb.inc(s.dma_start(out=rrep[:, 2051:2054],
                                       in_=g_hbm[128 * cc:128 * cc + 128, 2048:2051]), "qt0", 16)
                kb.se(f)
                kb.sync_step("ve")
                def f(v, cc=cc, tns=tns, off=off, ds=ds):
                    v.scalar_tensor_tensor(tns[:, off:off + 2051], tns[:, off:off + 2051],
                                           sc[:, SC_GNW + ds:SC_GNW + ds + 1],
                                           rrep[:, 0:2051], ALU.mult, ALU.mult)
                    kb.incl(v, "ve")
                kb.ve(f)
                kb.sync_step("ae")
                def f(a, cc=cc):
                    a.activation(scb[:, 0:2048], scb[:, 0:2048], AF.Silu)
                    a.activation(rrep[:, 2051:2054], rrep[:, 2051:2054], AF.Silu)
                    kb.incl(a, "ae")
                kb.ae(f)
                kb.sync_step("ve")
                def f(v, tns=tns, off=off):
                    v.tensor_tensor(tns[:, off:off + 2048], tns[:, off:off + 2048],
                                    scb[:, 0:2048], ALU.mult)
                    v.tensor_tensor(tns[:, off + 2048:off + 2051], tns[:, off + 2048:off + 2051],
                                    rrep[:, 2051:2054], ALU.mult)
                    kb.incl(v, "ve")
                kb.ve(f)

    def emit_wso_wout():
        def og_moving(k, n0, nw):
            tns, off = ogch(k)
            return tns[:, off + n0:off + n0 + nw]

        def evac_y(col, pp, n0, nw, tem):
            def f(v, col=col, pp=pp, n0=n0, nw=nw, tem=tem):
                v.wait_ge(kb.sem(tem[0]), tem[1])
                v.tensor_copy(xln[:, col * NL + n0:col * NL + n0 + nw], pp[0:128, 0:nw])
                kb.incl(v, "ve")
            kb.ve(f)
            return ("ve", kb.n["ve"])

        emit_proj(w["L0_attn_W_so"], 12, 8, 2051, og_moving, evac_y)
        kb.wait("se", "ve", "te")
        def f(s):
            for c in range(8):
                kb.inc(s.dma_start(out=a1[:, c * NL:c * NL + 2051],
                                   in_=zs_hbm[128 * c:128 * (c + 1), :]), "qt0", 16)
        kb.se(f)
        kb.wait("ve", "qt0")
        def yz_moving(k, n0, nw):
            if k < 8:
                return xln[:, k * NL + n0:k * NL + n0 + nw]
            return a1[:, (k - 8) * NL + n0:(k - 8) * NL + n0 + nw]

        def evac_res(col, pp, n0, nw, tem):
            def f(v, col=col, pp=pp, n0=n0, nw=nw, tem=tem):
                v.wait_ge(kb.sem(tem[0]), tem[1])
                v.tensor_tensor(res[:, col * NL + 1 + n0:col * NL + 1 + n0 + nw],
                                res[:, col * NL + 1 + n0:col * NL + 1 + n0 + nw],
                                pp[0:128, 0:nw], ALU.add)
                kb.incl(v, "ve")
            kb.ve(f)
            return ("ve", kb.n["ve"])

        emit_proj(w["L0_attn_W_out"], 16, 8, 2051, yz_moving, evac_res)

    def emit_routing(ncols, col0):
        kb.wait("ve", "qa", "qs")
        kb.wait("se", "ve", "te", "ae")
        def f(s):
            kb.inc(s.dma_start(out=sc[0:8, SC_NRM:SC_NRM + 32], in_=w["L1_ffn_tile_sigs"]), "qs", 16)
        kb.se(f)
        kb.wait("ae", "qs")
        def f(a):
            a.activation(sc[0:8, SC_NRM:SC_NRM + 32], sc[0:8, SC_NRM:SC_NRM + 32], AF.Sign)
            kb.incl(a, "ae")
        kb.ae(f)
        kb.wait("te", "ae")
        def f(t):
            t.transpose(P4[0:32, 0:8], sc[0:8, SC_NRM:SC_NRM + 32], identF[0:8, 0:8])
            kb.incl(t, "te")
        kb.te(f)
        kb.wait("ve", "te")
        def f(v):
            v.tensor_copy(scb[0:32, 0:8], P4[0:32, 0:8])
            kb.incl(v, "ve")
        kb.ve(f)
        kb.wait("ge", "ve")
        def f(g):
            kb.inc(g.dma_start(out=rrep[0:32, 0:ncols], in_=pe_in[:, col0:col0 + ncols]), "qg", 16)
        kb.ge(f)
        kb.wait("te", "ge", "qg", "ve")
        for ti, (t0, tw) in enumerate(_ntiles(ncols, 128)):
            def f(t, t0=t0, tw=tw):
                t.matmul(P4[0:tw, 0:8], rrep[0:32, t0:t0 + tw], scb[0:32, 0:8],
                         start=True, stop=True)
                kb.incl(t, "te")
            kb.te(f)
            kb.sync_step("ve")
            def f(v, tw=tw):
                v.reduce_max(sc[0:tw, SC_STAT:SC_STAT + 1], P4[0:tw, 0:8], axis=AX.X)
                v.tensor_scalar(sc[0:tw, SC_STAT + 1:SC_STAT + 9], P4[0:tw, 0:8],
                                sc[0:tw, SC_STAT:SC_STAT + 1], None, ALU.is_ge)
                kb.incl(v, "ve")
            kb.ve(f)
            kb.sync_step("te")
            def f(t, tw=tw):
                t.transpose(P5[0:8, 0:tw], sc[0:tw, SC_STAT + 1:SC_STAT + 9],
                            identF[0:tw, 0:tw])
                kb.incl(t, "te")
            kb.te(f)
            kb.sync_step("ve")
            def f(v, tw=tw):
                v.tensor_copy(sc[0:8, SC_NRM + 100:SC_NRM + 100 + tw], P5[0:8, 0:tw])
                kb.incl(v, "ve")
            kb.ve(f)
            kb.sync_step("ae")
            def f(a, t0=t0, tw=tw):
                kb.inc(a.dma_start(out=onehot_hbm[:, t0:t0 + tw],
                                   in_=sc[0:8, SC_NRM + 100:SC_NRM + 100 + tw]), "qrw", 16)
            kb.ae(f)

    def emit_ffn1(ncols, col0):
        emit_routing(ncols, col0)
        kb.wait("ge", "ve", "qrw")
        def f(g):
            kb.inc(g.dma_start(out=scb[0:8, 0:1024], in_=w["L1_ffn_b"]), "qg", 16)
        kb.ge(f)
        kb.wait("ge", "qrw", "ve", "te")
        def f(g):
            kb.inc(g.dma_start(out=a1[0:8, 0:ncols], in_=onehot_hbm[:, 0:ncols]), "qg", 16)
        kb.ge(f)
        tiles = _ntiles(ncols, 512)
        kb.wait("te", "qg", "ge")
        bias_free = {}
        for col in range(8):
            for ti, (n0, nw) in enumerate(tiles):
                gi = col * len(tiles) + ti
                pp = P0 if gi % 2 == 0 else P1
                if gi >= 2:
                    kb.waitm("te", [bias_free[gi - 2]])
                def f(t, col=col, n0=n0, nw=nw, pp=pp):
                    t.matmul(pp[0:128, 0:nw], scb[0:8, 128 * col:128 * col + 128],
                             a1[0:8, n0:n0 + nw], start=True, stop=True)
                    kb.incl(t, "te")
                kb.te(f)
                tev = kb.n["te"]
                def f(v, col=col, n0=n0, nw=nw, pp=pp, tev=tev):
                    v.wait_ge(kb.sem("te"), tev)
                    v.scalar_tensor_tensor(res[:, col * NL + col0 + n0:col * NL + col0 + n0 + nw],
                                           pp[0:128, 0:nw], 0.5,
                                           res[:, col * NL + col0 + n0:col * NL + col0 + n0 + nw],
                                           ALU.mult, ALU.add)
                    kb.incl(v, "ve")
                kb.ve(f)
                bias_free[gi] = ("ve", kb.n["ve"])
        for e in range(8):
            kb.wait("ge", "ve", "te")
            def f(g, e=e):
                kb.inc(g.dma_start(out=rrep[:, 0:ncols],
                                   in_=onehot_hbm[e:e + 1, 0:ncols].broadcast_to((128, ncols))),
                       "qbg", 16)
            kb.ge(f)
            kb.wait("ve", "qbg")
            def f(v):
                for c in range(8):
                    v.tensor_tensor(a2[:, c * NL:c * NL + ncols],
                                    xln[:, c * NL:c * NL + ncols], rrep[:, 0:ncols],
                                    ALU.mult)
                kb.incl(v, "ve")
            kb.ve(f)

            def xt_moving(k, n0, nw):
                return a2[:, k * NL + n0:k * NL + n0 + nw]

            def evac_acc(col, pp, n0, nw, tem):
                def f(v, col=col, pp=pp, n0=n0, nw=nw, tem=tem):
                    v.wait_ge(kb.sem(tem[0]), tem[1])
                    v.scalar_tensor_tensor(res[:, col * NL + col0 + n0:col * NL + col0 + n0 + nw],
                                           pp[0:128, 0:nw], 0.5,
                                           res[:, col * NL + col0 + n0:col * NL + col0 + n0 + nw],
                                           ALU.mult, ALU.add)
                    kb.incl(v, "ve")
                kb.ve(f)
                return ("ve", kb.n["ve"])

            emit_proj(bass.AP(wT_hbm.tensor, 1024 * 1024 * e, [[1024, 1024], [1, 1024]]),
                      8, 8, ncols, xt_moving, evac_acc, bf16src=True)

    def emit_wT_prep():
        for e in range(8):
            emit_trans_prep(
                bass.AP(w["L1_ffn_W"].tensor, 1024 * 1024 * e, [[1024, 1024], [1, 1024]]),
                bass.AP(wT_hbm.tensor, 1024 * 1024 * e, [[1024, 1024], [1, 1024]]),
                8, 8)

    def emit_l1_proj(which):
        w_ap = {"q": w["L1_attn_q_w"], "k": w["L1_attn_k_w"], "v": w["L1_attn_v_w"]}[which]
        kb.wait("ae", "qa")
        kb.wait("ve", "qa")
        tap = {"q": SC_CQW, "k": SC_CKW, "v": SC_CVW}[which]
        row0 = {"q": 0, "k": 1024, "v": 2048}[which]

        def evac(col, pp, n0, nw, tem):
            def f(v, col=col, pp=pp, n0=n0, nw=nw, tem=tem):
                v.wait_ge(kb.sem(tem[0]), tem[1])
                v.tensor_copy(a1[:, col * NL + n0:col * NL + n0 + nw], pp[0:128, 0:nw])
                kb.incl(v, "ve")
            kb.ve(f)
            return ("ve", kb.n["ve"])

        emit_proj(w_ap, 8, 8, 2051, xln_moving, evac)
        kb.wait("ge", "te", "ve")
        def f(g):
            kb.inc(g.dma_start(out=rrep[:, 0:NL], in_=hmask.broadcast_to((128, NL))), "qbg", 16)
        kb.ge(f)
        kb.wait("ve", "qbg")
        for c in range(8):
            def f(v, c=c):
                u = a1[:, c * NL:c * NL + 2051]
                v.tensor_tensor(u[:, 0:3], u[:, 0:3], rrep[:, 1:4], ALU.mult)
                v.tensor_scalar(mrep[:, 0:SPAN], u[:, 0:SPAN],
                                sc[:, tap + 4 * c:tap + 4 * c + 1], None, ALU.mult)
                for jj in range(1, 4):
                    v.scalar_tensor_tensor(mrep[:, 0:SPAN], u[:, jj:jj + SPAN],
                                           sc[:, tap + 4 * c + jj:tap + 4 * c + jj + 1],
                                           mrep[:, 0:SPAN], ALU.mult, ALU.add)
                kb.incl(v, "ve")
            kb.ve(f)
            kb.sync_step("ae")
            def f(a, c=c):
                a.activation(a2[:, c * NL:c * NL + SPAN], mrep[:, 0:SPAN], AF.Silu)
                kb.incl(a, "ae")
            kb.ae(f)
            kb.sync_step("ve")
        if which == "v":
            kb.wait("ae", "ve", "ae")
            def f(a, row0=row0):
                for c in range(8):
                    kb.inc(a.dma_start(out=ag3b_i[row0 + 128 * c:row0 + 128 * (c + 1), :],
                                       in_=a2[:, c * NL:c * NL + SPAN]), "qa", 16)
            kb.ae(f)
            return
        for c in range(8):
            for ti, (n0, nw) in enumerate(_ntiles(SPAN, 512)):
                kb.sync_step("ae")
                def f(a, c=c, n0=n0, nw=nw):
                    a.activation(scb[:, 0:nw], a2[:, c * NL + n0:c * NL + n0 + nw], AF.Square)
                    kb.incl(a, "ae")
                kb.ae(f)
                kb.sync_step("te")
                def f(t, n0=n0, nw=nw):
                    t.matmul(P4[0:1, 0:nw], cbb[:, CB_ONES:CB_ONES + 1], scb[:, 0:nw],
                             start=True, stop=True)
                    kb.incl(t, "te")
                kb.te(f)
                kb.sync_step("ve")
                def f(v, n0=n0, nw=nw):
                    v.tensor_scalar(mrep[0:1, n0:n0 + nw], P4[0:1, 0:nw], 1e-12, None, ALU.add)
                    kb.incl(v, "ve")
                kb.ve(f)
            kb.sync_step("ae")
            def f(a, c=c):
                kb.inc(a.dma_start(out=rows_hbm[16 + c:17 + c, 0:SPAN], in_=mrep[0:1, 0:SPAN]), "qrw", 16)
            kb.ae(f)
            kb.sync_step("ve")
        kb.sync_step("se")
        def f(s):
            kb.inc(s.dma_start(out=mrep[0:8, 0:SPAN], in_=rows_hbm[16:24, 0:SPAN]), "qbc", 16)
        kb.se(f)
        kb.sync_step("ae")
        def f(a):
            a.activation(mrep[0:8, 0:SPAN], mrep[0:8, 0:SPAN], AF.Sqrt)
            kb.incl(a, "ae")
        kb.ae(f)
        kb.sync_step("ve")
        def f(v):
            v.reciprocal(mrep[0:8, 0:SPAN], mrep[0:8, 0:SPAN])
            kb.incl(v, "ve")
        kb.ve(f)
        kb.sync_step("ae")
        def f(a):
            kb.inc(a.dma_start(out=rows_hbm[16:24, 0:SPAN], in_=mrep[0:8, 0:SPAN]), "qrw", 16)
        kb.ae(f)
        for c in range(8):
            kb.sync_step("ge")
            def f(g, c=c):
                kb.inc(g.dma_start(out=rrep[:, 0:SPAN],
                                   in_=rows_hbm[16 + c:17 + c, 0:SPAN].broadcast_to((128, SPAN))),
                       "qbg", 16)
            kb.ge(f)
            kb.sync_step("ve")
            def f(v, c=c):
                v.tensor_tensor(a2[:, c * NL:c * NL + SPAN], a2[:, c * NL:c * NL + SPAN],
                                rrep[:, 0:SPAN], ALU.mult)
                kb.incl(v, "ve")
            kb.ve(f)
            kb.sync_step("ae")
            def f(a, c=c, row0=row0):
                kb.inc(a.dma_start(out=ag3b_i[row0 + 128 * c:row0 + 128 * (c + 1), :],
                                   in_=a2[:, c * NL:c * NL + SPAN]), "qa", 16)
            kb.ae(f)
            kb.sync_step("ge")

    def emit_l1_small():
        kb.wait("se", "ve", "te")
        def f(s):
            for k in range(8):
                kb.inc(s.dma_start(out=wst[:, 8 * k:8 * k + 8],
                                   in_=w["L1_attn_b_w"][128 * k:128 * k + 128, :]), "qs", 16)
                kb.inc(s.dma_start(out=wst[:, 64 + 8 * k:64 + 8 * k + 8],
                                   in_=w["L1_attn_mix_w"][128 * k:128 * k + 128, :]), "qs", 16)
        kb.se(f)
        kb.wait("ve", "qs")
        def f(v):
            v.tensor_copy(scb[:, 0:128], wst[:, 0:128])
            kb.incl(v, "ve")
        kb.ve(f)
        kb.wait("te", "ve")
        for ti, (n0, nw) in enumerate(_ntiles(SPAN, 512)):
            def f(t, n0=n0, nw=nw):
                for k in range(8):
                    t.matmul(P0[0:8, 0:nw], scb[:, 8 * k:8 * k + 8],
                             xln[:, k * NL + 3 + n0:k * NL + 3 + n0 + nw],
                             start=(k == 0), stop=(k == 7))
                for k in range(8):
                    t.matmul(P1[0:8, 0:nw], scb[:, 64 + 8 * k:64 + 8 * k + 8],
                             xln[:, k * NL + 3 + n0:k * NL + 3 + n0 + nw],
                             start=(k == 0), stop=(k == 7))
                kb.incl(t, "te")
            kb.te(f)
            tev = kb.n["te"]
            def f(a, n0=n0, nw=nw, tev=tev):
                a.wait_ge(kb.sem("te"), tev)
                a.activation(sc[0:8, SC_NRM + n0:SC_NRM + n0 + nw], P0[0:8, 0:nw], AF.Sigmoid)
                a.activation(sc[32:40, SC_NRM + n0:SC_NRM + n0 + nw], P1[0:8, 0:nw],
                             AF.Sigmoid, bias=sc[0:8, SC_MXB:SC_MXB + 1])
                kb.incl(a, "ae")
            kb.ae(f)
            kb.wait("te", "ae")
        kb.wait("ae", "ae")
        def f(a):
            kb.inc(a.dma_start(out=ag3f_i, in_=sc[0:8, SC_NRM:SC_NRM + SPAN]), "qa", 16)
            kb.inc(a.dma_start(out=g1_hbm, in_=sc[32:40, SC_NRM:SC_NRM + SPAN]), "qa", 16)
        kb.ae(f)

    emit_oread_L0()
    emit_onorm_gate_L0()
    emit_wso_wout()
    d = dbg_out("x2", [D, NL])
    if d is not None:
        kb.wait("ae", "ve")
        def f(a, d=d):
            for c in range(8):
                kb.inc(a.dma_start(out=d[128 * c:128 * (c + 1), :],
                                   in_=res[:, c * NL:(c + 1) * NL]), "qa", 16)
            a.wait_ge(kb.sem("qa"), kb.n["qa"])
        kb.ae(f)
    emit_ln(ncols=2051, col0=1, quant=True)
    emit_ffn0(2051, col0=1)
    emit_wT_prep()
    emit_ln(ncols=2051, col0=1)
    emit_ffn1(2051, col0=1)
    d = dbg_out("x3b", [D, NL])
    if d is not None:
        kb.wait("ae", "ve")
        def f(a, d=d):
            for c in range(8):
                kb.inc(a.dma_start(out=d[128 * c:128 * (c + 1), :],
                                   in_=res[:, c * NL:(c + 1) * NL]), "qa", 16)
            a.wait_ge(kb.sem("qa"), kb.n["qa"])
        kb.ae(f)
    emit_ln(ncols=2051, col0=1)
    emit_l1_proj("q")
    emit_l1_proj("k")
    emit_l1_proj("v")
    emit_l1_small()
    d = dbg_out("ag3b", [3072, SPAN], BF16)
    if d is not None:
        kb.wait("ae", "qa", "ve")
        def f(a, d=d):
            kb.inc(a.dma_start(out=d, in_=ag3b_i), "qa", 16)
            a.wait_ge(kb.sem("qa"), kb.n["qa"])
        kb.ae(f)
    d = dbg_out("ag3f", [8, SPAN], F32)
    if d is not None:
        kb.wait("ae", "qa", "ve")
        def f(a, d=d):
            kb.inc(a.dma_start(out=d, in_=ag3f_i), "qa", 16)
            a.wait_ge(kb.sem("qa"), kb.n["qa"])
        kb.ae(f)
    kb.wait("ge", "qa", "qs", "ve", "ae", "te", "qrw")
    def f(g):
        kb.inc(g.collective_compute("AllGather", ALU.bypass,
                                    replica_groups=[list(range(8))],
                                    ins=[ag3b_i.opt()], outs=[ag3b_o.opt()]), "cc")
        kb.inc(g.collective_compute("AllGather", ALU.bypass,
                                    replica_groups=[list(range(8))],
                                    ins=[ag3f_i.opt()], outs=[ag3f_o.opt()]), "cc")
    kb.ge(f)

    def delta_chunk_L1(u, j, c):
        tok0 = 128 * c
        kb.sync_step("se")
        def f(s):
            kb.inc(s.dma_start(out=a2[:, A2_KTM:A2_KTM + 128],
                               in_=a1[:, A1_KH + tok0:A1_KH + tok0 + 128], transpose=True), "qt0", 16)
            kb.inc(s.dma_start(out=a2[:, A2_VTM:A2_VTM + 128],
                               in_=a1[:, A1_V0 + tok0:A1_V0 + tok0 + 128], transpose=True), "qt0", 16)
        kb.se(f)
        kb.sync_step("te")
        def f(t):
            t.matmul(P2[0:128, 0:128], a1[:, A1_KH + tok0:A1_KH + tok0 + 128],
                     a1[:, A1_KL + tok0:A1_KL + tok0 + 128], start=True, stop=True)
            t.matmul(P3[0:128, 0:128], a1[:, A1_KH + tok0:A1_KH + tok0 + 128],
                     a1[:, A1_QH + tok0:A1_QH + tok0 + 128], start=True, stop=True)
            kb.incl(t, "te")
        kb.te(f)
        kb.sync_step("ve")
        def f(v):
            v.tensor_tensor(a1[:, A1_DT:A1_DT + 128], P2[0:128, 0:128],
                            cfm(CF_MSK_BD), ALU.mult)
            v.tensor_tensor(a1[:, A1_NT:A1_NT + 128], P2[0:128, 0:128],
                            cfm(CF_MSK_BELOW), ALU.mult)
            v.tensor_tensor(a1[:, A1_LT:A1_LT + 128], P3[0:128, 0:128],
                            cfm(CF_MSK_UP_INCL), ALU.mult)
            v.transpose(a1[:, A1_DD:A1_DD + 128], a1[:, A1_DT:A1_DT + 128])
            v.scalar_tensor_tensor(a1[:, A1_GT:A1_GT + 128], a1[:, A1_DT:A1_DT + 128],
                                   -1.0, identB, ALU.mult, ALU.add)
            kb.incl(v, "ve")
        kb.ve(f)
        kb.sync_step("te")
        def f(t):
            t.matmul(P4[0:128, 0:128], a1[:, A1_DD:A1_DD + 128],
                     a1[:, A1_DT:A1_DT + 128], start=True, stop=True)
            kb.incl(t, "te")
        kb.te(f)
        for lvl in range(4):
            kb.sync_step("ve")
            def f(v):
                v.tensor_copy(a1[:, A1_PW:A1_PW + 128], P4[0:128, 0:128])
                v.transpose(a1[:, A1_PWT:A1_PWT + 128], a1[:, A1_PW:A1_PW + 128])
                kb.incl(v, "ve")
            kb.ve(f)
            kb.sync_step("te")
            def f(t, lvl=lvl):
                t.matmul(P5[0:128, 0:128], a1[:, A1_PWT:A1_PWT + 128],
                         a1[:, A1_GT:A1_GT + 128], start=True, stop=True)
                if lvl < 3:
                    t.matmul(P4[0:128, 0:128], a1[:, A1_PWT:A1_PWT + 128],
                             a1[:, A1_PW:A1_PW + 128], start=True, stop=True)
                kb.incl(t, "te")
            kb.te(f)
            kb.sync_step("ve")
            def f(v, lvl=lvl):
                v.tensor_tensor(a1[:, A1_GT:A1_GT + 128], a1[:, A1_GT:A1_GT + 128],
                                P5[0:128, 0:128], ALU.add)
                if lvl == 3:
                    v.transpose(a1[:, A1_GB:A1_GB + 128], a1[:, A1_GT:A1_GT + 128])
                    v.tensor_copy(a1[:, A1_TF:A1_TF + 128], a1[:, A1_GB:A1_GB + 128])
                kb.incl(v, "ve")
            kb.ve(f)
        for it in range(3):
            kb.sync_step("te")
            def f(t):
                t.matmul(P4[0:128, 0:128], a1[:, A1_NT:A1_NT + 128],
                         a1[:, A1_TF:A1_TF + 128], start=True, stop=True)
                kb.incl(t, "te")
            kb.te(f)
            kb.sync_step("ve")
            def f(v):
                v.tensor_copy(a1[0:128, A1_TS:A1_TS + 128], P4[0:128, 0:128])
                kb.incl(v, "ve")
            kb.ve(f)
            kb.sync_step("te")
            def f(t):
                t.matmul(P5[0:128, 0:128], a1[:, A1_GT:A1_GT + 128],
                         a1[:, A1_TS:A1_TS + 128], start=True, stop=True)
                kb.incl(t, "te")
            kb.te(f)
            kb.sync_step("ve")
            def f(v):
                v.tensor_tensor(a1[:, A1_TF:A1_TF + 128], a1[:, A1_GB:A1_GB + 128],
                                P5[0:128, 0:128], ALU.subtract)
                kb.incl(v, "ve")
            kb.ve(f)
        kb.sync_step("te")
        def f(t):
            t.transpose(PT[:, 0:128], a1[:, A1_TF:A1_TF + 128], identB)
            kb.incl(t, "te")
        kb.te(f)
        kb.sync_step("ve")
        def f(v):
            v.tensor_copy(a1[:, A1_TT:A1_TT + 128], PT[:, 0:128])
            v.tensor_scalar(a2[:, A2_VT:A2_VT + 128], a2[:, A2_VTM:A2_VTM + 128],
                            sd[:, SD_BET + c:SD_BET + c + 1], None, ALU.mult)
            v.tensor_scalar(a2[:, A2_KBAR:A2_KBAR + 128], a2[:, A2_KTM:A2_KTM + 128],
                            sd[:, SD_BET + c:SD_BET + c + 1], None, ALU.mult)
            kb.incl(v, "ve")
        kb.ve(f)
        kb.sync_step("te")
        def f(t):
            t.matmul(P4[0:128, 0:128], a1[:, A1_TT:A1_TT + 128],
                     a2[:, A2_VT:A2_VT + 128], start=True, stop=True)
            t.matmul(P5[0:128, 0:128], a2[:, A2_KBAR:A2_KBAR + 128],
                     a1[:, A1_TT:A1_TT + 128], start=True, stop=True)
            kb.incl(t, "te")
        kb.te(f)
        kb.sync_step("ve")
        def f(v):
            v.tensor_copy(a2[:, A2_WHI:A2_WHI + 128], P5[0:128, 0:128])
            kb.incl(v, "ve")
        kb.ve(f)
        kb.sync_step("ve")
        def f(v):
            v.tensor_copy(a1[:, A1_TS:A1_TS + 128], P4[0:128, 0:128])
            kb.incl(v, "ve")
        kb.ve(f)
        kb.sync_step("te")
        def f(t):
            t.matmul(P4[0:128, 0:128], a2[:, A2_WHI:A2_WHI + 128],
                     a2[:, A2_SBH:A2_SBH + 128], start=True, stop=True)
            kb.incl(t, "te")
        kb.te(f)
        kb.sync_step("ve")
        def f(v):
            v.tensor_tensor(a2[:, A2_VT:A2_VT + 128], a1[:, A1_TS:A1_TS + 128],
                            P4[0:128, 0:128], ALU.subtract)
            kb.incl(v, "ve")
        kb.ve(f)
        kb.sync_step("te")
        def f(t):
            t.matmul(P6[0:128, 0:128], a2[:, A2_SBH:A2_SBH + 128],
                     a1[:, A1_QH + tok0:A1_QH + tok0 + 128], start=True, stop=False)
            t.matmul(P6[0:128, 0:128], a2[:, A2_VT:A2_VT + 128],
                     a1[:, A1_LT:A1_LT + 128], start=False, stop=True)
            kb.incl(t, "te")
        kb.te(f)
        kb.sync_step("ve")
        def f(v):
            v.tensor_copy(a2[:, A2_OST:A2_OST + 128], P6[0:128, 0:128])
            kb.incl(v, "ve")
        kb.ve(f)
        kb.sync_step("ae")
        def f(a, u=u, j=j, c=c):
            kb.inc(a.dma_start(out=ag4_i[128 * u:128 * (u + 1),
                                         SPAN * j + tok0:SPAN * j + tok0 + 128],
                               in_=a2[:, A2_OST:A2_OST + 128]), "qo", 16)
        kb.ae(f)
        kb.sync_step("te")
        def f(t):
            t.matmul(P4[0:128, 0:128], a2[:, A2_KTM:A2_KTM + 128],
                     a2[:, A2_VT:A2_VT + 128], start=True, stop=True)
            kb.incl(t, "te")
        kb.te(f)
        kb.sync_step("ve")
        def f(v):
            v.tensor_tensor(wst[:, 256:384], wst[:, 256:384], P4[0:128, 0:128], ALU.add)
            v.tensor_copy(a2[:, A2_SBH:A2_SBH + 128], wst[:, 256:384])
            kb.incl(v, "ve")
        kb.ve(f)

    def emit_delta_L1():
        for u in range(2):
            kb.sync_step("ve")
            def f(v):
                v.memset(wst[:, 256:384], 0.0)
                v.memset(a2[:, A2_SBH:A2_SBH + 128], 0.0)
                kb.incl(v, "ve")
            kb.ve(f)
            for j in range(4):
                kb.sync_step("ge")
                def f(g, u=u, j=j):
                    for t_, col in enumerate((A1_QH, A1_KH, A1_V0)):
                        gc = GX_L1 + u * 12 + j * 3 + t_
                        kb.inc(g.indirect_dma_start(
                            out=a1[:, col:col + SPAN], out_offset=None, in_=ag3b_o,
                            in_offset=bass.IndirectOffsetOnAxis(
                                ap=gix[:, gc:gc + 1], axis=0)), "qgB", 16)
                    kb.inc(g.indirect_dma_start(
                        out=mrep[0:64, 0:SPAN], out_offset=None, in_=ag3f_o,
                        in_offset=bass.IndirectOffsetOnAxis(
                            ap=gix[0:64, GX_B1 + j:GX_B1 + j + 1], axis=0)), "qgB", 16)
                kb.ge(f)
                kb.sync_step("ae")
                def f(a, u=u):
                    kb.inc(a.dma_start(out=rows_hbm[4:5, 0:SPAN],
                                       in_=mrep[32 * u:32 * u + 1, 0:SPAN]), "qrw", 16)
                kb.ae(f)
                kb.sync_step("se")
                def f(s):
                    rh = rows_hbm.tensor
                    kb.inc(s.dma_start(out=wst[0:16, 1152:1280],
                                       in_=bass.AP(rh, 4 * L, [[128, 16], [1, 128]])), "qbc", 16)
                kb.se(f)
                kb.sync_step("ge")
                def f(g):
                    kb.inc(g.dma_start(out=rrep[:, 0:SPAN],
                                       in_=rows_hbm[4:5, 0:SPAN].broadcast_to((128, SPAN))),
                           "qbg", 16)
                kb.ge(f)
                kb.sync_step("te")
                def f(t):
                    t.transpose(P3[0:128, 0:16], wst[0:16, 1152:1280], identF[0:16, 0:16])
                    kb.incl(t, "te")
                kb.te(f)
                kb.sync_step("ve")
                def f(v):
                    v.tensor_copy(sd[:, SD_BET:SD_BET + 16], P3[0:128, 0:16])
                    v.tensor_tensor(a1[:, A1_KL:A1_KL + SPAN], a1[:, A1_KH:A1_KH + SPAN],
                                    rrep[:, 0:SPAN], ALU.mult)
                    kb.incl(v, "ve")
                kb.ve(f)
                for c in range(16):
                    delta_chunk_L1(u, j, c)

    kb.wait("ge", "cc")
    emit_delta_L1()
    d = dbg_out("oft1", [2 * DV1, L], BF16)
    if d is not None:
        kb.wait("ae", "qo", "ve", "te")
        def f(a, d=d):
            kb.inc(a.dma_start(out=d, in_=ag4_i), "qa", 16)
            a.wait_ge(kb.sem("qa"), kb.n["qa"])
        kb.ae(f)
    kb.wait("ge", "qo", "ve", "ae", "te")
    def f(g):
        kb.inc(g.collective_compute("AllGather", ALU.bypass,
                                    replica_groups=[list(range(8))],
                                    ins=[ag4_i.opt()], outs=[ag4_o.opt()]), "cc")
    kb.ge(f)

    kb.wait("ge", "cc")
    kb.sync_step("ge")
    def f(g):
        inw = bass.AP(ag4_o.tensor, 0, [[2048, 4 * 2048], [1, 2048]])
        for hh in range(8):
            kb.inc(g.indirect_dma_start(
                out=a1[:, hh * NL:hh * NL + SPAN], out_offset=None, in_=inw,
                in_offset=bass.IndirectOffsetOnAxis(
                    ap=gix[:, GX_OE + hh:GX_OE + hh + 1], axis=0)), "qgB", 16)
    kb.ge(f)
    kb.sync_step("se")
    def f(s):
        for hh in range(8):
            kb.inc(s.dma_start(out=a2[:, hh * NL:hh * NL + SPAN],
                               in_=ag3b_i[2048 + 128 * hh:2048 + 128 * (hh + 1), :]), "qt0", 16)
    kb.se(f)
    for hh in range(8):
        kb.sync_step("ge")
        def f(g, hh=hh):
            kb.inc(g.dma_start(out=rrep[:, 0:SPAN],
                               in_=g1_hbm[hh:hh + 1, :].broadcast_to((128, SPAN))), "qbg", 16)
        kb.ge(f)
        kb.sync_step("ve")
        def f(v, hh=hh):
            o = a1[:, hh * NL:hh * NL + SPAN]
            vt = a2[:, hh * NL:hh * NL + SPAN]
            v.tensor_tensor(o, o, vt, ALU.subtract)
            v.tensor_tensor(o, o, rrep[:, 0:SPAN], ALU.mult)
            v.tensor_tensor(o, o, vt, ALU.add)
            kb.incl(v, "ve")
        kb.ve(f)
        for ti, (n0, nw) in enumerate(_ntiles(SPAN, 512)):
            kb.sync_step("ae")
            def f(a, hh=hh, n0=n0, nw=nw):
                a.activation(scb[:, 0:nw], a1[:, hh * NL + n0:hh * NL + n0 + nw], AF.Square)
                kb.incl(a, "ae")
            kb.ae(f)
            kb.sync_step("te")
            def f(t, n0=n0, nw=nw):
                t.matmul(P4[0:1, 0:nw], cbb[:, CB_ONES:CB_ONES + 1], scb[:, 0:nw],
                         start=True, stop=True)
                kb.incl(t, "te")
            kb.te(f)
            kb.sync_step("ve")
            def f(v, n0=n0, nw=nw):
                v.tensor_scalar(mrep[0:1, n0:n0 + nw], P4[0:1, 0:nw],
                                1.0 / 128.0, 1e-5, ALU.mult, ALU.add)
                kb.incl(v, "ve")
            kb.ve(f)
        kb.sync_step("ae")
        def f(a, hh=hh):
            kb.inc(a.dma_start(out=rows_hbm[24 + hh:25 + hh, 0:SPAN], in_=mrep[0:1, 0:SPAN]), "qrw", 16)
        kb.ae(f)
        kb.sync_step("ve")
    kb.sync_step("se")
    def f(s):
        kb.inc(s.dma_start(out=mrep[0:8, 0:SPAN], in_=rows_hbm[24:32, 0:SPAN]), "qbc", 16)
    kb.se(f)
    kb.sync_step("ae")
    def f(a):
        a.activation(mrep[0:8, 0:SPAN], mrep[0:8, 0:SPAN], AF.Sqrt)
        kb.incl(a, "ae")
    kb.ae(f)
    kb.sync_step("ve")
    def f(v):
        v.reciprocal(mrep[0:8, 0:SPAN], mrep[0:8, 0:SPAN])
        kb.incl(v, "ve")
    kb.ve(f)
    kb.sync_step("ae")
    def f(a):
        kb.inc(a.dma_start(out=rows_hbm[24:32, 0:SPAN], in_=mrep[0:8, 0:SPAN]), "qrw", 16)
    kb.ae(f)
    for hh in range(8):
        kb.sync_step("ge")
        def f(g, hh=hh):
            kb.inc(g.dma_start(out=rrep[:, 0:SPAN],
                               in_=rows_hbm[24 + hh:25 + hh, 0:SPAN].broadcast_to((128, SPAN))),
                   "qbg", 16)
        kb.ge(f)
        kb.sync_step("ve")
        def f(v, hh=hh):
            v.scalar_tensor_tensor(a1[:, hh * NL:hh * NL + SPAN], a1[:, hh * NL:hh * NL + SPAN],
                                   sc[:, SC_ONW:SC_ONW + 1], rrep[:, 0:SPAN],
                                   ALU.mult, ALU.mult)
            kb.incl(v, "ve")
        kb.ve(f)
        kb.sync_step("ge")
    def on_moving(k, n0, nw):
        return a1[:, k * NL + n0:k * NL + n0 + nw]

    def evac_res4(col, pp, n0, nw, tem):
        def f(v, col=col, pp=pp, n0=n0, nw=nw, tem=tem):
            v.wait_ge(kb.sem(tem[0]), tem[1])
            v.tensor_tensor(res[:, col * NL + 4 + n0:col * NL + 4 + n0 + nw],
                            res[:, col * NL + 4 + n0:col * NL + 4 + n0 + nw],
                            pp[0:128, 0:nw], ALU.add)
            kb.incl(v, "ve")
        kb.ve(f)
        return ("ve", kb.n["ve"])

    emit_proj(w["L1_attn_o_w"], 8, 8, SPAN, on_moving, evac_res4)
    d = dbg_out("x5", [D, NL])
    if d is not None:
        kb.wait("ae", "ve")
        def f(a, d=d):
            for c in range(8):
                kb.inc(a.dma_start(out=d[128 * c:128 * (c + 1), :],
                                   in_=res[:, c * NL:(c + 1) * NL]), "qa", 16)
            a.wait_ge(kb.sem("qa"), kb.n["qa"])
        kb.ae(f)
    emit_ln(ncols=SPAN, col0=4)
    emit_ffn1(SPAN, col0=4)

    kb.wait("se", "ve", "te", "ae", "ge", "cc", "qa", "qo")
    def f(s):
        for c in range(8):
            kb.inc(s.dma_start(out=y_out[128 * c:128 * (c + 1), :],
                               in_=res[:, c * NL + HL:c * NL + HL + SPAN]), "qs", 16)
        s.wait_ge(kb.sem("qs"), kb.n["qs"])
    kb.se(f)

    es.close()
    return nc, sorted(dbg.keys())



def _gidx_table(r):
    b, qd = divmod(r, 4)
    h = r % 4
    t = np.zeros((128, 96), np.uint32)
    p = np.arange(128)
    for j in range(4):
        rank = 4 * b + j
        col = np.zeros(128, np.uint32)
        col[:32] = 8 * rank + h
        col[32:64] = 8 * rank + 4 + h
        t[:, 0 + j] = col
        offs = [192 * h, 192 * h + 128, 768 + 192 * h, 768 + 192 * h + 128,
                1536 + 384 * h, 1536 + 384 * h + 128, 1536 + 384 * h + 256]
        for tt_, off in enumerate(offs):
            t[:, 4 + 7 * j + tt_] = 3072 * rank + off + np.minimum(p, 127)
    for cc in range(12):
        hh, ds = divmod(cc, 3)
        row = 384 * (4 * b + hh) + 128 * ds + p
        t[:, 32 + cc] = row * 4 + qd
        t[:, 44 + cc] = np.maximum(row * 128 + 32 * qd - 1, 0)
    for u in range(2):
        hh = 2 * qd + u if False else (2 * r + u) % 8
        for j in range(4):
            rank = 4 * b + j
            for tt_, off in enumerate((0, 1024, 2048)):
                t[:, 56 + u * 12 + j * 3 + tt_] = 3072 * rank + off + 128 * hh + p
    for j in range(4):
        rank = 4 * b + j
        col = np.zeros(128, np.uint32)
        col[:32] = 8 * rank + (2 * r) % 8
        col[32:64] = 8 * rank + (2 * r + 1) % 8
        t[:, 80 + j] = col
    for hh in range(8):
        u = 8 * b + hh
        row = 256 * (u // 2) + 128 * (u % 2) + p
        t[:, 84 + hh] = row * 4 + qd
    return t


def _host_layout(inputs):
    x = np.asarray(inputs["L0_attn_x"])
    cf, cbm = _host_consts()
    pos_all = np.arange(L, dtype=np.float32)
    div = np.exp(np.arange(0, 32, 2, dtype=np.float32) * (-math.log(10000.0) / 32))
    pe_full = np.zeros((L, 32), np.float32)
    ang = pos_all[:, None] * div[None, :]
    pe_full[:, 0::2] = np.sin(ang)
    pe_full[:, 1::2] = np.cos(ang)

    maps = []
    for r in range(NCORE):
        b, qd = divmod(r, 4)
        s0 = qd * SPAN
        lo, hi = s0 - HL, s0 + SPAN + HR
        clo, chi = max(lo, 0), min(hi, L)
        xs = np.zeros((NL, D), np.float32)
        xs[clo - lo:chi - lo] = x[b, clo:chi]
        pes = np.zeros((NL, 32), np.float32)
        pes[clo - lo:chi - lo] = pe_full[clo:chi]
        g = np.arange(lo, hi)
        hm = ((g >= 0) & (g < L)).astype(np.float32)[None, :]
        m = {
            "x_fm": np.ascontiguousarray(xs.T),
            "pe_fm": np.ascontiguousarray(pes.T),
            "hmask": np.ascontiguousarray(hm),
            "consts_f": cf,
            "consts_b": cbm,
            "gidx": _gidx_table(r),
        }
        for name in ["L0_attn_W_in", "L0_attn_W_q", "L0_attn_W_k", "L0_attn_W_v",
                     "L0_attn_W_g", "L0_attn_W_gk", "L0_attn_W_b", "L0_attn_W_so",
                     "L0_attn_W_out", "L0_ffn_weight_fp8", "L1_attn_q_w",
                     "L1_attn_k_w", "L1_attn_v_w", "L1_attn_b_w", "L1_attn_mix_w",
                     "L1_attn_o_w", "L1_ffn_tile_sigs"]:
            m[name] = np.ascontiguousarray(np.asarray(inputs[name], dtype=np.float32))
        m["L1_ffn_W"] = np.ascontiguousarray(
            np.asarray(inputs["L1_ffn_W"], np.float32).reshape(8 * 1024, 1024))
        m["L1_ffn_b"] = np.ascontiguousarray(np.asarray(inputs["L1_ffn_b"], np.float32))
        for name in ["L0_attn_conv_x_w", "L0_attn_conv_z_w", "L1_attn_conv_q_w",
                     "L1_attn_conv_k_w", "L1_attn_conv_v_w"]:
            m[name] = np.ascontiguousarray(
                np.asarray(inputs[name], np.float32).reshape(1024, 4))
        for name in ["L0_attn_conv_x_b", "L0_attn_conv_z_b", "L0_attn_b_b",
                     "L0_attn_A_log", "L0_attn_dt_bias", "L0_attn_g_norm_w",
                     "L0_ffn_bias", "L0_ffn_input_scale", "L0_ffn_weight_scale",
                     "L1_attn_mix_b", "L1_attn_mix_bias", "L1_attn_o_norm_w"]:
            m[name] = np.ascontiguousarray(
                np.asarray(inputs[name], np.float32).reshape(-1, 1))
        maps.append(m)
    return maps


def kernel(**inputs):
    maps = _host_layout(inputs)
    nc, dbg_names = build_program()
    res = run_bass_kernel_spmd(nc, maps, core_ids=list(range(NCORE)))
    out = np.zeros((B, L, D), np.float32)
    for r in range(NCORE):
        b, qd = divmod(r, 4)
        out[b, qd * SPAN:(qd + 1) * SPAN, :] = np.asarray(
            res.results[r]["y_fm"], np.float32).T
    if DEBUG:
        kernel.debug = [{n: np.asarray(res.results[r]["dbg_" + n]) for n in dbg_names}
                        for r in range(NCORE)]
    return out
